# version 04
# baseline (speedup 1.0000x reference)
import contextlib
import numpy as np
import ml_dtypes
import concourse.bass as bass
import concourse.mybir as mybir
from concourse.bass import IndirectOffsetOnAxis
from concourse.bass_utils import run_bass_kernel_spmd

F32 = mybir.dt.float32
BF16 = mybir.dt.bfloat16
I32 = mybir.dt.int32
U16 = mybir.dt.uint16
AF = mybir.ActivationFunctionType
ALU = mybir.AluOpType
AX = mybir.AxisListType

NCORES = 8
D = 1024
KT = 8
B_PER_CORE = 2
S = 2048
TOK = B_PER_CORE * S
CHUNK = 128
NCHUNK = S // CHUNK

ENGINE_NAMES = ("sync", "scalar", "vector", "tensor", "gpsimd")


class Op:
    __slots__ = ("eng", "fn", "inc", "deps", "cum", "is_dma", "sem_key", "prewait")

    def __init__(self, eng, fn, inc, deps, is_dma):
        self.eng = eng
        self.fn = fn
        self.inc = inc
        self.deps = deps
        self.is_dma = is_dma
        self.cum = 0
        self.sem_key = None
        self.prewait = None


NDMASEM = 6


_BC_REG_CACHE = {}


def bc_reg(e, val):
    key = (id(e), val)
    if key not in _BC_REG_CACHE:
        _BC_REG_CACHE[key] = e.to_reg(val)
    return _BC_REG_CACHE[key]


class Seq:

    def __init__(self, nc):
        self.nc = nc
        self.ops = []
        self.fence_ops = {}

    def op(self, eng, fn, deps=(), dma=False):
        deps = [d for d in deps if d is not None]
        f = self.fence_ops.get(eng)
        if f is not None:
            deps.append(f)
        o = Op(eng, fn, 16 if dma else 1, deps, dma)
        self.ops.append(o)
        return o

    def tails(self):
        seen = set()
        out = []
        cum = {}
        ndma = {}
        for o in reversed(self.ops):
            if o.is_dma:
                k = (o.eng, "d")
                ndma.setdefault(k, 0)
                if ndma[k] < NDMASEM:
                    out.append(o)
                    ndma[k] += 1
            else:
                if (o.eng, "c") not in seen:
                    seen.add((o.eng, "c"))
                    out.append(o)
        return out

    def fence(self):
        t = self.tails()
        self.fence_ops = {}
        for en in ENGINE_NAMES:
            self.fence_ops[en] = self.op(en, lambda e: e.nop(), deps=tuple(t))

    def dma(self, queue, out, in_, deps=(), **kw):
        return self.op(queue, lambda e, out=out, in_=in_, kw=kw: e.dma_start(out=out, in_=in_, **kw),
                       deps, dma=True)

    def build(self, stack):
        nc = self.nc
        cmp_sems = {en: stack.enter_context(nc.semaphore(f"c_{en}")) for en in ENGINE_NAMES}
        dma_queues = sorted({o.eng for o in self.ops if o.is_dma})
        dma_sems = {}
        for q in dma_queues:
            for r in range(NDMASEM):
                dma_sems[(q, r)] = stack.enter_context(nc.semaphore(f"d_{q}{r}"))
        cum = {en: 0 for en in ENGINE_NAMES}
        dma_n = {q: 0 for q in dma_queues}
        sem_uses = {k: 0 for k in dma_sems}
        for o in self.ops:
            if o.is_dma:
                k = (o.eng, dma_n[o.eng] % NDMASEM)
                dma_n[o.eng] += 1
                sem_uses[k] += 1
                o.sem_key = k
                o.cum = 16 * sem_uses[k]
                if sem_uses[k] > 1:
                    o.prewait = (k, o.cum - 16)
            else:
                cum[o.eng] += 1
                o.cum = cum[o.eng]
                o.sem_key = o.eng
        ops = self.ops
        block = stack.enter_context(nc.Block())

        def sem_of(key):
            return dma_sems[key] if isinstance(key, tuple) else cmp_sems[key]

        def body_for(name):
            def body(engobj):
                waited = {}
                for o in ops:
                    if o.eng != name:
                        continue
                    for d in o.deps:
                        if d.cum > waited.get(d.sem_key, 0):
                            engobj.wait_ge(sem_of(d.sem_key), d.cum)
                            waited[d.sem_key] = d.cum
                    if o.prewait is not None and o.prewait[1] > waited.get(o.prewait[0], 0):
                        engobj.wait_ge(sem_of(o.prewait[0]), o.prewait[1])
                        waited[o.prewait[0]] = o.prewait[1]
                    o.fn(engobj).then_inc(sem_of(o.sem_key), o.inc)
                if name == "sync":
                    for en in ENGINE_NAMES:
                        if en != "sync" and cum[en] > 0:
                            engobj.wait_ge(cmp_sems[en], cum[en])
                    for k, n in sem_uses.items():
                        if n > 0:
                            engobj.wait_ge(dma_sems[k], 16 * n)
            return body

        block.sync(body_for("sync"))
        block.scalar(body_for("scalar"))
        block.vector(body_for("vector"))
        block.tensor(body_for("tensor"))
        block.gpsimd(body_for("gpsimd"))


def bf(x):
    return np.ascontiguousarray(np.asarray(x, dtype=np.float32)).astype(ml_dtypes.bfloat16)


def f32c(x):
    return np.ascontiguousarray(np.asarray(x, dtype=np.float32))


class Ctx:
    def __init__(self):
        self.nc = bass.Bass("TRN2", target_bir_lowering=False)
        self.seq = Seq(self.nc)
        self.stack = contextlib.ExitStack()
        self.scopes = [self.stack]

    def push(self):
        st = contextlib.ExitStack()
        self.scopes.append(st)
        return st

    def pop(self):
        self.scopes.pop().close()

    def sb(self, name, shape, dtype):
        return self.scopes[-1].enter_context(self.nc.sbuf_tensor(name, shape, dtype))

    def ps(self, name, shape, dtype=F32):
        return self.scopes[-1].enter_context(self.nc.psum_tensor(name, shape, dtype))

    def inp(self, name, shape, dtype=F32):
        return self.nc.dram_tensor(name, shape, dtype, kind="ExternalInput").ap()

    def outp(self, name, shape, dtype=F32):
        return self.nc.dram_tensor(name, shape, dtype, kind="ExternalOutput").ap()

    def dram(self, name, shape, dtype):
        return self.nc.dram_tensor(name, shape, dtype, kind="Internal").ap()

    def finish(self):
        self.seq.build(self.stack)
        self.stack.close()
        return self.nc



XCOLS = 2 * (S + 1)
BATCH_BASE = [1, S + 2]


def emit_l0_attn(c, x0_ap, wwT_ap, wrT_ap, dec_ap, la_ap, x_fm):
    s = c.seq

    ident_bf = c.sb("ident_bf", [128, 128], BF16)
    ii = c.sb("ident_i", [128, 128], I32)
    jj = c.sb("ident_j", [128, 128], I32)
    oi1 = s.op("gpsimd", lambda e: e.iota(ii[:], pattern=[[1, 128]], base=0, channel_multiplier=0))
    oi2 = s.op("gpsimd", lambda e: e.iota(jj[:], pattern=[[0, 128]], base=0, channel_multiplier=1))
    dep_id = s.op("vector", lambda e: e.tensor_tensor(out=ident_bf[:], in0=ii[:], in1=jj[:], op=ALU.is_equal),
                  deps=(oi1, oi2))

    pb = [c.ps(f"pb{i}", [128, 512], F32) for i in range(6)]
    pTx = [c.ps("pTa", [128, 128], BF16), c.ps("pTb", [128, 128], BF16)]
    dep_pb = {i: None for i in range(6)}
    dep_pT = {0: None, 1: None}

    sc = c.sb("sc", [1, 8], F32)
    ones_row = c.sb("ones_row", [1, 128], F32)
    o_scz = s.op("vector", lambda e: e.memset(sc[:], 0.0))
    ld_dec = s.dma("sync", sc[0:1, 0:1], dec_ap, deps=(o_scz,))
    ld_la = s.dma("sync", sc[0:1, 3:4], la_ap, deps=(o_scz,))
    o = s.op("scalar", lambda e: e.activation(out=sc[0:1, 1:2], in_=sc[0:1, 0:1], func=AF.Sigmoid), deps=(ld_dec,))
    o = s.op("scalar", lambda e: e.activation(out=sc[0:1, 1:2], in_=sc[0:1, 1:2], func=AF.Ln), deps=(o,))
    o_gC = s.op("scalar", lambda e: e.activation(out=sc[0:1, 2:3], in_=sc[0:1, 1:2], func=AF.Exp,
                                                 scale=float(CHUNK)), deps=(o,))
    o_al = s.op("scalar", lambda e: e.activation(out=sc[0:1, 3:4], in_=sc[0:1, 3:4], func=AF.Exp), deps=(ld_la,))
    o_ones = s.op("vector", lambda e: e.memset(ones_row[:], 1.0))

    cols = c.sb("cols", [128, 8], F32)
    mm = s.op("tensor", lambda e: e.matmul(pb[0][:, 0:8], ones_row[0:1, :], sc[0:1, 0:8], start=True, stop=True),
              deps=(o_gC, o_al, o_ones))
    o_cols = s.op("vector", lambda e: e.tensor_copy(out=cols[:], in_=pb[0][:, 0:8]), deps=(mm,))
    dep_pb[0] = o_cols
    lg_col = cols[:, 1:2]
    gC_col = cols[:, 2:3]
    al_col = cols[:, 3:4]

    iot_row = c.sb("iot_row", [1, 128], I32)
    iot_col = c.sb("iot_col", [128, 1], I32)
    frow = c.sb("frow", [1, 128], F32)
    fcol = c.sb("fcol", [128, 1], F32)
    gw_col = c.sb("gw_col", [128, 1], F32)
    gp_row = c.sb("gp_row", [1, 128], F32)
    gp_bcast = c.sb("gp_bcast", [128, 128], BF16)
    o1 = s.op("gpsimd", lambda e: e.iota(iot_row[:], pattern=[[1, 128]], base=0, channel_multiplier=0))
    o2 = s.op("gpsimd", lambda e: e.iota(iot_col[:], pattern=[[0, 1]], base=0, channel_multiplier=1))
    o3 = s.op("vector", lambda e: e.tensor_copy(out=frow[:], in_=iot_row[:]), deps=(o1,))
    o4a = s.op("vector", lambda e: e.tensor_copy(out=fcol[:], in_=iot_col[:]), deps=(o2,))
    o4 = s.op("vector", lambda e: e.tensor_scalar(out=fcol[:], in0=fcol[:], scalar1=-1.0, scalar2=127.0,
                                                  op0=ALU.mult, op1=ALU.add), deps=(o4a,))
    o5 = s.op("scalar", lambda e: e.activation(out=gp_row[:], in_=frow[:], func=AF.Exp, scale=sc[0:1, 1:2]),
              deps=(o3, o_gC))
    o_gw = s.op("scalar", lambda e: e.activation(out=gw_col[:], in_=fcol[:], func=AF.Exp, scale=lg_col),
                deps=(o4, o_cols))
    mm = s.op("tensor", lambda e: e.matmul(pb[1][:, 0:128], ones_row[0:1, :], gp_row[0:1, :], start=True, stop=True),
              deps=(o5, o_ones))
    o_gp = s.op("vector", lambda e: e.tensor_copy(out=gp_bcast[:], in_=pb[1][:, 0:128]), deps=(mm,))
    dep_pb[1] = o_gp

    mT = c.sb("mT", [128, 128], F32)
    dif = c.sb("dif", [128, 128], I32)
    mtmp = c.sb("mtmp", [128, 128], F32)
    o1 = s.op("gpsimd", lambda e: e.iota(dif[:], pattern=[[1, 128]], base=-1, channel_multiplier=-1))
    o2 = s.op("vector", lambda e: e.tensor_copy(out=mtmp[:], in_=dif[:]), deps=(o1,))
    o3 = s.op("vector", lambda e: e.tensor_scalar(out=mT[:], in0=mtmp[:], scalar1=0.0, scalar2=None,
                                                  op0=ALU.is_ge), deps=(o2,))
    o4 = s.op("vector", lambda e: e.tensor_scalar(out=mtmp[:], in0=mtmp[:], scalar1=0.0, scalar2=None,
                                                  op0=ALU.max), deps=(o3,))
    o5 = s.op("scalar", lambda e: e.activation(out=mtmp[:], in_=mtmp[:], func=AF.Exp, scale=lg_col),
              deps=(o4, o_cols))
    o_mT = s.op("vector", lambda e: e.tensor_mul(out=mT[:], in0=mT[:], in1=mtmp[:]), deps=(o5,))

    wwT_bf = [c.sb(f"wwT{k}", [128, 1024], BF16) for k in range(KT)]
    wrT_bf = [c.sb(f"wrT{k}", [128, 1024], BF16) for k in range(KT)]
    wstage = [c.sb(f"wstage{i}", [128, 1024], F32) for i in range(2)]
    dep_w = None
    prev_cast = [None, None]
    for idx, (ap, tiles) in enumerate(((wwT_ap, wwT_bf), (wrT_ap, wrT_bf))):
        for k in range(KT):
            par = (idx * KT + k) % 2
            ld = s.dma("sync", wstage[par][:], ap[128 * k:128 * (k + 1), :], deps=(prev_cast[par],))
            cp = s.op("vector", lambda e, t=tiles[k], par=par: e.tensor_copy(out=t[:], in_=wstage[par][:]),
                      deps=(ld,))
            prev_cast[par] = cp
            dep_w = cp

    zops = [s.op("vector", lambda e, k=k: e.memset(x_fm[k][:], 0.0)) for k in range(KT)]
    xs_f = [c.sb(f"xsf{i}", [128, 1024], F32) for i in range(2)]
    xs_b = [c.sb(f"xsb{i}", [128, 1024], BF16) for i in range(2)]
    prev_cast = [None, None]
    last_tr = [None, None]
    last_evac = None
    for t in range(TOK // 128):
        par = t % 2
        b = t // 16
        col0 = BATCH_BASE[b] + (t % 16) * 128
        ld = s.dma("sync", xs_f[par][:], x0_ap[128 * t:128 * (t + 1), :], deps=(prev_cast[par],))
        cast = s.op("vector", lambda e, par=par: e.tensor_copy(out=xs_b[par][:], in_=xs_f[par][:]),
                    deps=(ld, last_tr[par]))
        prev_cast[par] = cast
        for k in range(KT):
            pk = k % 2
            tr = s.op("tensor", lambda e, par=par, k=k, pk=pk: e.transpose(
                pTx[pk][:], xs_b[par][:, 128 * k:128 * (k + 1)], ident_bf[:]),
                deps=(cast, dep_id, dep_pT[pk]))
            ev = s.op("scalar", lambda e, k=k, col0=col0, pk=pk: e.copy(
                x_fm[k][:, col0:col0 + 128], pTx[pk][:]),
                deps=(tr, zops[k]))
            dep_pT[pk] = ev
            last_evac = ev
            last_tr[par] = tr
    load_done = last_evac

    stateT = [c.sb(f"st{j}", [128, 1024], BF16) for j in range(KT)]
    vT = c.sb("vT", [128, 1024], BF16)
    vgw = c.sb("vgw", [128, 1024], BF16)
    reads_bf = c.sb("reads_bf", [128, 1024], BF16)
    wkT = [c.sb("wkT0", [128, 1024], BF16), c.sb("wkT1", [128, 1024], BF16)]
    smT = [c.sb("smT0", [128, 128], BF16), c.sb("smT1", [128, 128], BF16)]
    rkgp = c.sb("rkgp", [128, 1024], BF16)
    r2tmp = c.sb("r2tmp", [128, 128], BF16)
    prev_r2 = [None]

    eps_col = c.sb("eps_col", [128, 1], F32)
    o_eps = s.op("vector", lambda e: e.memset(eps_col[:], 1e-5))
    ones_bf = c.sb("ones_bf", [128, 1], BF16)
    o_onesb = s.op("vector", lambda e: e.memset(ones_bf[:], 1.0))
    sq_bf = c.sb("sq_bf", [128, 512], BF16)
    rows = c.sb("rows", [1, 1024], F32)
    row2 = c.sb("row2", [1, 512], F32)
    tmp_ln = c.sb("tmp_ln", [128, 512], BF16)

    last_stt = [None] * KT
    pe_tail = load_done
    last_resid = None
    last_ln = None

    def emit_early(par_dst, t0, extra_dep):
        evs = []
        for k in range(KT):
            pk = k % 2
            tr = s.op("tensor", lambda e, k=k, t0=t0, pk=pk: e.transpose(
                pTx[pk][:], x_fm[k][:, t0 - 1:t0 + 127], ident_bf[:]),
                deps=(dep_pT[pk], extra_dep, load_done))
            ev = s.op("scalar", lambda e, k=k, pk=pk: e.copy(
                wkT[par_dst][:, 128 * k:128 * (k + 1)], pTx[pk][:]), deps=(tr,))
            dep_pT[pk] = ev
            evs.append(ev)
        mmS = None
        for k in range(KT):
            mmS = s.op("tensor", lambda e, k=k, t0=t0: e.matmul(
                pb[2][:, 0:128], x_fm[k][:, t0 - 1:t0 + 127], x_fm[k][:, t0:t0 + CHUNK],
                start=(k == 0), stop=(k == KT - 1)),
                deps=(dep_pb[2], extra_dep, load_done))
        ev_smT = s.op("vector", lambda e: e.tensor_tensor(
            out=smT[par_dst][:], in0=pb[2][:, 0:128], in1=mT[:], op=ALU.mult), deps=(mmS, o_mT, extra_dep))
        dep_pb[2] = ev_smT
        return evs, ev_smT, mmS

    for b in range(B_PER_CORE):
        base = BATCH_BASE[b]
        zst = [s.op("vector", lambda e, j=j: e.memset(stateT[j][:], 0.0), deps=(last_stt[j],))
               for j in range(KT)]
        last_stt = zst

        wk_evs0, smT_ev0, smS_mm0 = emit_early(0, base, pe_tail)
        wk_ready = [wk_evs0, [None] * KT]
        smT_ready = [smT_ev0, None]
        sm_mm = [smS_mm0, None]

        for ci in range(NCHUNK):
            par = ci % 2
            t0 = base + CHUNK * ci

            next_evs, next_sm, next_sm_mm = (None, None, None)
            if ci < NCHUNK - 1:
                next_evs, next_sm, next_sm_mm = emit_early(1 - par, t0 + CHUNK, pe_tail)
                wk_ready[1 - par] = next_evs
                smT_ready[1 - par] = next_sm
                sm_mm[1 - par] = next_sm_mm

            vstop = [None, None]
            for half in range(2):
                for k in range(KT):
                    vstop[half] = s.op("tensor", lambda e, half=half, k=k, t0=t0: e.matmul(
                        pb[half][:, 0:512], x_fm[k][:, t0:t0 + CHUNK],
                        wwT_bf[k][:, 512 * half:512 * (half + 1)],
                        start=(k == 0), stop=(k == KT - 1)),
                        deps=(dep_pb[half], dep_w, load_done))
            ev_vT = []
            for half in range(2):
                e1 = s.op("vector", lambda e, half=half: e.tensor_copy(
                    out=vT[:, 512 * half:512 * (half + 1)], in_=pb[half][:, 0:512]),
                    deps=(vstop[half], pe_tail))
                ev_vT.append(e1)
                dep_pb[half] = e1
            ev_vgw = s.op("vector", lambda e: e.tensor_scalar(
                out=vgw[:], in0=vT[:], scalar1=gw_col[:, 0:1], scalar2=None, op0=ALU.mult),
                deps=(ev_vT[0], ev_vT[1], o_gw, pe_tail))

            ev_rkgp = []
            for k in range(KT):
                ev_rkgp.append(s.op("vector", lambda e, k=k, t0=t0: e.tensor_tensor(
                    out=rkgp[:, 128 * k:128 * (k + 1)], in0=x_fm[k][:, t0:t0 + CHUNK], in1=gp_bcast[:],
                    op=ALU.mult), deps=(o_gp, pe_tail)))

            ev_reads = []
            for i in range(KT):
                bank = 3 if i % 2 == 0 else 2
                lo = 0 if i % 2 == 0 else 128
                for j in range(KT):
                    s.op("tensor", lambda e, i=i, j=j, bank=bank, lo=lo: e.matmul(
                        pb[bank][:, lo:lo + 128], stateT[j][:, 128 * i:128 * (i + 1)],
                        rkgp[:, 128 * j:128 * (j + 1)], start=(j == 0), stop=False),
                        deps=(dep_pb[bank], ev_rkgp[j], last_stt[j]))
                mmop = s.op("tensor", lambda e, i=i, bank=bank, par=par, lo=lo: e.matmul(
                    pb[bank][:, lo:lo + 128], vT[:, 128 * i:128 * (i + 1)], smT[par][:],
                    start=False, stop=True),
                    deps=(ev_vT[i // 4], smT_ready[par]))
                ev = s.op("vector", lambda e, i=i, bank=bank, lo=lo: e.tensor_copy(
                    out=reads_bf[:, 128 * i:128 * (i + 1)], in_=pb[bank][:, lo:lo + 128]), deps=(mmop,))
                dep_pb[bank] = ev
                ev_reads.append(ev)

            resid_ops = []
            for eidx in range(KT):
                bank = eidx % 2
                mmop = None
                for i in range(KT):
                    mmop = s.op("tensor", lambda e, eidx=eidx, i=i, bank=bank: e.matmul(
                        pb[bank][:, 0:128], wrT_bf[i][:, 128 * eidx:128 * (eidx + 1)],
                        reads_bf[:, 128 * i:128 * (i + 1)],
                        start=(i == 0), stop=(i == KT - 1)),
                        deps=(dep_pb[bank], ev_reads[i], dep_w))
                ev0 = s.op("vector", lambda e, eidx=eidx, t0=t0, bank=bank: e.scalar_tensor_tensor(
                    out=r2tmp[:], in0=pb[bank][:, 0:128], scalar=al_col[:, 0:1],
                    in1=x_fm[eidx][:, t0:t0 + CHUNK], op0=ALU.mult, op1=ALU.add),
                    deps=(mmop, o_cols, prev_r2[0]))
                ev = s.op("vector", lambda e, eidx=eidx, t0=t0: e.tensor_add(
                    out=x_fm[eidx][:, t0:t0 + CHUNK], in0=r2tmp[:],
                    in1=x_fm[eidx][:, t0:t0 + CHUNK]),
                    deps=(ev0,
                          next_evs[eidx] if next_evs else None,
                          next_sm_mm if (next_sm_mm and eidx == KT - 1) else None))
                prev_r2[0] = ev
                dep_pb[bank] = ev
                resid_ops.append(ev)
            last_resid = resid_ops[-1]

            new_stt = []
            pe_last = None
            for j in range(KT):
                stts = None
                for half in range(2):
                    bank = 4 + half
                    mmop = s.op("tensor", lambda e, j=j, half=half, bank=bank, par=par: e.matmul(
                        pb[bank][:, 0:512], wkT[par][:, 128 * j:128 * (j + 1)],
                        vgw[:, 512 * half:512 * (half + 1)], start=True, stop=True),
                        deps=(dep_pb[bank], ev_vgw, wk_ready[par][j], last_stt[j]))
                    pe_last = mmop
                    stts = s.op("vector", lambda e, j=j, half=half, bank=bank: e.scalar_tensor_tensor(
                        out=stateT[j][:, 512 * half:512 * (half + 1)],
                        in0=stateT[j][:, 512 * half:512 * (half + 1)],
                        scalar=gC_col[:, 0:1], in1=pb[bank][:, 0:512], op0=ALU.mult, op1=ALU.add),
                        deps=(mmop, o_cols))
                    dep_pb[bank] = stts
                new_stt.append(stts)
            last_stt = new_stt
            pe_tail = pe_last

        prev_mm2 = None
        for q in range(4):
            cols0 = base + 512 * q
            mm1 = None
            mm2 = None
            sq_ops = []
            for k in range(KT):
                sq = s.op("scalar", lambda e, k=k, cols0=cols0: e.activation(
                    out=sq_bf[:], in_=x_fm[k][:, cols0:cols0 + 512], func=AF.Square),
                    deps=(last_resid, prev_mm2))
                mm1 = s.op("tensor", lambda e, k=k, cols0=cols0: e.matmul(
                    pb[2][0:1, 0:512], ones_bf[:, 0:1], x_fm[k][:, cols0:cols0 + 512],
                    start=(k == 0), stop=(k == KT - 1)),
                    deps=(o_onesb, last_resid, dep_pb[2]))
                mm2 = s.op("tensor", lambda e, k=k: e.matmul(
                    pb[3][0:1, 0:512], ones_bf[:, 0:1], sq_bf[:],
                    start=(k == 0), stop=(k == KT - 1)),
                    deps=(sq, dep_pb[3]))
                prev_mm2 = mm2
                sq_ops.append(sq)
            o1 = s.op("vector", lambda e: e.tensor_scalar(out=rows[0:1, 0:512], in0=pb[2][0:1, 0:512],
                                                          scalar1=1.0 / D, scalar2=None, op0=ALU.mult),
                      deps=(mm1,))
            o2 = s.op("vector", lambda e: e.tensor_scalar(out=rows[0:1, 512:1024], in0=pb[3][0:1, 0:512],
                                                          scalar1=1.0 / D, scalar2=None, op0=ALU.mult),
                      deps=(mm2,))
            dep_pb[2] = o1
            dep_pb[3] = o2
            msq = s.op("vector", lambda e: e.tensor_mul(out=row2[:], in0=rows[0:1, 0:512],
                                                        in1=rows[0:1, 0:512]), deps=(o1,))
            var = s.op("vector", lambda e: e.tensor_sub(out=rows[0:1, 512:1024], in0=rows[0:1, 512:1024],
                                                        in1=row2[:]), deps=(msq, o2))
            ln1 = s.op("scalar", lambda e: e.activation(out=rows[0:1, 512:1024], in_=rows[0:1, 512:1024],
                                                        func=AF.Ln, bias=eps_col[0:1, 0:1]), deps=(var, o_eps))
            rstd = s.op("scalar", lambda e: e.activation(out=rows[0:1, 512:1024], in_=rows[0:1, 512:1024],
                                                         func=AF.Exp, scale=-0.5), deps=(ln1,))
            mrs = s.op("vector", lambda e: e.tensor_mul(out=rows[0:1, 0:512], in0=rows[0:1, 0:512],
                                                        in1=rows[0:1, 512:1024]), deps=(rstd,))
            bc1 = s.op("tensor", lambda e: e.matmul(pb[0][:, 0:512], ones_row[0:1, :], rows[0:1, 512:1024],
                                                    start=True, stop=True), deps=(mrs, o_ones, dep_pb[0]))
            bc2 = s.op("tensor", lambda e: e.matmul(pb[1][:, 0:512], ones_row[0:1, :], rows[0:1, 0:512],
                                                    start=True, stop=True), deps=(mrs, o_ones, dep_pb[1]))
            norm_last = None
            for k in range(KT):
                t1 = s.op("vector", lambda e, k=k, cols0=cols0: e.tensor_tensor(
                    out=tmp_ln[:], in0=x_fm[k][:, cols0:cols0 + 512], in1=pb[0][:, 0:512], op=ALU.mult),
                    deps=(bc1, sq_ops[k], norm_last))
                t2 = s.op("vector", lambda e, k=k, cols0=cols0: e.tensor_tensor(
                    out=x_fm[k][:, cols0:cols0 + 512], in0=tmp_ln[:], in1=pb[1][:, 0:512], op=ALU.subtract),
                    deps=(t1, bc2))
                norm_last = t2
            dep_pb[0] = norm_last
            dep_pb[1] = norm_last
            last_ln = norm_last
        pe_tail = last_ln

    return last_ln


def build_A():
    c = Ctx()
    x0 = c.inp("x0", [TOK, D], F32)
    wwT = c.inp("wwT", [D, D], F32)
    wrT = c.inp("wrT", [D, D], F32)
    dec = c.inp("dec", [1, 1], F32)
    la = c.inp("la", [1, 1], F32)
    x1 = c.outp("x1", [KT, 128, TOK], BF16)

    x_fm = [c.sb(f"xfm{k}", [128, XCOLS], BF16) for k in range(KT)]
    done = emit_l0_attn(c, x0, wwT, wrT, dec, la, x_fm)
    s = c.seq
    for k in range(KT):
        for b in range(B_PER_CORE):
            s.dma("sync", x1[k, :, b * S:(b + 1) * S],
                  x_fm[k][:, BATCH_BASE[b]:BATCH_BASE[b] + S], deps=(done,))
    return c.finish()


def prep_A(inputs):
    x0 = f32c(np.asarray(inputs["L0_attn_out"]).reshape(NCORES, TOK, D))
    wwT = f32c(np.asarray(inputs["L0_attn_W_write"]).T)
    wrT = f32c(np.asarray(inputs["L0_attn_W_read"]).T)
    dec = f32c(np.asarray(inputs["L0_attn_decay"]).reshape(1, 1))
    la = f32c(np.asarray(inputs["L0_attn_log_alpha"]).reshape(1, 1))
    return [{"x0": x0[cc], "wwT": wwT, "wrT": wrT, "dec": dec, "la": la}
            for cc in range(NCORES)]


def run_A(inputs):
    nc = build_A()
    res = run_bass_kernel_spmd(nc, prep_A(inputs), core_ids=list(range(NCORES)))
    return [np.asarray(r["x1"]) for r in res.results]


def kernel(**inputs):
    return kernel_fused(**inputs)


def kernel_unfused(**inputs):
    x1 = run_A(inputs)
    x2 = run_B(inputs, x1)
    rc = run_C(inputs, x2)
    x3tok = [np.asarray(r["x3tok"]) for r in rc]
    oh = [np.asarray(r["oh_out"]) for r in rc]
    gate = [np.asarray(r["gate_out"]) for r in rc]
    rd = run_D(inputs, x3tok, oh)
    y = [np.asarray(r["y_out"]) for r in rd]
    sidx = [np.asarray(r["sidx_out"]) for r in rd]
    outs = run_E(y, x3tok, sidx, gate)
    return np.concatenate([o.reshape(B_PER_CORE, S, D) for o in outs], axis=0)



def emit_ln_block(c, s, pb, x_tiles, cols0, width, dep_in, ln, dep_pb):
    prev_mm2 = dep_pb.get("ln_mm2")
    mm1 = None
    mm2 = None
    sq_ops = []
    for k in range(KT):
        sq = s.op("scalar", lambda e, k=k: e.activation(
            out=ln["sq_bf"][:, 0:width], in_=x_tiles[k][:, cols0:cols0 + width], func=AF.Square),
            deps=(dep_in, prev_mm2))
        mm1 = s.op("tensor", lambda e, k=k: e.matmul(
            pb[2][0:1, 0:width], ln["ones_bf"][:, 0:1], x_tiles[k][:, cols0:cols0 + width],
            start=(k == 0), stop=(k == KT - 1)),
            deps=(ln["o_onesb"], dep_in, dep_pb[2]))
        mm2 = s.op("tensor", lambda e, k=k: e.matmul(
            pb[3][0:1, 0:width], ln["ones_bf"][:, 0:1], ln["sq_bf"][:, 0:width],
            start=(k == 0), stop=(k == KT - 1)),
            deps=(sq, dep_pb[3]))
        prev_mm2 = mm2
        sq_ops.append(sq)
    dep_pb["ln_mm2"] = mm2
    rows = ln["rows"]
    row2 = ln["row2"]
    o1 = s.op("vector", lambda e: e.tensor_scalar(out=rows[0:1, 0:width], in0=pb[2][0:1, 0:width],
                                                  scalar1=1.0 / D, scalar2=None, op0=ALU.mult), deps=(mm1,))
    o2 = s.op("vector", lambda e: e.tensor_scalar(out=rows[0:1, 512:512 + width], in0=pb[3][0:1, 0:width],
                                                  scalar1=1.0 / D, scalar2=None, op0=ALU.mult), deps=(mm2,))
    dep_pb[2] = o1
    dep_pb[3] = o2
    msq = s.op("vector", lambda e: e.tensor_mul(out=row2[0:1, 0:width], in0=rows[0:1, 0:width],
                                                in1=rows[0:1, 0:width]), deps=(o1,))
    var = s.op("vector", lambda e: e.tensor_sub(out=rows[0:1, 512:512 + width], in0=rows[0:1, 512:512 + width],
                                                in1=row2[0:1, 0:width]), deps=(msq, o2))
    ln1 = s.op("scalar", lambda e: e.activation(out=rows[0:1, 512:512 + width], in_=rows[0:1, 512:512 + width],
                                                func=AF.Ln, bias=ln["eps"][0:1, 0:1]), deps=(var, ln["o_eps"]))
    rstd = s.op("scalar", lambda e: e.activation(out=rows[0:1, 512:512 + width], in_=rows[0:1, 512:512 + width],
                                                 func=AF.Exp, scale=-0.5), deps=(ln1,))
    mrs = s.op("vector", lambda e: e.tensor_mul(out=rows[0:1, 0:width], in0=rows[0:1, 0:width],
                                                in1=rows[0:1, 512:512 + width]), deps=(rstd,))
    bc1 = s.op("tensor", lambda e: e.matmul(pb[0][:, 0:width], ln["ones_row"][0:1, :], rows[0:1, 512:512 + width],
                                            start=True, stop=True), deps=(mrs, ln["o_ones"], dep_pb[0]))
    bc2 = s.op("tensor", lambda e: e.matmul(pb[1][:, 0:width], ln["ones_row"][0:1, :], rows[0:1, 0:width],
                                            start=True, stop=True), deps=(mrs, ln["o_ones"], dep_pb[1]))
    norm_last = None
    for k in range(KT):
        t1 = s.op("vector", lambda e, k=k: e.tensor_tensor(
            out=ln["tmp_ln"][:, 0:width], in0=x_tiles[k][:, cols0:cols0 + width], in1=pb[0][:, 0:width],
            op=ALU.mult), deps=(bc1, sq_ops[k], norm_last))
        t2 = s.op("vector", lambda e, k=k: e.tensor_tensor(
            out=x_tiles[k][:, cols0:cols0 + width], in0=ln["tmp_ln"][:, 0:width], in1=pb[1][:, 0:width],
            op=ALU.subtract), deps=(t1, bc2))
        norm_last = t2
    dep_pb[0] = norm_last
    dep_pb[1] = norm_last
    return norm_last


_LN_N = [0]


def make_ln_kit(c, s, sq=None, tmp=None, rows=None, row2=None):
    _LN_N[0] += 1
    p = f"ln{_LN_N[0]}_"
    ln = {}
    ln["ones_bf"] = c.sb(p + "ones_bf", [128, 1], BF16)
    ln["o_onesb"] = s.op("vector", lambda e: e.memset(ln["ones_bf"][:], 1.0))
    ln["ones_row"] = c.sb(p + "ones_row", [1, 128], F32)
    ln["o_ones"] = s.op("vector", lambda e: e.memset(ln["ones_row"][:], 1.0))
    ln["sq_bf"] = sq if sq is not None else c.sb(p + "sq", [128, 512], BF16)
    ln["rows"] = rows if rows is not None else c.sb(p + "rows", [1, 1024], F32)
    ln["row2"] = row2 if row2 is not None else c.sb(p + "row2", [1, 512], F32)
    ln["tmp_ln"] = tmp if tmp is not None else c.sb(p + "tmp", [128, 512], BF16)
    ln["eps"] = c.sb(p + "eps", [1, 1], F32)
    ln["o_eps"] = s.op("vector", lambda e: e.memset(ln["eps"][:], 1e-5))
    return ln


def emit_l0_ffn(c, xin_ap=None, xout_ap=None, x_fm=None, dep_in=None):
    nc, s = c.nc, c.seq
    w1t = c.inp("w1t", [32, D, 128], F32)
    w2t = c.inp("w2t", [32, D, 128], F32)
    h1t = c.inp("h1t", [32, 4096, 128], F32)
    h2t = c.inp("h2t", [32, 4096, 128], F32)
    w3t = c.inp("w3t", [KT, 4096, 128], F32)
    s1v = c.inp("s1v", [128, 32], F32)
    s2v = c.inp("s2v", [128, 32], F32)
    sh1v = c.inp("sh1v", [128, 32], F32)
    sh2v = c.inp("sh2v", [128, 32], F32)
    s3v = c.inp("s3v", [128, KT], F32)

    g1d = c.dram("g1d", [4096, 1024], BF16)
    g2d = c.dram("g2d", [4096, 1024], BF16)

    pb = [c.ps(f"fb{i}", [128, 512], F32) for i in range(6)]
    dep_pb = {i: None for i in range(6)}

    if x_fm is None:
        x_half = [c.sb(f"xh{k}", [128, 1024], BF16) for k in range(KT)]
    hbuf = [c.sb(f"hb{o}", [128, 1024], BF16) for o in range(32)]
    wbg = [c.sb(f"wbg{i}", [128, 4096], BF16) for i in range(2)]
    wsm = [wbg[0][:, 0:1024], wbg[0][:, 1024:2048], wbg[1][:, 0:1024], wbg[1][:, 1024:2048]]
    scl = c.sb("scl", [128, 32 * 4 + KT], F32)
    g1r = c.sb("g1r", [128, 2048], BF16)
    ab = [g1r[:, 0:512], g1r[:, 512:1024]]
    ln = make_ln_kit(c, s, sq=g1r[:, 1024:1536], tmp=g1r[:, 1536:2048])

    ld_s = []
    for i, ap in enumerate((s1v, s2v, sh1v, sh2v)):
        ld_s.append(s.dma("sync", scl[:, 32 * i:32 * (i + 1)], ap))
    ld_s.append(s.dma("sync", scl[:, 128:128 + KT], s3v))
    dep_scl = ld_s[-1]
    S1, S2, SH1, SH2, S3 = (lambda o: scl[:, o:o + 1],
                            lambda o: scl[:, 32 + o:33 + o],
                            lambda o: scl[:, 64 + o:65 + o],
                            lambda o: scl[:, 96 + o:97 + o],
                            lambda o: scl[:, 128 + o:129 + o])

    final_op = dep_in
    for half in range(4):
        hc0 = 1024 * half
        if x_fm is not None:
            cb0 = BATCH_BASE[half // 2] + 1024 * (half % 2)
            x_half = [x_fm[k][:, cb0:cb0 + 1024] for k in range(KT)]
            x_lds = [final_op] * KT
            dep_x = final_op
        else:
            x_lds = []
            for k in range(KT):
                x_lds.append(s.dma("sync", x_half[k][:], xin_ap[k, :, hc0:hc0 + 1024],
                                   deps=(final_op,)))
            dep_x = x_lds[-1]

        wdep = [dep_x, dep_x]
        last_h = {}
        prev_evh = [None]
        for o in range(32):
            par = o % 2
            ldw = s.dma("gpsimd", wsm[2 * par][:].rearrange("p (a f) -> p a f", a=8),
                        w1t[o].rearrange("(a p) f -> p a f", p=128), deps=(wdep[par],))
            ldw2 = s.dma("gpsimd", wsm[2 * par + 1][:].rearrange("p (a f) -> p a f", a=8),
                         w2t[o].rearrange("(a p) f -> p a f", p=128), deps=(wdep[par], ldw))
            mm_last = None
            for q in range(2):
                qp = q % 2
                ma = None
                mb = None
                for k in range(KT):
                    ma = s.op("tensor", lambda e, o=o, q=q, k=k, par=par, qp=qp: e.matmul(
                        pb[qp][:, 0:512], wsm[2 * par][:, 128 * k:128 * (k + 1)],
                        x_half[k][:, 512 * q:512 * (q + 1)], start=(k == 0), stop=(k == KT - 1)),
                        deps=(dep_pb[qp], ldw, x_lds[k]))
                    mb = s.op("tensor", lambda e, o=o, q=q, k=k, par=par, qp=qp: e.matmul(
                        pb[2 + qp][:, 0:512], wsm[2 * par + 1][:, 128 * k:128 * (k + 1)],
                        x_half[k][:, 512 * q:512 * (q + 1)], start=(k == 0), stop=(k == KT - 1)),
                        deps=(dep_pb[2 + qp], ldw2, x_lds[k]))
                eva = s.op("scalar", lambda e, o=o, qp=qp: e.activation(
                    out=ab[0][:], in_=pb[qp][:, 0:512], func=AF.Silu, scale=S1(o)),
                    deps=(ma, dep_scl, prev_evh[0]))
                evb = s.op("vector", lambda e, o=o, qp=qp: e.tensor_scalar(
                    out=ab[1][:], in0=pb[2 + qp][:, 0:512], scalar1=S2(o), scalar2=None, op0=ALU.mult),
                    deps=(mb, dep_scl, prev_evh[0]))
                evh = s.op("vector", lambda e, o=o, q=q: e.tensor_mul(
                    out=hbuf[o][:, 512 * q:512 * (q + 1)], in0=ab[0][:], in1=ab[1][:]),
                    deps=(eva, evb))
                prev_evh[0] = evh
                dep_pb[qp] = eva
                dep_pb[2 + qp] = evh
                mm_last = evh
            wdep[par] = mm_last
            last_h[o] = mm_last

        def hid_stage(win_ap, rhs_tiles, rhs_dep, scale_fn, out_dram, func, init_dep):
            wdep2 = [init_dep, init_dep]
            st_ops = []
            for o in range(32):
                par = o % 2
                ldw = s.dma("gpsimd", wbg[par][:].rearrange("p (a f) -> p a f", a=32),
                            win_ap[o].rearrange("(a p) f -> p a f", p=128), deps=(wdep2[par],))
                prev_st = st_ops[-1] if st_ops else None
                row_evs = []
                for q in range(2):
                    qp = q % 2
                    mm = None
                    for k in range(32):
                        mm = s.op("tensor", lambda e, o=o, q=q, k=k, par=par, qp=qp: e.matmul(
                            pb[qp][:, 0:512], wbg[par][:, 128 * k:128 * (k + 1)],
                            rhs_tiles[k][:, 512 * q:512 * (q + 1)], start=(k == 0), stop=(k == 31)),
                            deps=(dep_pb[qp], ldw, rhs_dep if isinstance(rhs_dep, Op) else rhs_dep[k]))
                    ev = s.op("scalar", lambda e, o=o, q=q, qp=qp: e.activation(
                        out=g1r[:, 512 * q:512 * (q + 1)], in_=pb[qp][:, 0:512], func=func,
                        scale=scale_fn(o)), deps=(mm, dep_scl, prev_st))
                    dep_pb[qp] = ev
                    row_evs.append(ev)
                st = s.dma("sync", out_dram[128 * o:128 * (o + 1), :], g1r[:, 0:1024],
                           deps=tuple(row_evs))
                wdep2[par] = st
                st_ops.append(st)
            return st_ops

        st2 = hid_stage(h1t, hbuf, last_h, SH1, g1d, AF.Gelu, last_h[31])

        g1_lds = []
        for o in range(32):
            g1_lds.append(s.dma("sync", hbuf[o][:], g1d[128 * o:128 * (o + 1), :],
                                deps=(st2[o], st2[-1])))
        st3 = hid_stage(h2t, hbuf, g1_lds, SH2, g2d, AF.Gelu, st2[-1])

        g2_lds = []
        for o in range(32):
            g2_lds.append(s.dma("sync", hbuf[o][:], g2d[128 * o:128 * (o + 1), :],
                                deps=(st3[o], st3[-1])))

        wdep4 = [st3[-1], st3[-1]]
        resid_ops = []
        for eidx in range(KT):
            par = eidx % 2
            ldw = s.dma("gpsimd", wbg[par][:].rearrange("p (a f) -> p a f", a=32),
                        w3t[eidx].rearrange("(a p) f -> p a f", p=128), deps=(wdep4[par],))
            last_ev = None
            for q in range(2):
                qp = q % 2
                mm = None
                for k in range(32):
                    mm = s.op("tensor", lambda e, eidx=eidx, q=q, k=k, par=par, qp=qp: e.matmul(
                        pb[qp][:, 0:512], wbg[par][:, 128 * k:128 * (k + 1)],
                        hbuf[k][:, 512 * q:512 * (q + 1)], start=(k == 0), stop=(k == 31)),
                        deps=(dep_pb[qp], ldw, g2_lds[k]))
                ev = s.op("vector", lambda e, eidx=eidx, q=q, qp=qp: e.scalar_tensor_tensor(
                    out=x_half[eidx][:, 512 * q:512 * (q + 1)], in0=pb[qp][:, 0:512],
                    scalar=S3(eidx), in1=x_half[eidx][:, 512 * q:512 * (q + 1)],
                    op0=ALU.mult, op1=ALU.add), deps=(mm, dep_scl))
                dep_pb[qp] = ev
                last_ev = ev
            wdep4[par] = last_ev
            resid_ops.append(last_ev)
        dep_resid = resid_ops[-1]

        st_final = []
        for q in range(2):
            lnop = emit_ln_block(c, s, pb, x_half, 512 * q, 512, dep_resid, ln, dep_pb)
            if xout_ap is not None:
                for k in range(KT):
                    st_final.append(s.dma("sync", xout_ap[k, :, hc0 + 512 * q:hc0 + 512 * (q + 1)],
                                          x_half[k][:, 512 * q:512 * (q + 1)], deps=(lnop,)))
            final_op = lnop
        if st_final:
            final_op = st_final[-1]
    return final_op, st_final


def build_B():
    c = Ctx()
    xin = c.inp("xin", [KT, 128, TOK], BF16)
    xout = c.outp("xout", [KT, 128, TOK], BF16)
    emit_l0_ffn(c, xin, xout)
    return c.finish()


def prep_B_weights(inputs):
    def tileT(w):
        wT = f32c(np.asarray(w).T)
        I_, O_ = wT.shape
        return f32c(wT.reshape(I_, O_ // 128, 128).transpose(1, 0, 2))
    def colv(v, ntiles):
        return f32c(np.asarray(v).reshape(ntiles, 128).T)
    return {
        "w1t": tileT(inputs["L0_ffn_w1_w"]),
        "w2t": tileT(inputs["L0_ffn_w2_w"]),
        "h1t": tileT(inputs["L0_ffn_h1_w"]),
        "h2t": tileT(inputs["L0_ffn_h2_w"]),
        "w3t": tileT(inputs["L0_ffn_w3_w"]),
        "s1v": colv(inputs["L0_ffn_w1_s"], 32),
        "s2v": colv(inputs["L0_ffn_w2_s"], 32),
        "sh1v": colv(inputs["L0_ffn_h1_s"], 32),
        "sh2v": colv(inputs["L0_ffn_h2_s"], 32),
        "s3v": colv(inputs["L0_ffn_w3_s"], KT),
    }


def run_B(inputs, x1_list):
    nc = build_B()
    w = prep_B_weights(inputs)
    maps = [dict(w, xin=np.ascontiguousarray(x1_list[cc])) for cc in range(NCORES)]
    res = run_bass_kernel_spmd(nc, maps, core_ids=list(range(NCORES)))
    return [np.asarray(r["xout"]) for r in res.results]



def emit_l1_attn(c, x_fm_ext=None, dep_in=None, gate_sb=None,
                 x3tok_dram=None, oh_dram=None):
    nc, s = c.nc, c.seq
    fused = x_fm_ext is not None
    if not fused:
        xin = c.inp("xin", [KT, 128, TOK], BF16)
    wq_t = c.inp("wq_t", [KT, D, 128], F32)
    wk_f = c.inp("wk_f", [D, D], F32)
    wv_f = c.inp("wv_f", [D, D], F32)
    wo_t = c.inp("wo_t", [KT, D, 128], F32)
    wdn = c.inp("wdn", [D, 256], F32)
    wuk = c.inp("wuk", [256, D], F32)
    wuv = c.inp("wuv", [256, D], F32)
    dtab = c.inp("dtab", [28, 128, 512], BF16)
    wg = c.inp("wg", [D, 16], F32)
    bg = c.inp("bg", [16, 1], F32)

    if not fused:
        xout = c.outp("xout", [KT, 128, TOK], BF16)
        x3tok = c.outp("x3tok", [TOK, D], BF16)
        oh_out = c.outp("oh_out", [16, TOK], BF16)
        gate_out = c.outp("gate_out", [128, TOK // 128], F32)
    else:
        x3tok = x3tok_dram
        oh_out = oh_dram

    pb = [c.ps(f"cb{i}", [128, 512], F32) for i in range(6)]
    pTx = [c.ps("cTa", [128, 128], BF16), c.ps("cTb", [128, 128], BF16)]
    dep_pb = {i: None for i in range(6)}
    dep_pT = {0: None, 1: None}
    ln = make_ln_kit(c, s)

    ident_bf = c.sb("idb", [128, 128], BF16)
    ident_f = c.sb("idf", [128, 128], F32)
    ii = c.sb("ci", [128, 128], I32)
    jj = c.sb("cj", [128, 128], I32)
    o1 = s.op("gpsimd", lambda e: e.iota(ii[:], pattern=[[1, 128]], base=0, channel_multiplier=0))
    o2 = s.op("gpsimd", lambda e: e.iota(jj[:], pattern=[[0, 128]], base=0, channel_multiplier=1))
    dep_idb = s.op("vector", lambda e: e.tensor_tensor(out=ident_bf[:], in0=ii[:], in1=jj[:], op=ALU.is_equal),
                   deps=(o1, o2))
    dep_idf = s.op("vector", lambda e: e.tensor_tensor(out=ident_f[:], in0=ii[:], in1=jj[:], op=ALU.is_equal),
                   deps=(o1, o2))

    if fused:
        x_fm = x_fm_ext
        x_lds = [dep_in] * KT
        CB = BATCH_BASE
        def qcol(q):
            return BATCH_BASE[q // 4] + 512 * (q % 4)
        def tcol(tt):
            return BATCH_BASE[tt // 16] + 128 * (tt % 16)
    else:
        x_fm = [c.sb(f"cx{k}", [128, TOK], BF16) for k in range(KT)]
        x_lds = [s.dma("sync", x_fm[k][:], xin[k]) for k in range(KT)]
        CB = [0, S]
        def qcol(q):
            return 512 * q
        def tcol(tt):
            return 128 * tt

    wsum_bf = [c.sb(f"ws{k}", [128, D], BF16) for k in range(KT)]
    wdn_bf = [c.sb(f"wd{k}", [128, 256], BF16) for k in range(KT)]
    wuk_bf = [c.sb(f"wukb{l}", [128, D], BF16) for l in range(2)]
    wuv_bf = [c.sb(f"wuvb{l}", [128, D], BF16) for l in range(2)]
    wg_bf = [c.sb(f"wgb{k}", [128, 16], BF16) for k in range(KT)]
    stg = [c.sb(f"cstg{i}", [128, 1024], F32) for i in range(2)]
    prev_use = [None, None]
    dep_w = []
    for k in range(KT):
        l1 = s.dma("sync", stg[0][:], wk_f[128 * k:128 * (k + 1), :], deps=(prev_use[0],))
        l2 = s.dma("sync", stg[1][:], wv_f[128 * k:128 * (k + 1), :], deps=(prev_use[1],))
        ad = s.op("vector", lambda e: e.tensor_add(out=stg[0][:], in0=stg[0][:], in1=stg[1][:]), deps=(l1, l2))
        cp = s.op("vector", lambda e, k=k: e.tensor_copy(out=wsum_bf[k][:], in_=stg[0][:]), deps=(ad,))
        prev_use = [cp, cp]
        dep_w.append(cp)
    l_dn = [s.dma("gpsimd", wdn_bf[k][:], wdn[128 * k:128 * (k + 1), :]) for k in range(KT)]
    l_uk = [s.dma("gpsimd", wuk_bf[l][:], wuk[128 * l:128 * (l + 1), :]) for l in range(2)]
    l_uv = [s.dma("gpsimd", wuv_bf[l][:], wuv[128 * l:128 * (l + 1), :]) for l in range(2)]
    l_wg = [s.dma("gpsimd", wg_bf[k][:], wg[128 * k:128 * (k + 1), :]) for k in range(KT)]
    dtab_sb = c.sb("dtab_sb", [128, 28 * 512], BF16)
    l_dt = s.dma("sync", dtab_sb[:].rearrange("p (a f) -> p a f", a=28),
                 dtab.rearrange("a p f -> p a f"), deps=(prev_use[0],))

    kv_att = [c.sb(f"kva{j}", [128, S], BF16) for j in range(KT)]
    latent = [c.sb(f"lat{l}", [128, S], BF16) for l in range(2)]
    q_pair = c.sb("q_pair", [128, S], BF16)
    krec = c.sb("krec", [128, S], BF16)
    vA = c.sb("vA", [128, 16 * 65], BF16)
    vB = c.sb("vB", [128, 16 * 65], BF16)
    o_vA1 = s.op("vector", lambda e: e.memset(vA[:], 1.0))
    o_vB1 = s.op("vector", lambda e: e.memset(vB[:], 1.0))
    ptA = c.sb("ptA", [128, 8 * 512], BF16)
    ptB = c.sb("ptB", [128, 8 * 512], BF16)
    _stgb = stg[0][:].bitcast(BF16)
    wqp = [_stgb[:, 0:D], _stgb[:, D:2 * D]]
    rrow = stg[1][0:1, 512:1024]
    rb_sb = stg[1][0:64, 0:512]

    last_ln = None
    for b in range(B_PER_CORE):
        bc = CB[b]
        ev_kv = []
        for o in range(KT):
            qp = o % 2
            mm = None
            for q in range(4):
                for k in range(KT):
                    mm = s.op("tensor", lambda e, o=o, q=q, k=k, qp=qp, bc=bc: e.matmul(
                        pb[qp][:, 0:512], wsum_bf[k][:, 128 * o:128 * (o + 1)],
                        x_fm[k][:, bc + 512 * q:bc + 512 * (q + 1)], start=(k == 0), stop=(k == KT - 1)),
                        deps=(dep_pb[qp], dep_w[k], x_lds[k], last_ln))
                ev = s.op("vector", lambda e, o=o, q=q, qp=qp: e.tensor_copy(
                    out=kv_att[o][:, 512 * q:512 * (q + 1)], in_=pb[qp][:, 0:512]), deps=(mm,))
                dep_pb[qp] = ev
            ev_kv.append(ev)
        ev_lat = []
        for l in range(2):
            qp = l % 2
            for q in range(4):
                mm = None
                for k in range(KT):
                    mm = s.op("tensor", lambda e, l=l, q=q, k=k, qp=qp: e.matmul(
                        pb[qp][:, 0:512], wdn_bf[k][:, 128 * l:128 * (l + 1)],
                        kv_att[k][:, 512 * q:512 * (q + 1)], start=(k == 0), stop=(k == KT - 1)),
                        deps=(dep_pb[qp], l_dn[k], ev_kv[k]))
                ev = s.op("vector", lambda e, l=l, q=q, qp=qp: e.tensor_copy(
                    out=latent[l][:, 512 * q:512 * (q + 1)], in_=pb[qp][:, 0:512]), deps=(mm,))
                dep_pb[qp] = ev
            ev_lat.append(ev)
        dep_lat = ev_lat[-1]

        wq_dep = [dep_lat, dep_lat]
        av_hist = [None, None]
        att_evs = []
        for hp in range(KT):
            wpar = hp % 2
            ldq = s.dma("gpsimd", wqp[wpar][:].rearrange("p (a f) -> p a f", a=KT),
                        wq_t[hp].rearrange("(a p) f -> p a f", p=128), deps=(wq_dep[wpar],))
            ev_q = None
            for q in range(4):
                qp = q % 2
                mm = None
                for k in range(KT):
                    mm = s.op("tensor", lambda e, q=q, k=k, qp=qp, wpar=wpar, bc=bc: e.matmul(
                        pb[qp][:, 0:512], wqp[wpar][:, 128 * k:128 * (k + 1)],
                        x_fm[k][:, bc + 512 * q:bc + 512 * (q + 1)], start=(k == 0), stop=(k == KT - 1)),
                        deps=(dep_pb[qp], ldq, att_evs[-1] if att_evs else None))
                ev_q = s.op("vector", lambda e, q=q, qp=qp: e.tensor_scalar(
                    out=q_pair[:, 512 * q:512 * (q + 1)], in0=pb[qp][:, 0:512],
                    scalar1=1.0 / 32.0, scalar2=None, op0=ALU.mult),
                    deps=(mm, att_evs[-1] if att_evs else None))
                dep_pb[qp] = ev_q
            ev_kr = None
            for q in range(4):
                qp = q % 2
                mm = None
                for l in range(2):
                    mm = s.op("tensor", lambda e, hp=hp, q=q, l=l, qp=qp: e.matmul(
                        pb[qp][:, 0:512], wuk_bf[l][:, 128 * hp:128 * (hp + 1)],
                        latent[l][:, 512 * q:512 * (q + 1)], start=(l == 0), stop=(l == 1)),
                        deps=(dep_pb[qp], l_uk[l], dep_lat, att_evs[-1] if att_evs else None))
                ev_kr = s.op("vector", lambda e, q=q, qp=qp: e.tensor_copy(
                    out=krec[:, 512 * q:512 * (q + 1)], in_=pb[qp][:, 0:512]),
                    deps=(mm, att_evs[-1] if att_evs else None))
                dep_pb[qp] = ev_kr
            ev_v = None
            for tt in range(16):
                bank = 2 + (tt % 2)
                mm = None
                for l in range(2):
                    mm = s.op("tensor", lambda e, hp=hp, tt=tt, l=l, bank=bank: e.matmul(
                        pb[bank][:, 0:128], latent[l][:, 128 * tt:128 * (tt + 1)],
                        wuv_bf[l][:, 128 * hp:128 * (hp + 1)], start=(l == 0), stop=(l == 1)),
                        deps=(dep_pb[bank], l_uv[l], dep_lat, att_evs[-1] if att_evs else None))
                e1 = s.op("vector", lambda e, tt=tt, bank=bank: e.tensor_copy(
                    out=vA[:, 65 * tt:65 * tt + 64], in_=pb[bank][:, 0:64]),
                    deps=(mm, o_vA1, att_evs[-1] if att_evs else None))
                ev_v = s.op("vector", lambda e, tt=tt, bank=bank: e.tensor_copy(
                    out=vB[:, 65 * tt:65 * tt + 64], in_=pb[bank][:, 64:128]),
                    deps=(mm, o_vB1, att_evs[-1] if att_evs else None))
                dep_pb[bank] = ev_v
            for qc in range(4):
                avA = av_hist[0]
                avB = av_hist[1]
                for kh in range(2):
                    ev_ptA = []
                    ev_ptB = []
                    for ki in range(8):
                        kt = 8 * kh + ki
                        m = kt - 4 * qc + 12
                        bA = 0 if ki % 2 == 0 else 2
                        bB = 1 if ki % 2 == 0 else 3
                        mmA = s.op("tensor", lambda e, kt=kt, qc=qc, bA=bA: e.matmul(
                            pb[bA][:, 0:512], krec[0:64, 128 * kt:128 * (kt + 1)],
                            q_pair[0:64, 512 * qc:512 * (qc + 1)], start=True, stop=True),
                            deps=(dep_pb[bA], ev_kr, ev_q))
                        mmB = s.op("tensor", lambda e, kt=kt, qc=qc, bB=bB: e.matmul(
                            pb[bB][:, 0:512], krec[64:128, 128 * kt:128 * (kt + 1)],
                            q_pair[64:128, 512 * qc:512 * (qc + 1)], start=True, stop=True),
                            deps=(dep_pb[bB], ev_kr, ev_q))
                        tA = s.op("vector", lambda e, ki=ki, m=m, bA=bA: e.tensor_tensor(
                            out=ptA[:, 512 * ki:512 * (ki + 1)], in0=pb[bA][:, 0:512],
                            in1=dtab_sb[:, 512 * m:512 * (m + 1)], op=ALU.mult),
                            deps=(mmA, l_dt, avA))
                        tB = s.op("vector", lambda e, ki=ki, m=m, bB=bB: e.tensor_tensor(
                            out=ptB[:, 512 * ki:512 * (ki + 1)], in0=pb[bB][:, 0:512],
                            in1=dtab_sb[:, 512 * m:512 * (m + 1)], op=ALU.mult),
                            deps=(mmB, l_dt, avB))
                        dep_pb[bA] = tA
                        dep_pb[bB] = tB
                        eA = s.op("scalar", lambda e, ki=ki: e.activation(
                            out=ptA[:, 512 * ki:512 * (ki + 1)], in_=ptA[:, 512 * ki:512 * (ki + 1)],
                            func=AF.Exp), deps=(tA,))
                        eB = s.op("scalar", lambda e, ki=ki: e.activation(
                            out=ptB[:, 512 * ki:512 * (ki + 1)], in_=ptB[:, 512 * ki:512 * (ki + 1)],
                            func=AF.Exp), deps=(tB,))
                        ev_ptA.append(eA)
                        ev_ptB.append(eB)
                    for ki in range(8):
                        kt = 8 * kh + ki
                        avA = s.op("tensor", lambda e, tt=kt, ki=ki, kt=kt: e.matmul(
                            pb[4][0:65, 0:512], vA[:, 65 * tt:65 * tt + 65], ptA[:, 512 * ki:512 * (ki + 1)],
                            start=(kt == 0), stop=(kt == 15)),
                            deps=(dep_pb[4] if kt == 0 else None, ev_ptA[ki], ev_v))
                        avB = s.op("tensor", lambda e, tt=kt, ki=ki, kt=kt: e.matmul(
                            pb[5][0:65, 0:512], vB[:, 65 * tt:65 * tt + 65], ptB[:, 512 * ki:512 * (ki + 1)],
                            start=(kt == 0), stop=(kt == 15)),
                            deps=(dep_pb[5] if kt == 0 else None, ev_ptB[ki], ev_v))
                for head, av, p65, pt_dep in ((0, avA, 4, None), (1, avB, 5, None)):
                    bco = 0 if head == 0 else 1
                    rc = s.op("vector", lambda e, p65=p65: e.reciprocal(out=rrow[:], in_=pb[p65][64:65, 0:512]),
                              deps=(av if head == 0 else avB,))
                    bcm = s.op("tensor", lambda e, bco=bco: e.matmul(
                        pb[bco][0:64, 0:512], ln["ones_row"][0:1, 0:64], rrow[0:1, :],
                        start=True, stop=True), deps=(rc, ln["o_ones"], dep_pb[bco]))
                    cpb = s.op("vector", lambda e, bco=bco: e.tensor_copy(out=rb_sb[:], in_=pb[bco][0:64, 0:512]),
                               deps=(bcm,))
                    dep_pb[bco] = cpb
                    evo = s.op("vector", lambda e, hp=hp, qc=qc, head=head, p65=p65: e.tensor_tensor(
                        out=kv_att[hp][64 * head:64 * (head + 1), 512 * qc:512 * (qc + 1)],
                        in0=pb[p65][0:64, 0:512], in1=rb_sb[:], op=ALU.mult),
                        deps=(cpb, dep_lat))
                    dep_pb[p65] = evo
                    att_evs.append(evo)
                av_hist[0] = avA
                av_hist[1] = avB
            wq_dep[wpar] = att_evs[-1]

        wo_dep = [att_evs[-1], att_evs[-1]]
        resid = []
        for eidx in range(KT):
            wpar = eidx % 2
            ldo = s.dma("gpsimd", wqp[wpar][:].rearrange("p (a f) -> p a f", a=KT),
                        wo_t[eidx].rearrange("(a p) f -> p a f", p=128), deps=(wo_dep[wpar],))
            last_ev = None
            for q in range(4):
                qp = q % 2
                mm = None
                for j in range(KT):
                    mm = s.op("tensor", lambda e, eidx=eidx, q=q, j=j, qp=qp, wpar=wpar: e.matmul(
                        pb[qp][:, 0:512], wqp[wpar][:, 128 * j:128 * (j + 1)],
                        kv_att[j][:, 512 * q:512 * (q + 1)], start=(j == 0), stop=(j == KT - 1)),
                        deps=(dep_pb[qp], ldo, att_evs[-1]))
                ev = s.op("vector", lambda e, eidx=eidx, q=q, qp=qp, bc=bc: e.tensor_tensor(
                    out=x_fm[eidx][:, bc + 512 * q:bc + 512 * (q + 1)], in0=pb[qp][:, 0:512],
                    in1=x_fm[eidx][:, bc + 512 * q:bc + 512 * (q + 1)], op=ALU.add),
                    deps=(mm,))
                dep_pb[qp] = ev
                last_ev = ev
            wo_dep[wpar] = last_ev
            resid.append(last_ev)
        for q in range(4):
            last_ln = emit_ln_block(c, s, pb, x_fm, bc + 512 * q, 512, resid[-1], ln, dep_pb)

    _lg1 = ptA[:].bitcast(F32)
    _lg2 = ptB[:].bitcast(F32)
    def lgv(col0, w):
        if col0 < 2048:
            return _lg1[0:16, col0:col0 + w]
        return _lg2[0:16, col0 - 2048:col0 - 2048 + w]
    bg_sb = c.sb("bg_sb", [16, 1], F32)
    l_bg = s.dma("sync", bg_sb[:], bg)
    ev_lg = []
    for q in range(TOK // 512):
        qp = q % 2
        mm = None
        for k in range(KT):
            mm = s.op("tensor", lambda e, q=q, k=k, qp=qp, qc0=qcol(q): e.matmul(
                pb[qp][0:16, 0:512], wg_bf[k][:, 0:16], x_fm[k][:, qc0:qc0 + 512],
                start=(k == 0), stop=(k == KT - 1)),
                deps=(dep_pb[qp], l_wg[k], last_ln))
        ev = s.op("vector", lambda e, q=q, qp=qp: e.tensor_scalar(
            out=lgv(512 * q, 512), in0=pb[qp][0:16, 0:512],
            scalar1=bg_sb[:, 0:1], scalar2=None, op0=ALU.add), deps=(mm, l_bg, av_hist[0], av_hist[1]))
        dep_pb[qp] = ev
        ev_lg.append(ev)

    ltok = c.sb("ltok", [128, 16], F32)
    ptok = c.sb("ptok", [128, 16], F32)
    nmax = c.sb("nmax", [128, 1], F32)
    ssum = c.sb("ssum", [128, 1], F32)
    oh_sb = c.sb("oh_sb", [128, 16], BF16)
    ohfm_v = [kv_att[1][0:16, :], kv_att[2][0:16, :]]
    if gate_sb is None:
        gate_sb = c.sb("gate_sb", [128, TOK // 128], F32)
    xtok_sb = kv_att[0][:, 0:D]
    prev = {"tr": None, "ev": None, "st": None, "oh": None, "xst": None}
    for tt in range(TOK // 128):
        bank = 2 + (tt % 2)
        q = (128 * tt) // 512
        tr = s.op("tensor", lambda e, tt=tt, bank=bank: e.transpose(
            pb[bank][:, 0:16], lgv(128 * tt, 128), ident_f[0:16, 0:16]),
            deps=(ev_lg[q], dep_idf, dep_pb[bank]))
        evl = s.op("vector", lambda e, bank=bank: e.tensor_copy(out=ltok[:], in_=pb[bank][:, 0:16]),
                   deps=(tr, prev["oh"]))
        dep_pb[bank] = evl
        mx = s.op("vector", lambda e: e.reduce_max(out=nmax[:], in_=ltok[:], axis=AX.X, negate=True),
                  deps=(evl,))
        ex = s.op("scalar", lambda e: e.activation(out=ptok[:], in_=ltok[:], func=AF.Exp, bias=nmax[:, 0:1]),
                  deps=(mx, prev["st"]))
        sm = s.op("vector", lambda e: e.reduce_sum(out=ssum[:], in_=ptok[:], axis=AX.X), deps=(ex,))
        gt = s.op("vector", lambda e, tt=tt: e.reciprocal(out=gate_sb[:, tt:tt + 1], in_=ssum[:]), deps=(sm,))
        oh = s.op("vector", lambda e: e.tensor_scalar(
            out=oh_sb[:], in0=ltok[:], scalar1=nmax[:, 0:1], scalar2=0.0, op0=ALU.add, op1=ALU.is_ge),
            deps=(mx, prev["st"]))
        pko = tt % 2
        troh = s.op("tensor", lambda e, pko=pko: e.transpose(
            pTx[pko][0:16, 0:128], oh_sb[:], ident_bf[:]), deps=(oh, dep_idb, dep_pT[pko]))
        evoh = s.op("scalar", lambda e, tt=tt, pko=pko: e.copy(
            ohfm_v[tt // 16][:, 128 * (tt % 16):128 * (tt % 16 + 1)], pTx[pko][0:16, 0:128]),
            deps=(troh,))
        dep_pT[pko] = evoh
        prev["st"] = evoh
        prev["oh"] = oh
        xevs = []
        for k in range(KT):
            pk = k % 2
            trx = s.op("tensor", lambda e, tt=tt, k=k, pk=pk, tc0=tcol(tt): e.transpose(
                pTx[pk][:], x_fm[k][:, tc0:tc0 + 128], ident_bf[:]),
                deps=(dep_pT[pk], last_ln, dep_idb))
            evx = s.op("scalar", lambda e, k=k, pk=pk: e.copy(
                xtok_sb[:, 128 * k:128 * (k + 1)], pTx[pk][:]), deps=(trx, prev["xst"]))
            dep_pT[pk] = evx
            xevs.append(evx)
        xst = s.dma("sync", x3tok[128 * tt:128 * (tt + 1), :], xtok_sb[:], deps=tuple(xevs))
        prev["xst"] = xst
    if not fused:
        s.dma("sync", gate_out, gate_sb[:], deps=(gt,))
    st_oh1 = s.dma("sync", oh_out[:, 0:S], ohfm_v[0], deps=(evoh,))
    st_oh2 = s.dma("sync", oh_out[:, S:TOK], ohfm_v[1], deps=(evoh,))
    if not fused:
        for k in range(KT):
            s.dma("sync", xout[k], x_fm[k][:], deps=(last_ln, prev["xst"]))
    return (gt, prev["xst"], st_oh1, st_oh2)


def build_C():
    c = Ctx()
    emit_l1_attn(c)
    return c.finish()


def prep_C_weights(inputs):
    def coltile(w):
        w = f32c(w)
        Dd, O_ = w.shape
        return f32c(w.reshape(Dd, O_ // 128, 128).transpose(1, 0, 2))
    lam = 0.01
    idx = np.arange(128)[:, None] - np.arange(512)[None, :]
    dtab = np.stack([np.exp(-lam * np.abs(128.0 * (m - 12) + idx)) for m in range(28)])
    return {
        "wq_t": coltile(inputs["L1_attn_W_q"]),
        "wk_f": f32c(inputs["L1_attn_W_k"]),
        "wv_f": f32c(inputs["L1_attn_W_v"]),
        "wo_t": coltile(inputs["L1_attn_W_o"]),
        "wdn": f32c(inputs["L1_attn_W_down_kv"]),
        "wuk": f32c(inputs["L1_attn_W_up_k"]),
        "wuv": f32c(inputs["L1_attn_W_up_v"]),
        "dtab": bf(dtab),
        "wg": f32c(inputs["L1_ffn_Wg"]),
        "bg": f32c(np.asarray(inputs["L1_ffn_bg"]).reshape(16, 1)),
    }


def run_C(inputs, x2_list):
    nc = build_C()
    w = prep_C_weights(inputs)
    maps = [dict(w, xin=np.ascontiguousarray(x2_list[cc])) for cc in range(NCORES)]
    res = run_bass_kernel_spmd(nc, maps, core_ids=list(range(NCORES)))
    return res.results



GTOK = NCORES * TOK
CAP = 2048
BIGIDX = 60000.0


def emit_moe(c, x_all_ext=None, oh_all_ext=None, y_out_ext=None, dep_in=None, cb8_ext=None):
    nc, s = c.nc, c.seq
    fused = x_all_ext is not None
    if fused:
        x_all = x_all_ext
        oh_all = oh_all_ext
    else:
        x_all = c.inp("x_all", [GTOK, D], BF16)
        oh_all = c.inp("oh_all", [NCORES * 16, TOK], BF16)
    w1e = c.inp("w1e", [2, 32, D, 128], F32)
    w2e = c.inp("w2e", [2, KT, 4096, 128], F32)
    b1e = c.inp("b1e", [2, 128, 32], F32)
    b2e = c.inp("b2e", [2, 128, KT], F32)
    cb8 = cb8_ext if cb8_ext is not None else c.inp("cb8", [8, 1], F32)

    if fused:
        y_out = y_out_ext
    else:
        y_out = c.outp("y_out", [TOK, D], BF16)
        sidx_out = c.outp("sidx_out", [128, GTOK // 128], I32)

    buf = c.dram("bufD", [TOK, D], BF16)
    sidx_dram = c.dram("sidxD", [NCORES, TOK], U16)
    loc_dram = c.dram("locD", [NCORES, TOK], U16)

    pb = [c.ps(f"db{i}", [128, 512], F32) for i in range(6)]
    pTx = [c.ps("dTa", [128, 128], BF16), c.ps("dTb", [128, 128], BF16)]
    dep_pb = {i: None for i in range(6)}
    dep_pT = {0: None, 1: None}

    ident_bf = c.sb("didb", [128, 128], BF16)
    ident_f = c.sb("didf", [128, 128], F32)
    ii = c.sb("dci", [128, 128], I32)
    jj = c.sb("dcj", [128, 128], I32)
    o1 = s.op("gpsimd", lambda e: e.iota(ii[:], pattern=[[1, 128]], base=0, channel_multiplier=0))
    o2 = s.op("gpsimd", lambda e: e.iota(jj[:], pattern=[[0, 128]], base=0, channel_multiplier=1))
    dep_idb = s.op("vector", lambda e: e.tensor_tensor(out=ident_bf[:], in0=ii[:], in1=jj[:], op=ALU.is_equal),
                   deps=(o1, o2))
    dep_idf = s.op("vector", lambda e: e.tensor_tensor(out=ident_f[:], in0=ii[:], in1=jj[:], op=ALU.is_equal),
                   deps=(o1, o2))
    xrow = [c.sb(f"xrowD{i}", [128, D], BF16) for i in range(4)]
    sidx_cols = c.sb("sxcD", [128, GTOK // 128], I32)
    loc_cols = c.sb("lccD", [128, GTOK // 128], I32)
    zrow = c.sb("zrowD", [128, D], BF16)
    yrow = c.sb("yrowD", [128, D], BF16)
    b1_sb = c.sb("b1D", [128, 32 * 2], F32)
    b2_sb = c.sb("b2D", [128, KT * 2], F32)

    with contextlib.ExitStack() as rst:
        def rsb(name, shape, dtype):
            return rst.enter_context(nc.sbuf_tensor(name, shape, dtype))
        packed_bf = rsb("packed_bf", [128, TOK], BF16)
        packed_f = rsb("packed_f", [128, TOK], F32)
        ones_big = rsb("ones_big", [128, TOK], F32)
        cum = rsb("cumD", [128, TOK], F32)
        gi = rsb("giD", [128, TOK], I32)
        hi_bf = rsb("hiD", [128, TOK], BF16)
        lo_bf = rsb("loD", [128, TOK], BF16)
        rows_hi = rsb("rhiD", [8, TOK], F32)
        rows_e = rsb("reD", [8, TOK], F32)
        keep8 = rsb("keep8D", [8, TOK], F32)
        su16 = rsb("su16D", [8, TOK], U16)
        rowbuf = rsb("rowbufD", [1, 256], F32)
        base_col = rsb("base_colD", [128, 1], F32)
        segsel = rsb("segselD", [128, 8], BF16)
        ecol_i = rsb("ecolD", [128, 1], I32)
        ecol = rsb("ecolfD", [128, 1], F32)
        cb_sb = rsb("cbD", [8, 1], F32)
        sidx_cols_u = rsb("sxcuD", [128, GTOK // 128], U16)
        loc_cols_u = rsb("lccuD", [128, GTOK // 128], U16)

        ld_oh = s.dma("sync", packed_bf[:], oh_all, deps=(dep_in,))
        cp_f = s.op("vector", lambda e: e.tensor_copy(out=packed_f[:], in_=packed_bf[:]), deps=(ld_oh,))
        o_ones = s.op("vector", lambda e: e.memset(ones_big[:], 1.0))
        sc_op = s.op("vector", lambda e: e.tensor_tensor_scan(
            out=cum[:], data0=packed_f[:], data1=ones_big[:], initial=0.0, op0=ALU.add, op1=ALU.mult),
            deps=(cp_f, o_ones))
        trs = s.op("tensor", lambda e: e.transpose(pb[0][0:1, 0:128], cum[:, TOK - 1:TOK], ident_f[:]),
                   deps=(sc_op, dep_idf))
        ev_r = s.op("vector", lambda e: e.tensor_copy(out=rowbuf[0:1, 0:128], in_=pb[0][0:1, 0:128]),
                    deps=(trs,))
        dep_pb[0] = ev_r
        prev_add = s.op("vector", lambda e: e.memset(rowbuf[0:1, 128:144], 0.0), deps=(ev_r,))
        for seg in range(1, NCORES):
            prev_add = s.op("vector", lambda e, seg=seg: e.tensor_add(
                out=rowbuf[0:1, 128 + 16 * seg:128 + 16 * (seg + 1)],
                in0=rowbuf[0:1, 128 + 16 * (seg - 1):128 + 16 * seg],
                in1=rowbuf[0:1, 16 * (seg - 1):16 * seg]), deps=(prev_add,))
        trb = s.op("tensor", lambda e: e.transpose(pb[1][0:128, 0:1], rowbuf[0:1, 128:256],
                                                   ident_f[0:1, 0:1]), deps=(prev_add, dep_pb[1]))
        ev_b = s.op("vector", lambda e: e.tensor_copy(out=base_col[:], in_=pb[1][0:128, 0:1]), deps=(trb,))
        dep_pb[1] = ev_b

        g1 = s.op("vector", lambda e: e.tensor_sub(out=cum[:], in0=cum[:], in1=packed_f[:]),
                  deps=(sc_op, ev_r))
        g2 = s.op("vector", lambda e: e.tensor_scalar(out=cum[:], in0=cum[:], scalar1=base_col[:, 0:1],
                                                      scalar2=None, op0=ALU.add), deps=(g1, ev_b))
        g3 = s.op("vector", lambda e: e.tensor_copy(out=gi[:], in_=cum[:]), deps=(g2,))
        g4 = s.op("vector", lambda e: e.tensor_scalar(out=gi[:], in0=gi[:], scalar1=8, scalar2=None,
                                                      op0=ALU.arith_shift_right), deps=(g3,))
        g5 = s.op("vector", lambda e: e.tensor_copy(out=hi_bf[:], in_=gi[:]), deps=(g4,))
        g6 = s.op("vector", lambda e: e.tensor_copy(out=gi[:], in_=cum[:]), deps=(g5,))
        g7 = s.op("vector", lambda e: e.tensor_scalar(out=gi[:], in0=gi[:], scalar1=255, scalar2=None,
                                                      op0=ALU.bitwise_and), deps=(g6,))
        g8 = s.op("vector", lambda e: e.tensor_copy(out=lo_bf[:], in_=gi[:]), deps=(g7,))
        mh = s.op("vector", lambda e: e.tensor_mul(out=hi_bf[:], in0=hi_bf[:], in1=packed_bf[:]),
                  deps=(g5, g8))
        mlo = s.op("vector", lambda e: e.tensor_mul(out=lo_bf[:], in0=lo_bf[:], in1=packed_bf[:]),
                   deps=(g8, mh))
        e1_ = s.op("gpsimd", lambda e: e.iota(ecol_i[:], pattern=[[0, 1]], base=0, channel_multiplier=1))
        e2_ = s.op("vector", lambda e: e.tensor_scalar(out=ecol_i[:], in0=ecol_i[:], scalar1=15,
                                                       scalar2=None, op0=ALU.bitwise_and), deps=(e1_,))
        e3_ = s.op("vector", lambda e: e.tensor_copy(out=ecol[:], in_=ecol_i[:]), deps=(e2_,))
        me = s.op("vector", lambda e: e.tensor_scalar(out=packed_bf[:], in0=packed_bf[:],
                                                      scalar1=ecol[:, 0:1], scalar2=None, op0=ALU.mult),
                  deps=(e3_, mh, mlo))
        pdiv = ecol_i
        sl3 = s.op("gpsimd", lambda e: e.iota(pdiv[:], pattern=[[0, 1]], base=0, channel_multiplier=1),
                   deps=(me,))
        sl4 = s.op("vector", lambda e: e.tensor_scalar(out=pdiv[:], in0=pdiv[:], scalar1=4, scalar2=None,
                                                       op0=ALU.arith_shift_right), deps=(sl3,))
        pdiv_f = rsb("pdivfD", [128, 1], F32)
        sl4b = s.op("vector", lambda e: e.tensor_copy(out=pdiv_f[:], in_=pdiv[:]), deps=(sl4,))
        sl_i = rsb("slD", [128, 8], I32)
        sl_f = rsb("slfD", [128, 8], F32)
        sl5 = s.op("gpsimd", lambda e: e.iota(sl_i[:], pattern=[[1, 8]], base=0, channel_multiplier=0))
        sl5b = s.op("vector", lambda e: e.tensor_copy(out=sl_f[:], in_=sl_i[:]), deps=(sl5,))
        sl6 = s.op("vector", lambda e: e.tensor_scalar(out=sl_f[:], in0=sl_f[:], scalar1=pdiv_f[:, 0:1],
                                                       scalar2=None, op0=ALU.subtract), deps=(sl5b, sl4b))
        sl7 = s.op("vector", lambda e: e.tensor_scalar(out=segsel[:], in0=sl_f[:], scalar1=0.0, scalar2=None,
                                                       op0=ALU.is_equal), deps=(sl6,))
        for q in range(TOK // 512):
            qp = q % 2
            m1 = s.op("tensor", lambda e, q=q, qp=qp: e.matmul(
                pb[2 + qp][0:8, 0:512], segsel[:], hi_bf[:, 512 * q:512 * (q + 1)], start=True, stop=True),
                deps=(dep_pb[2 + qp], sl7, mh))
            m2 = s.op("tensor", lambda e, q=q, qp=qp: e.matmul(
                pb[4 + qp][0:8, 0:512], segsel[:], lo_bf[:, 512 * q:512 * (q + 1)], start=True, stop=True),
                deps=(dep_pb[4 + qp], sl7, mlo))
            ev1 = s.op("vector", lambda e, q=q, qp=qp: e.tensor_copy(
                out=rows_hi[:, 512 * q:512 * (q + 1)], in_=pb[2 + qp][0:8, 0:512]), deps=(m1,))
            ev2 = s.op("vector", lambda e, q=q, qp=qp: e.scalar_tensor_tensor(
                out=rows_hi[:, 512 * q:512 * (q + 1)], in0=rows_hi[:, 512 * q:512 * (q + 1)], scalar=256.0,
                in1=pb[4 + qp][0:8, 0:512], op0=ALU.mult, op1=ALU.add), deps=(ev1, m2))
            m3 = s.op("tensor", lambda e, q=q, qp=qp: e.matmul(
                pb[2 + qp][0:8, 0:512], segsel[:], packed_bf[:, 512 * q:512 * (q + 1)], start=True,
                stop=True), deps=(ev1, me))
            ev3 = s.op("vector", lambda e, q=q, qp=qp: e.tensor_copy(
                out=rows_e[:, 512 * q:512 * (q + 1)], in_=pb[2 + qp][0:8, 0:512]), deps=(m3,))
            dep_pb[2 + qp] = ev3
            dep_pb[4 + qp] = ev2
        ld_cb = s.dma("sync", cb_sb[:], cb8)
        k1 = s.op("vector", lambda e: e.tensor_scalar(out=keep8[:], in0=rows_hi[:], scalar1=float(CAP),
                                                      scalar2=None, op0=ALU.is_lt), deps=(ev2,))
        sx1 = s.op("vector", lambda e: e.tensor_scalar(out=rows_e[:], in0=rows_e[:], scalar1=float(CAP),
                                                       scalar2=None, op0=ALU.mult), deps=(ev3,))
        sx2 = s.op("vector", lambda e: e.tensor_add(out=rows_e[:], in0=rows_e[:], in1=rows_hi[:]),
                   deps=(sx1, ev2))
        sx3 = s.op("vector", lambda e: e.tensor_mul(out=rows_e[:], in0=rows_e[:], in1=keep8[:]),
                   deps=(sx2, k1))
        sx4 = s.op("vector", lambda e: e.tensor_scalar(out=keep8[:], in0=keep8[:], scalar1=-BIGIDX,
                                                       scalar2=BIGIDX, op0=ALU.mult, op1=ALU.add),
                   deps=(sx3,))
        sx5 = s.op("vector", lambda e: e.tensor_add(out=rows_e[:], in0=rows_e[:], in1=keep8[:]),
                   deps=(sx4,))
        lc1 = s.op("vector", lambda e: e.tensor_scalar(out=keep8[:], in0=rows_e[:], scalar1=cb_sb[0:8, 0:1],
                                                       scalar2=None, op0=ALU.subtract), deps=(sx5, ld_cb))
        lc2 = s.op("vector", lambda e: e.tensor_scalar(out=rows_hi[:], in0=keep8[:], scalar1=0.0,
                                                       scalar2=None, op0=ALU.is_ge), deps=(lc1,))
        lc3 = s.op("vector", lambda e: e.tensor_scalar(out=cum[0:8, 0:TOK], in0=keep8[:],
                                                       scalar1=float(TOK), scalar2=None, op0=ALU.is_lt),
                   deps=(lc2, g6))
        lc4 = s.op("vector", lambda e: e.tensor_mul(out=rows_hi[:], in0=rows_hi[:], in1=cum[0:8, 0:TOK]),
                   deps=(lc3,))
        lc5 = s.op("vector", lambda e: e.tensor_mul(out=keep8[:], in0=keep8[:], in1=rows_hi[:]),
                   deps=(lc4,))
        lc6 = s.op("vector", lambda e: e.tensor_scalar(out=rows_hi[:], in0=rows_hi[:], scalar1=-BIGIDX,
                                                       scalar2=BIGIDX, op0=ALU.mult, op1=ALU.add),
                   deps=(lc5,))
        lc7 = s.op("vector", lambda e: e.tensor_add(out=keep8[:], in0=keep8[:], in1=rows_hi[:]),
                   deps=(lc6,))
        cu1 = s.op("vector", lambda e: e.tensor_copy(out=su16[:], in_=rows_e[:]), deps=(sx5, lc1))
        st1 = s.dma("sync", sidx_dram, su16[:], deps=(cu1,))
        cu2 = s.op("vector", lambda e: e.tensor_copy(out=su16[:], in_=keep8[:]), deps=(lc7, st1))
        st2 = s.dma("sync", loc_dram, su16[:], deps=(cu2,))
        tt1 = s.dma("scalar", sidx_cols_u[:], sidx_dram.rearrange("s (b f) -> (s b) f", f=128),
                    deps=(st1,), transpose=True)
        tt2 = s.dma("scalar", loc_cols_u[:], loc_dram.rearrange("s (b f) -> (s b) f", f=128),
                    deps=(st2,), transpose=True)
        cv1 = s.op("vector", lambda e: e.tensor_copy(out=sidx_cols[:], in_=sidx_cols_u[:]), deps=(tt1,))
        cv2 = s.op("vector", lambda e: e.tensor_copy(out=loc_cols[:], in_=loc_cols_u[:]), deps=(tt2,))
        st_sidx = st1 if fused else s.dma("sync", sidx_out, sidx_cols[:], deps=(cv1,))

        oz = s.op("vector", lambda e: e.memset(zrow[:], 0.0))
        z_sts = []
        for tt in range(TOK // 128):
            z_sts.append(s.dma("sync", buf[128 * tt:128 * (tt + 1), :], zrow[:], deps=(oz,)))
        prev_scat = [None] * 4
        scat_ops = []
        for tt in range(GTOK // 128):
            par = tt % 4
            ld = s.dma("sync", xrow[par][:], x_all[128 * tt:128 * (tt + 1), :],
                       deps=(prev_scat[par], dep_in))
            deps_sc = (ld, cv2) + tuple(z_sts[-NDMASEM:])
            sc2 = s.op("gpsimd", lambda e, tt=tt, par=par: e.indirect_dma_start(
                out=buf, out_offset=IndirectOffsetOnAxis(loc_cols[:, tt:tt + 1], 0),
                in_=xrow[par][:], in_offset=None, bounds_check=bc_reg(e, TOK - 1), oob_is_err=False),
                deps=deps_sc, dma=True)
            prev_scat[par] = sc2
            scat_ops.append(sc2)
        barrier = tuple(scat_ops[-NDMASEM:]) + (st_sidx, cv1, cv2)
    bfm = [c.sb(f"bfmD{k}", [128, CAP], BF16) for k in range(KT)]
    hD = [c.sb(f"hD{o}", [128, CAP], BF16) for o in range(32)]
    wstgD = [c.sb(f"wstgD{i}", [128, 4096], BF16) for i in range(2)]
    lb1 = [s.dma("sync", b1_sb[:, 32 * ex:32 * (ex + 1)], b1e[ex]) for ex in range(2)]
    lb2 = [s.dma("sync", b2_sb[:, KT * ex:KT * (ex + 1)], b2e[ex]) for ex in range(2)]
    prev_yst = None
    prev_bfm_user = None
    for ex in range(2):
        tr_evs = []
        for tt in range(CAP // 128):
            par = tt % 4
            ld = s.dma("sync", xrow[par][:], buf[CAP * ex + 128 * tt:CAP * ex + 128 * (tt + 1), :],
                       deps=barrier + (prev_scat[par], prev_bfm_user))
            ev = None
            tr = None
            for k in range(KT):
                pk = k % 2
                tr = s.op("tensor", lambda e, k=k, par=par, pk=pk: e.transpose(
                    pTx[pk][:], xrow[par][:, 128 * k:128 * (k + 1)], ident_bf[:]),
                    deps=(ld, dep_idb, dep_pT[pk]))
                ev = s.op("scalar", lambda e, k=k, tt=tt, pk=pk: e.copy(
                    bfm[k][:, 128 * tt:128 * (tt + 1)], pTx[pk][:]),
                    deps=(tr, prev_bfm_user) + barrier)
                dep_pT[pk] = ev
            prev_scat[par] = tr
            tr_evs.append(ev)
        dep_bfm = tr_evs[-1]
        wdep = [dep_bfm, dep_bfm]
        last_hA = {}
        for o in range(32):
            wpar = o % 2
            ldw = s.dma("gpsimd", wstgD[wpar][:, 0:1024].rearrange("p (a f) -> p a f", a=8),
                        w1e[ex, o].rearrange("(a p) f -> p a f", p=128), deps=(wdep[wpar],) + barrier)
            last_ev = None
            for q in range(CAP // 512):
                qp = q % 2
                mm = None
                for k in range(KT):
                    mm = s.op("tensor", lambda e, o=o, q=q, k=k, qp=qp, wpar=wpar: e.matmul(
                        pb[qp][:, 0:512], wstgD[wpar][:, 128 * k:128 * (k + 1)],
                        bfm[k][:, 512 * q:512 * (q + 1)], start=(k == 0), stop=(k == KT - 1)),
                        deps=(dep_pb[qp], ldw, dep_bfm))
                ev = s.op("scalar", lambda e, o=o, q=q, qp=qp, ex=ex: e.activation(
                    out=hD[o][:, 512 * q:512 * (q + 1)], in_=pb[qp][:, 0:512], func=AF.Relu,
                    bias=b1_sb[:, 32 * ex + o:32 * ex + o + 1]), deps=(mm, lb1[ex]))
                dep_pb[qp] = ev
                last_ev = ev
            wdep[wpar] = last_ev
            last_hA[o] = last_ev
        wdepB = [last_hA[31], last_hA[31]]
        yfm_evs = {}
        for didx in range(KT):
            wpar = didx % 2
            ldw = s.dma("gpsimd", wstgD[wpar][:].rearrange("p (a f) -> p a f", a=32),
                        w2e[ex, didx].rearrange("(a p) f -> p a f", p=128), deps=(wdepB[wpar],))
            last_ev = None
            for q in range(CAP // 512):
                qp = q % 2
                mm = None
                for k in range(32):
                    mm = s.op("tensor", lambda e, didx=didx, q=q, k=k, qp=qp, wpar=wpar: e.matmul(
                        pb[qp][:, 0:512], wstgD[wpar][:, 128 * k:128 * (k + 1)],
                        hD[k][:, 512 * q:512 * (q + 1)], start=(k == 0), stop=(k == 31)),
                        deps=(dep_pb[qp], ldw, last_hA[k]))
                ev = s.op("vector", lambda e, didx=didx, q=q, qp=qp, ex=ex: e.tensor_scalar(
                    out=bfm[didx][:, 512 * q:512 * (q + 1)], in0=pb[qp][:, 0:512],
                    scalar1=b2_sb[:, KT * ex + didx:KT * ex + didx + 1], scalar2=None, op0=ALU.add),
                    deps=(mm, lb2[ex]))
                dep_pb[qp] = ev
                last_ev = ev
            wdepB[wpar] = last_ev
            yfm_evs[didx] = last_ev
        for tt in range(CAP // 128):
            xevs = []
            for k in range(KT):
                pk = k % 2
                tr = s.op("tensor", lambda e, k=k, tt=tt, pk=pk: e.transpose(
                    pTx[pk][:], bfm[k][:, 128 * tt:128 * (tt + 1)], ident_bf[:]),
                    deps=(yfm_evs[k], dep_pT[pk]))
                ev = s.op("scalar", lambda e, k=k, pk=pk: e.copy(
                    yrow[:, 128 * k:128 * (k + 1)], pTx[pk][:]), deps=(tr, prev_yst))
                dep_pT[pk] = ev
                xevs.append(ev)
            prev_yst = s.dma("sync", y_out[CAP * ex + 128 * tt:CAP * ex + 128 * (tt + 1), :], yrow[:],
                             deps=tuple(xevs))
        prev_bfm_user = prev_yst
    return prev_yst, sidx_dram


def build_D():
    c = Ctx()
    emit_moe(c)
    return c.finish()


def prep_D_weights(inputs, core):
    w1 = f32c(inputs["L1_ffn_W1"][2 * core:2 * core + 2])
    w2 = f32c(inputs["L1_ffn_W2"][2 * core:2 * core + 2])
    w1e = f32c(w1.reshape(2, D, 32, 128).transpose(0, 2, 1, 3))
    w2e = f32c(w2.reshape(2, 4096, KT, 128).transpose(0, 2, 1, 3))
    b1 = f32c(inputs["L1_ffn_b1"][2 * core:2 * core + 2])
    b2 = f32c(inputs["L1_ffn_b2"][2 * core:2 * core + 2])
    b1e = f32c(b1.reshape(2, 32, 128).transpose(0, 2, 1))
    b2e = f32c(b2.reshape(2, KT, 128).transpose(0, 2, 1))
    cb = np.full((8, 1), 4096.0 * core, np.float32)
    return {"w1e": w1e, "w2e": w2e, "b1e": b1e, "b2e": b2e, "cb8": cb}


def run_D(inputs, x3tok_list, oh_list):
    nc = build_D()
    x_all = np.concatenate([np.asarray(x) for x in x3tok_list], axis=0)
    oh_all = np.concatenate([np.asarray(o) for o in oh_list], axis=0)
    maps = []
    for cc in range(NCORES):
        m = prep_D_weights(inputs, cc)
        m["x_all"] = x_all
        m["oh_all"] = oh_all
        maps.append(m)
    res = run_bass_kernel_spmd(nc, maps, core_ids=list(range(NCORES)))
    return res.results



def emit_final(c, y_all_ext=None, x3t_ext=None, sidx_dram=None, gate_sb_ext=None,
               cb8_ap=None, dep_in=None):
    nc, s = c.nc, c.seq
    fused = y_all_ext is not None
    if fused:
        y_all = y_all_ext
        x3t = x3t_ext
    else:
        y_all = c.inp("y_all", [GTOK, D], BF16)
        x3t = c.inp("x3t", [TOK, D], BF16)
        sidx_c = c.inp("sidx_c", [128, TOK // 128], I32)
        gate_c = c.inp("gate_c", [128, TOK // 128], F32)
    out = c.outp("out", [TOK, D], F32)

    si_sb = c.sb("siE", [128, TOK // 128], I32)
    if fused:
        gt_sb = gate_sb_ext
        cidf = c.sb("cidfE", [128, 1], F32)
        cidi = c.sb("cidiE", [128, 1], I32)
        onesr = c.sb("onesrE", [1, 128], F32)
        own_u = c.sb("ownuE", [128, TOK], U16)
        si_u = c.sb("siuE", [128, TOK // 128], U16)
        own_dram = c.dram("ownE", [1, TOK], U16)
        pE = c.ps("pE", [128, 8], F32)
        lc = s.dma("sync", cidf[0:1, 0:1], cb8_ap[0:1, 0:1], deps=(dep_in,))
        c1 = s.op("vector", lambda e: e.tensor_scalar(out=cidf[0:1, 0:1], in0=cidf[0:1, 0:1],
                                                      scalar1=1.0 / TOK, scalar2=None, op0=ALU.mult),
                  deps=(lc,))
        c0 = s.op("vector", lambda e: e.memset(onesr[:], 1.0))
        cbm = s.op("tensor", lambda e: e.matmul(pE[:, 0:1], onesr[0:1, :], cidf[0:1, 0:1],
                                                start=True, stop=True), deps=(c1, c0))
        c2a = s.op("vector", lambda e: e.tensor_copy(out=cidf[:], in_=pE[:, 0:1]), deps=(cbm,))
        c2 = s.op("vector", lambda e: e.tensor_copy(out=cidi[:], in_=cidf[:]), deps=(c2a,))
        gidx = s.op("gpsimd", lambda e: e.indirect_dma_start(
            out=own_u[:], out_offset=None, in_=sidx_dram,
            in_offset=IndirectOffsetOnAxis(cidi[:, 0:1], 0),
            bounds_check=bc_reg(e, NCORES - 1), oob_is_err=False), deps=(c2, dep_in), dma=True)
        stq = s.dma("sync", own_dram, own_u[0:1, :], deps=(gidx,))
        ttq = s.dma("scalar", si_u[:], own_dram.rearrange("o (a f) -> (o a) f", f=128),
                    deps=(stq,), transpose=True)
        l1 = s.op("vector", lambda e: e.tensor_copy(out=si_sb[:], in_=si_u[:]), deps=(ttq,))
        l2 = dep_in
    else:
        gt_sb = c.sb("gtE", [128, TOK // 128], F32)
        l1 = s.dma("sync", si_sb[:], sidx_c)
        l2 = s.dma("sync", gt_sb[:], gate_c)
    eps = c.sb("epsE", [128, 1], F32)
    o_eps = s.op("vector", lambda e: e.memset(eps[:], 1e-5))

    yg = [c.sb(f"ygE{i}", [128, D], BF16) for i in range(2)]
    xr = [c.sb(f"xrE{i}", [128, D], BF16) for i in range(2)]
    sf = c.sb("sfE", [128, D], F32)
    sq = c.sb("sqE", [128, D], F32)
    st_ = c.sb("stE", [128, 8], F32)
    ov = c.sb("ovE", [128, D], F32)
    prev = [None, None]
    prev_ln = None
    for tt in range(TOK // 128):
        par = tt % 2
        mz = s.op("vector", lambda e, par=par: e.memset(yg[par][:], 0.0), deps=(prev[par],))
        g = s.op("gpsimd", lambda e, tt=tt, par=par: e.indirect_dma_start(
            out=yg[par][:], out_offset=None, in_=y_all,
            in_offset=IndirectOffsetOnAxis(si_sb[:, tt:tt + 1], 0),
            bounds_check=bc_reg(e, GTOK - 1), oob_is_err=False), deps=(mz, l1), dma=True)
        lx = s.dma("sync", xr[par][:], x3t[128 * tt:128 * (tt + 1), :], deps=(prev[par],))
        sm = s.op("vector", lambda e, tt=tt, par=par: e.scalar_tensor_tensor(
            out=sf[:], in0=yg[par][:], scalar=gt_sb[:, tt:tt + 1], in1=xr[par][:],
            op0=ALU.mult, op1=ALU.add), deps=(g, lx, l2, prev_ln))
        sqo = s.op("vector", lambda e: e.tensor_mul(out=sq[:], in0=sf[:], in1=sf[:]), deps=(sm,))
        r1 = s.op("vector", lambda e: e.reduce_sum(out=st_[:, 0:1], in_=sf[:], axis=AX.X), deps=(sm,))
        r2 = s.op("vector", lambda e: e.reduce_sum(out=st_[:, 1:2], in_=sq[:], axis=AX.X), deps=(sqo,))
        r3 = s.op("vector", lambda e: e.tensor_scalar(out=st_[:, 0:2], in0=st_[:, 0:2], scalar1=1.0 / D,
                                                      scalar2=None, op0=ALU.mult), deps=(r1, r2))
        r4 = s.op("vector", lambda e: e.tensor_mul(out=st_[:, 2:3], in0=st_[:, 0:1], in1=st_[:, 0:1]),
                  deps=(r3,))
        r5 = s.op("vector", lambda e: e.tensor_sub(out=st_[:, 1:2], in0=st_[:, 1:2], in1=st_[:, 2:3]),
                  deps=(r4,))
        r6 = s.op("scalar", lambda e: e.activation(out=st_[:, 1:2], in_=st_[:, 1:2], func=AF.Ln,
                                                   bias=eps[:, 0:1]), deps=(r5, o_eps))
        r7 = s.op("scalar", lambda e: e.activation(out=st_[:, 1:2], in_=st_[:, 1:2], func=AF.Exp,
                                                   scale=-0.5), deps=(r6,))
        n1 = s.op("vector", lambda e: e.tensor_scalar(out=ov[:], in0=sf[:], scalar1=st_[:, 0:1],
                                                      scalar2=None, op0=ALU.subtract), deps=(r7,))
        n2 = s.op("vector", lambda e: e.tensor_scalar(out=ov[:], in0=ov[:], scalar1=st_[:, 1:2],
                                                      scalar2=None, op0=ALU.mult), deps=(n1,))
        stout = s.dma("sync", out[128 * tt:128 * (tt + 1), :], ov[:], deps=(n2,))
        prev[par] = sm
        prev_ln = stout
    return prev_ln


def build_E():
    c = Ctx()
    emit_final(c)
    return c.finish()


def run_E(y_list, x3tok_list, sidx_list, gate_list):
    nc = build_E()
    y_all = np.concatenate([np.asarray(y) for y in y_list], axis=0)
    maps = []
    for cc in range(NCORES):
        maps.append({
            "y_all": y_all,
            "x3t": np.ascontiguousarray(x3tok_list[cc]),
            "sidx_c": np.ascontiguousarray(np.asarray(sidx_list[cc])[:, 32 * cc:32 * (cc + 1)]),
            "gate_c": np.ascontiguousarray(gate_list[cc]),
        })
    res = run_bass_kernel_spmd(nc, maps, core_ids=list(range(NCORES)))
    return [np.asarray(r["out"]) for r in res.results]



def build_fused():
    c = Ctx()
    s = c.seq
    x0 = c.inp("x0", [TOK, D], F32)
    wwT = c.inp("wwT", [D, D], F32)
    wrT = c.inp("wrT", [D, D], F32)
    dec = c.inp("dec", [1, 1], F32)
    la = c.inp("la", [1, 1], F32)
    cb8 = c.inp("cb8", [8, 1], F32)

    gate_sb = c.sb("gate_g", [128, TOK // 128], F32)
    x3tok_loc = c.dram("x3tok_loc", [TOK, D], BF16)
    ohfm_loc = c.dram("ohfm_loc", [16, TOK], BF16)
    x_all_g = c.dram("x_all_g", [GTOK, D], BF16)
    ohall_g = c.dram("ohall_g", [NCORES * 16, TOK], BF16)
    y_loc = c.dram("y_loc", [TOK, D], BF16)
    y_all_g = c.dram("y_all_g", [GTOK, D], BF16)

    c.push()
    x_fm = [c.sb(f"xg{k}", [128, XCOLS], BF16) for k in range(KT)]
    c.push()
    emit_l0_attn(c, x0, wwT, wrT, dec, la, x_fm)
    c.pop()
    s.fence()
    c.push()
    emit_l0_ffn(c, x_fm=x_fm)
    c.pop()
    s.fence()
    c.push()
    emit_l1_attn(c, x_fm_ext=x_fm, gate_sb=gate_sb, x3tok_dram=x3tok_loc, oh_dram=ohfm_loc)
    c.pop()
    c.pop()
    s.fence()
    rg = [list(range(NCORES))]
    s.op("gpsimd", lambda e: e.collective_compute(
        "AllGather", ALU.bypass, replica_groups=rg, ins=[x3tok_loc.opt()], outs=[x_all_g.opt()]))
    s.op("gpsimd", lambda e: e.collective_compute(
        "AllGather", ALU.bypass, replica_groups=rg, ins=[ohfm_loc.opt()], outs=[ohall_g.opt()]))
    s.fence()
    c.push()
    _, sidx_dram = emit_moe(c, x_all_ext=x_all_g, oh_all_ext=ohall_g, y_out_ext=y_loc, cb8_ext=cb8)
    c.pop()
    s.fence()
    s.op("gpsimd", lambda e: e.collective_compute(
        "AllGather", ALU.bypass, replica_groups=rg, ins=[y_loc.opt()], outs=[y_all_g.opt()]))
    s.fence()
    c.push()
    emit_final(c, y_all_ext=y_all_g, x3t_ext=x3tok_loc, sidx_dram=sidx_dram,
               gate_sb_ext=gate_sb, cb8_ap=cb8)
    c.pop()
    return c.finish()


def kernel_fused(**inputs):
    nc = build_fused()
    wB = prep_B_weights(inputs)
    wC = prep_C_weights(inputs)
    mapsA = prep_A(inputs)
    maps = []
    for cc in range(NCORES):
        m = dict(mapsA[cc])
        m.update(wB)
        m.update(wC)
        m.update(prep_D_weights(inputs, cc))
        maps.append(m)
    res = run_bass_kernel_spmd(nc, maps, core_ids=list(range(NCORES)))
    outs = [np.asarray(r["out"]) for r in res.results]
    return np.concatenate([o.reshape(B_PER_CORE, S, D) for o in outs], axis=0)


# version 06 (final)
# speedup vs baseline: 1.0105x; 1.0105x over previous
import contextlib
import numpy as np
import ml_dtypes
import concourse.bass as bass
import concourse.mybir as mybir
from concourse.bass import IndirectOffsetOnAxis
from concourse.bass_utils import run_bass_kernel_spmd

F32 = mybir.dt.float32
BF16 = mybir.dt.bfloat16
I32 = mybir.dt.int32
U16 = mybir.dt.uint16
AF = mybir.ActivationFunctionType
ALU = mybir.AluOpType
AX = mybir.AxisListType

NCORES = 8
D = 1024
KT = 8
B_PER_CORE = 2
S = 2048
TOK = B_PER_CORE * S
CHUNK = 128
NCHUNK = S // CHUNK

ENGINE_NAMES = ("sync", "scalar", "vector", "tensor", "gpsimd")


class Op:
    __slots__ = ("eng", "fn", "inc", "deps", "cum", "is_dma", "sem_key", "prewait")

    def __init__(self, eng, fn, inc, deps, is_dma):
        self.eng = eng
        self.fn = fn
        self.inc = inc
        self.deps = deps
        self.is_dma = is_dma
        self.cum = 0
        self.sem_key = None
        self.prewait = None


NDMASEM = 6


_BC_REG_CACHE = {}


def bc_reg(e, val):
    key = (id(e), val)
    if key not in _BC_REG_CACHE:
        _BC_REG_CACHE[key] = e.to_reg(val)
    return _BC_REG_CACHE[key]


class Seq:

    def __init__(self, nc):
        self.nc = nc
        self.ops = []
        self.fence_ops = {}

    def op(self, eng, fn, deps=(), dma=False):
        deps = [d for d in deps if d is not None]
        f = self.fence_ops.get(eng)
        if f is not None:
            deps.append(f)
        o = Op(eng, fn, 16 if dma else 1, deps, dma)
        self.ops.append(o)
        return o

    def tails(self):
        seen = set()
        out = []
        cum = {}
        ndma = {}
        for o in reversed(self.ops):
            if o.is_dma:
                k = (o.eng, "d")
                ndma.setdefault(k, 0)
                if ndma[k] < NDMASEM:
                    out.append(o)
                    ndma[k] += 1
            else:
                if o.cum is None and not hasattr(o, "_t"):
                    pass
                if (o.eng, "c") not in seen:
                    seen.add((o.eng, "c"))
                    out.append(o)
        return out

    def fence(self):
        t = self.tails()
        self.fence_ops = {}
        for en in ENGINE_NAMES:
            self.fence_ops[en] = self.op(en, lambda e: e.nop(), deps=tuple(t))

    def dma(self, queue, out, in_, deps=(), **kw):
        return self.op(queue, lambda e, out=out, in_=in_, kw=kw: e.dma_start(out=out, in_=in_, **kw),
                       deps, dma=True)

    def build(self, stack):
        nc = self.nc
        cmp_sems = {en: stack.enter_context(nc.semaphore(f"c_{en}")) for en in ENGINE_NAMES}
        dma_queues = sorted({o.eng for o in self.ops if o.is_dma})
        dma_sems = {}
        for q in dma_queues:
            for r in range(NDMASEM):
                dma_sems[(q, r)] = stack.enter_context(nc.semaphore(f"d_{q}{r}"))
        ALL_INC = True
        needed = set()
        for o in self.ops:
            for d in o.deps:
                needed.add(id(d))
        if ALL_INC:
            needed = {id(o) for o in self.ops}
        cum = {en: 0 for en in ENGINE_NAMES}
        dma_n = {q: 0 for q in dma_queues}
        sem_uses = {k: 0 for k in dma_sems}
        for o in self.ops:
            if o.is_dma:
                k = (o.eng, dma_n[o.eng] % NDMASEM)
                dma_n[o.eng] += 1
                sem_uses[k] += 1
                o.sem_key = k
                o.cum = 16 * sem_uses[k]
                if sem_uses[k] > 1:
                    o.prewait = (k, o.cum - 16)
            elif id(o) in needed:
                cum[o.eng] += 1
                o.cum = cum[o.eng]
                o.sem_key = o.eng
            else:
                o.cum = None
                o.sem_key = o.eng
        ops = self.ops
        block = stack.enter_context(nc.Block())

        def sem_of(key):
            return dma_sems[key] if isinstance(key, tuple) else cmp_sems[key]

        def body_for(name):
            def body(engobj):
                waited = {}
                for o in ops:
                    if o.eng != name:
                        continue
                    for d in o.deps:
                        if d.cum > waited.get(d.sem_key, 0):
                            engobj.wait_ge(sem_of(d.sem_key), d.cum)
                            waited[d.sem_key] = d.cum
                    if o.prewait is not None and o.prewait[1] > waited.get(o.prewait[0], 0):
                        engobj.wait_ge(sem_of(o.prewait[0]), o.prewait[1])
                        waited[o.prewait[0]] = o.prewait[1]
                    ins = o.fn(engobj)
                    if o.cum is not None:
                        ins.then_inc(sem_of(o.sem_key), o.inc)
                if name == "sync":
                    for en in ENGINE_NAMES:
                        if en != "sync" and cum[en] > 0:
                            engobj.wait_ge(cmp_sems[en], cum[en])
                    for k, n in sem_uses.items():
                        if n > 0:
                            engobj.wait_ge(dma_sems[k], 16 * n)
            return body

        block.sync(body_for("sync"))
        block.scalar(body_for("scalar"))
        block.vector(body_for("vector"))
        block.tensor(body_for("tensor"))
        block.gpsimd(body_for("gpsimd"))


def bf(x):
    return np.ascontiguousarray(np.asarray(x, dtype=np.float32)).astype(ml_dtypes.bfloat16)


def f32c(x):
    return np.ascontiguousarray(np.asarray(x, dtype=np.float32))


class Ctx:
    def __init__(self):
        self.nc = bass.Bass("TRN2", target_bir_lowering=False)
        self.seq = Seq(self.nc)
        self.stack = contextlib.ExitStack()
        self.scopes = [self.stack]

    def push(self):
        st = contextlib.ExitStack()
        self.scopes.append(st)
        return st

    def pop(self):
        self.scopes.pop().close()

    def sb(self, name, shape, dtype):
        return self.scopes[-1].enter_context(self.nc.sbuf_tensor(name, shape, dtype))

    def ps(self, name, shape, dtype=F32):
        return self.scopes[-1].enter_context(self.nc.psum_tensor(name, shape, dtype))

    def inp(self, name, shape, dtype=F32):
        return self.nc.dram_tensor(name, shape, dtype, kind="ExternalInput").ap()

    def outp(self, name, shape, dtype=F32):
        return self.nc.dram_tensor(name, shape, dtype, kind="ExternalOutput").ap()

    def dram(self, name, shape, dtype):
        return self.nc.dram_tensor(name, shape, dtype, kind="Internal").ap()

    def finish(self):
        self.seq.build(self.stack)
        self.stack.close()
        return self.nc



XCOLS = 2 * (S + 1)
BATCH_BASE = [1, S + 2]


def emit_l0_attn(c, x0_ap, wwT_ap, wrT_ap, dec_ap, la_ap, x_fm):
    s = c.seq

    ident_bf = c.sb("ident_bf", [128, 128], BF16)
    ii = c.sb("ident_i", [128, 128], I32)
    jj = c.sb("ident_j", [128, 128], I32)
    oi1 = s.op("gpsimd", lambda e: e.iota(ii[:], pattern=[[1, 128]], base=0, channel_multiplier=0))
    oi2 = s.op("gpsimd", lambda e: e.iota(jj[:], pattern=[[0, 128]], base=0, channel_multiplier=1))
    dep_id = s.op("vector", lambda e: e.tensor_tensor(out=ident_bf[:], in0=ii[:], in1=jj[:], op=ALU.is_equal),
                  deps=(oi1, oi2))

    pb = [c.ps(f"pb{i}", [128, 512], F32) for i in range(6)]
    pTx = [c.ps("pTa", [128, 128], BF16), c.ps("pTb", [128, 128], BF16)]
    dep_pb = {i: None for i in range(6)}
    dep_pT = {0: None, 1: None}

    sc = c.sb("sc", [1, 8], F32)
    ones_row = c.sb("ones_row", [1, 128], F32)
    o_scz = s.op("vector", lambda e: e.memset(sc[:], 0.0))
    ld_dec = s.dma("sync", sc[0:1, 0:1], dec_ap, deps=(o_scz,))
    ld_la = s.dma("sync", sc[0:1, 3:4], la_ap, deps=(o_scz,))
    o = s.op("scalar", lambda e: e.activation(out=sc[0:1, 1:2], in_=sc[0:1, 0:1], func=AF.Sigmoid), deps=(ld_dec,))
    o = s.op("scalar", lambda e: e.activation(out=sc[0:1, 1:2], in_=sc[0:1, 1:2], func=AF.Ln), deps=(o,))
    o_gC = s.op("scalar", lambda e: e.activation(out=sc[0:1, 2:3], in_=sc[0:1, 1:2], func=AF.Exp,
                                                 scale=float(CHUNK)), deps=(o,))
    o_al = s.op("scalar", lambda e: e.activation(out=sc[0:1, 3:4], in_=sc[0:1, 3:4], func=AF.Exp), deps=(ld_la,))
    o_ones = s.op("vector", lambda e: e.memset(ones_row[:], 1.0))

    cols = c.sb("cols", [128, 8], F32)
    mm = s.op("tensor", lambda e: e.matmul(pb[0][:, 0:8], ones_row[0:1, :], sc[0:1, 0:8], start=True, stop=True),
              deps=(o_gC, o_al, o_ones))
    o_cols = s.op("vector", lambda e: e.tensor_copy(out=cols[:], in_=pb[0][:, 0:8]), deps=(mm,))
    dep_pb[0] = o_cols
    lg_col = cols[:, 1:2]
    gC_col = cols[:, 2:3]
    al_col = cols[:, 3:4]

    iot_row = c.sb("iot_row", [1, 128], I32)
    iot_col = c.sb("iot_col", [128, 1], I32)
    frow = c.sb("frow", [1, 128], F32)
    fcol = c.sb("fcol", [128, 1], F32)
    gw_col = c.sb("gw_col", [128, 1], F32)
    gp_row = c.sb("gp_row", [1, 128], F32)
    gp_bcast = c.sb("gp_bcast", [128, 128], BF16)
    o1 = s.op("gpsimd", lambda e: e.iota(iot_row[:], pattern=[[1, 128]], base=0, channel_multiplier=0))
    o2 = s.op("gpsimd", lambda e: e.iota(iot_col[:], pattern=[[0, 1]], base=0, channel_multiplier=1))
    o3 = s.op("vector", lambda e: e.tensor_copy(out=frow[:], in_=iot_row[:]), deps=(o1,))
    o4a = s.op("vector", lambda e: e.tensor_copy(out=fcol[:], in_=iot_col[:]), deps=(o2,))
    o4 = s.op("vector", lambda e: e.tensor_scalar(out=fcol[:], in0=fcol[:], scalar1=-1.0, scalar2=127.0,
                                                  op0=ALU.mult, op1=ALU.add), deps=(o4a,))
    o5 = s.op("scalar", lambda e: e.activation(out=gp_row[:], in_=frow[:], func=AF.Exp, scale=sc[0:1, 1:2]),
              deps=(o3, o_gC))
    o_gw = s.op("scalar", lambda e: e.activation(out=gw_col[:], in_=fcol[:], func=AF.Exp, scale=lg_col),
                deps=(o4, o_cols))
    mm = s.op("tensor", lambda e: e.matmul(pb[1][:, 0:128], ones_row[0:1, :], gp_row[0:1, :], start=True, stop=True),
              deps=(o5, o_ones))
    o_gp = s.op("vector", lambda e: e.tensor_copy(out=gp_bcast[:], in_=pb[1][:, 0:128]), deps=(mm,))
    dep_pb[1] = o_gp

    mT = c.sb("mT", [128, 128], F32)
    dif = c.sb("dif", [128, 128], I32)
    mtmp = c.sb("mtmp", [128, 128], F32)
    o1 = s.op("gpsimd", lambda e: e.iota(dif[:], pattern=[[1, 128]], base=-1, channel_multiplier=-1))
    o2 = s.op("vector", lambda e: e.tensor_copy(out=mtmp[:], in_=dif[:]), deps=(o1,))
    o3 = s.op("vector", lambda e: e.tensor_scalar(out=mT[:], in0=mtmp[:], scalar1=0.0, scalar2=None,
                                                  op0=ALU.is_ge), deps=(o2,))
    o4 = s.op("vector", lambda e: e.tensor_scalar(out=mtmp[:], in0=mtmp[:], scalar1=0.0, scalar2=None,
                                                  op0=ALU.max), deps=(o3,))
    o5 = s.op("scalar", lambda e: e.activation(out=mtmp[:], in_=mtmp[:], func=AF.Exp, scale=lg_col),
              deps=(o4, o_cols))
    o_mT = s.op("vector", lambda e: e.tensor_mul(out=mT[:], in0=mT[:], in1=mtmp[:]), deps=(o5,))

    wwT_bf = [c.sb(f"wwT{k}", [128, 1024], BF16) for k in range(KT)]
    wrT_bf = [c.sb(f"wrT{k}", [128, 1024], BF16) for k in range(KT)]
    wstage = [c.sb(f"wstage{i}", [128, 1024], F32) for i in range(2)]
    dep_w = None
    prev_cast = [None, None]
    for idx, (ap, tiles) in enumerate(((wwT_ap, wwT_bf), (wrT_ap, wrT_bf))):
        for k in range(KT):
            par = (idx * KT + k) % 2
            ld = s.dma("sync", wstage[par][:], ap[128 * k:128 * (k + 1), :], deps=(prev_cast[par],))
            cp = s.op("vector", lambda e, t=tiles[k], par=par: e.tensor_copy(out=t[:], in_=wstage[par][:]),
                      deps=(ld,))
            prev_cast[par] = cp
            dep_w = cp

    zops = [s.op("vector", lambda e, k=k: e.memset(x_fm[k][:], 0.0)) for k in range(KT)]
    xs_f = [c.sb(f"xsf{i}", [128, 1024], F32) for i in range(2)]
    xs_b = [c.sb(f"xsb{i}", [128, 1024], BF16) for i in range(2)]
    prev_cast = [None, None]
    last_tr = [None, None]
    last_evac = None
    for t in range(TOK // 128):
        par = t % 2
        b = t // 16
        col0 = BATCH_BASE[b] + (t % 16) * 128
        ld = s.dma("sync", xs_f[par][:], x0_ap[128 * t:128 * (t + 1), :], deps=(prev_cast[par],))
        cast = s.op("vector", lambda e, par=par: e.tensor_copy(out=xs_b[par][:], in_=xs_f[par][:]),
                    deps=(ld, last_tr[par]))
        prev_cast[par] = cast
        for k in range(KT):
            pk = k % 2
            tr = s.op("tensor", lambda e, par=par, k=k, pk=pk: e.transpose(
                pTx[pk][:], xs_b[par][:, 128 * k:128 * (k + 1)], ident_bf[:]),
                deps=(cast, dep_id, dep_pT[pk]))
            ev = s.op("scalar", lambda e, k=k, col0=col0, pk=pk: e.copy(
                x_fm[k][:, col0:col0 + 128], pTx[pk][:]),
                deps=(tr, zops[k]))
            dep_pT[pk] = ev
            last_evac = ev
            last_tr[par] = tr
    load_done = last_evac

    stateT = [c.sb(f"st{j}", [128, 1024], BF16) for j in range(KT)]
    vT = c.sb("vT", [128, 1024], BF16)
    vgw = c.sb("vgw", [128, 1024], BF16)
    reads_bf = c.sb("reads_bf", [128, 1024], BF16)
    wkT = [c.sb("wkT0", [128, 1024], BF16), c.sb("wkT1", [128, 1024], BF16)]
    smT = [c.sb("smT0", [128, 128], BF16), c.sb("smT1", [128, 128], BF16)]
    rkgp = c.sb("rkgp", [128, 1024], BF16)
    r2tmp = c.sb("r2tmp", [128, 128], BF16)
    prev_r2 = [None]

    eps_col = c.sb("eps_col", [128, 1], F32)
    o_eps = s.op("vector", lambda e: e.memset(eps_col[:], 1e-5))
    ones_bf = c.sb("ones_bf", [128, 1], BF16)
    o_onesb = s.op("vector", lambda e: e.memset(ones_bf[:], 1.0))
    sq_bf = c.sb("sq_bf", [128, 512], BF16)
    rows = c.sb("rows", [1, 1024], F32)
    row2 = c.sb("row2", [1, 512], F32)
    tmp_ln = c.sb("tmp_ln", [128, 512], BF16)

    last_stt = [None] * KT
    pe_tail = load_done
    last_resid = None
    last_ln = None

    def emit_early(par_dst, t0, extra_dep):
        evs = []
        for k in range(KT):
            pk = k % 2
            tr = s.op("tensor", lambda e, k=k, t0=t0, pk=pk: e.transpose(
                pTx[pk][:], x_fm[k][:, t0 - 1:t0 + 127], ident_bf[:]),
                deps=(dep_pT[pk], extra_dep, load_done))
            ev = s.op("scalar", lambda e, k=k, pk=pk: e.copy(
                wkT[par_dst][:, 128 * k:128 * (k + 1)], pTx[pk][:]), deps=(tr,))
            dep_pT[pk] = ev
            evs.append(ev)
        mmS = None
        for k in range(KT):
            mmS = s.op("tensor", lambda e, k=k, t0=t0: e.matmul(
                pb[2][:, 0:128], x_fm[k][:, t0 - 1:t0 + 127], x_fm[k][:, t0:t0 + CHUNK],
                start=(k == 0), stop=(k == KT - 1)),
                deps=(dep_pb[2], extra_dep, load_done))
        ev_smT = s.op("vector", lambda e: e.tensor_tensor(
            out=smT[par_dst][:], in0=pb[2][:, 0:128], in1=mT[:], op=ALU.mult), deps=(mmS, o_mT, extra_dep))
        dep_pb[2] = ev_smT
        return evs, ev_smT, mmS

    for b in range(B_PER_CORE):
        base = BATCH_BASE[b]
        zst = [s.op("vector", lambda e, j=j: e.memset(stateT[j][:], 0.0), deps=(last_stt[j],))
               for j in range(KT)]
        last_stt = zst

        wk_evs0, smT_ev0, smS_mm0 = emit_early(0, base, pe_tail)
        wk_ready = [wk_evs0, [None] * KT]
        smT_ready = [smT_ev0, None]
        sm_mm = [smS_mm0, None]

        for ci in range(NCHUNK):
            par = ci % 2
            t0 = base + CHUNK * ci

            next_evs, next_sm, next_sm_mm = (None, None, None)
            if ci < NCHUNK - 1:
                next_evs, next_sm, next_sm_mm = emit_early(1 - par, t0 + CHUNK, pe_tail)
                wk_ready[1 - par] = next_evs
                smT_ready[1 - par] = next_sm
                sm_mm[1 - par] = next_sm_mm

            vstop = [None, None]
            for half in range(2):
                for k in range(KT):
                    vstop[half] = s.op("tensor", lambda e, half=half, k=k, t0=t0: e.matmul(
                        pb[half][:, 0:512], x_fm[k][:, t0:t0 + CHUNK],
                        wwT_bf[k][:, 512 * half:512 * (half + 1)],
                        start=(k == 0), stop=(k == KT - 1)),
                        deps=(dep_pb[half], dep_w, load_done))
            ev_vT = []
            for half in range(2):
                e1 = s.op("vector", lambda e, half=half: e.tensor_copy(
                    out=vT[:, 512 * half:512 * (half + 1)], in_=pb[half][:, 0:512]),
                    deps=(vstop[half], pe_tail))
                ev_vT.append(e1)
                dep_pb[half] = e1
            ev_vgw = s.op("vector", lambda e: e.tensor_scalar(
                out=vgw[:], in0=vT[:], scalar1=gw_col[:, 0:1], scalar2=None, op0=ALU.mult),
                deps=(ev_vT[0], ev_vT[1], o_gw, pe_tail))

            ev_rkgp = []
            for k in range(KT):
                ev_rkgp.append(s.op("vector", lambda e, k=k, t0=t0: e.tensor_tensor(
                    out=rkgp[:, 128 * k:128 * (k + 1)], in0=x_fm[k][:, t0:t0 + CHUNK], in1=gp_bcast[:],
                    op=ALU.mult), deps=(o_gp, pe_tail)))

            ev_reads = []
            for i in range(KT):
                bank = 3 if i % 2 == 0 else 2
                lo = 0 if i % 2 == 0 else 128
                for j in range(KT):
                    s.op("tensor", lambda e, i=i, j=j, bank=bank, lo=lo: e.matmul(
                        pb[bank][:, lo:lo + 128], stateT[j][:, 128 * i:128 * (i + 1)],
                        rkgp[:, 128 * j:128 * (j + 1)], start=(j == 0), stop=False),
                        deps=(dep_pb[bank], ev_rkgp[j], last_stt[j]))
                mmop = s.op("tensor", lambda e, i=i, bank=bank, par=par, lo=lo: e.matmul(
                    pb[bank][:, lo:lo + 128], vT[:, 128 * i:128 * (i + 1)], smT[par][:],
                    start=False, stop=True),
                    deps=(ev_vT[i // 4], smT_ready[par]))
                ev = s.op("vector", lambda e, i=i, bank=bank, lo=lo: e.tensor_copy(
                    out=reads_bf[:, 128 * i:128 * (i + 1)], in_=pb[bank][:, lo:lo + 128]), deps=(mmop,))
                dep_pb[bank] = ev
                ev_reads.append(ev)

            resid_ops = []
            for eidx in range(KT):
                bank = eidx % 2
                mmop = None
                for i in range(KT):
                    mmop = s.op("tensor", lambda e, eidx=eidx, i=i, bank=bank: e.matmul(
                        pb[bank][:, 0:128], wrT_bf[i][:, 128 * eidx:128 * (eidx + 1)],
                        reads_bf[:, 128 * i:128 * (i + 1)],
                        start=(i == 0), stop=(i == KT - 1)),
                        deps=(dep_pb[bank], ev_reads[i], dep_w))
                ev0 = s.op("vector", lambda e, eidx=eidx, t0=t0, bank=bank: e.scalar_tensor_tensor(
                    out=r2tmp[:], in0=pb[bank][:, 0:128], scalar=al_col[:, 0:1],
                    in1=x_fm[eidx][:, t0:t0 + CHUNK], op0=ALU.mult, op1=ALU.add),
                    deps=(mmop, o_cols, prev_r2[0]))
                ev = s.op("vector", lambda e, eidx=eidx, t0=t0: e.tensor_add(
                    out=x_fm[eidx][:, t0:t0 + CHUNK], in0=r2tmp[:],
                    in1=x_fm[eidx][:, t0:t0 + CHUNK]),
                    deps=(ev0,
                          next_evs[eidx] if next_evs else None,
                          next_sm_mm if (next_sm_mm and eidx == KT - 1) else None))
                prev_r2[0] = ev
                dep_pb[bank] = ev
                resid_ops.append(ev)
            last_resid = resid_ops[-1]

            new_stt = []
            pe_last = None
            for j in range(KT):
                stts = None
                for half in range(2):
                    bank = 4 + half
                    mmop = s.op("tensor", lambda e, j=j, half=half, bank=bank, par=par: e.matmul(
                        pb[bank][:, 0:512], wkT[par][:, 128 * j:128 * (j + 1)],
                        vgw[:, 512 * half:512 * (half + 1)], start=True, stop=True),
                        deps=(dep_pb[bank], ev_vgw, wk_ready[par][j], last_stt[j]))
                    pe_last = mmop
                    stts = s.op("vector", lambda e, j=j, half=half, bank=bank: e.scalar_tensor_tensor(
                        out=stateT[j][:, 512 * half:512 * (half + 1)],
                        in0=stateT[j][:, 512 * half:512 * (half + 1)],
                        scalar=gC_col[:, 0:1], in1=pb[bank][:, 0:512], op0=ALU.mult, op1=ALU.add),
                        deps=(mmop, o_cols))
                    dep_pb[bank] = stts
                new_stt.append(stts)
            last_stt = new_stt
            pe_tail = pe_last

        prev_mm2 = None
        for q in range(4):
            cols0 = base + 512 * q
            mm1 = None
            mm2 = None
            sq_ops = []
            for k in range(KT):
                sq = s.op("scalar", lambda e, k=k, cols0=cols0: e.activation(
                    out=sq_bf[:], in_=x_fm[k][:, cols0:cols0 + 512], func=AF.Square),
                    deps=(last_resid, prev_mm2))
                mm1 = s.op("tensor", lambda e, k=k, cols0=cols0: e.matmul(
                    pb[2][0:1, 0:512], ones_bf[:, 0:1], x_fm[k][:, cols0:cols0 + 512],
                    start=(k == 0), stop=(k == KT - 1)),
                    deps=(o_onesb, last_resid, dep_pb[2]))
                mm2 = s.op("tensor", lambda e, k=k: e.matmul(
                    pb[3][0:1, 0:512], ones_bf[:, 0:1], sq_bf[:],
                    start=(k == 0), stop=(k == KT - 1)),
                    deps=(sq, dep_pb[3]))
                prev_mm2 = mm2
                sq_ops.append(sq)
            o1 = s.op("vector", lambda e: e.tensor_scalar(out=rows[0:1, 0:512], in0=pb[2][0:1, 0:512],
                                                          scalar1=1.0 / D, scalar2=None, op0=ALU.mult),
                      deps=(mm1,))
            o2 = s.op("vector", lambda e: e.tensor_scalar(out=rows[0:1, 512:1024], in0=pb[3][0:1, 0:512],
                                                          scalar1=1.0 / D, scalar2=None, op0=ALU.mult),
                      deps=(mm2,))
            dep_pb[2] = o1
            dep_pb[3] = o2
            msq = s.op("vector", lambda e: e.tensor_mul(out=row2[:], in0=rows[0:1, 0:512],
                                                        in1=rows[0:1, 0:512]), deps=(o1,))
            var = s.op("vector", lambda e: e.tensor_sub(out=rows[0:1, 512:1024], in0=rows[0:1, 512:1024],
                                                        in1=row2[:]), deps=(msq, o2))
            ln1 = s.op("scalar", lambda e: e.activation(out=rows[0:1, 512:1024], in_=rows[0:1, 512:1024],
                                                        func=AF.Ln, bias=eps_col[0:1, 0:1]), deps=(var, o_eps))
            rstd = s.op("scalar", lambda e: e.activation(out=rows[0:1, 512:1024], in_=rows[0:1, 512:1024],
                                                         func=AF.Exp, scale=-0.5), deps=(ln1,))
            mrs = s.op("vector", lambda e: e.tensor_mul(out=rows[0:1, 0:512], in0=rows[0:1, 0:512],
                                                        in1=rows[0:1, 512:1024]), deps=(rstd,))
            bc1 = s.op("tensor", lambda e: e.matmul(pb[0][:, 0:512], ones_row[0:1, :], rows[0:1, 512:1024],
                                                    start=True, stop=True), deps=(mrs, o_ones, dep_pb[0]))
            bc2 = s.op("tensor", lambda e: e.matmul(pb[1][:, 0:512], ones_row[0:1, :], rows[0:1, 0:512],
                                                    start=True, stop=True), deps=(mrs, o_ones, dep_pb[1]))
            norm_last = None
            for k in range(KT):
                t1 = s.op("vector", lambda e, k=k, cols0=cols0: e.tensor_tensor(
                    out=tmp_ln[:], in0=x_fm[k][:, cols0:cols0 + 512], in1=pb[0][:, 0:512], op=ALU.mult),
                    deps=(bc1, sq_ops[k], norm_last))
                t2 = s.op("vector", lambda e, k=k, cols0=cols0: e.tensor_tensor(
                    out=x_fm[k][:, cols0:cols0 + 512], in0=tmp_ln[:], in1=pb[1][:, 0:512], op=ALU.subtract),
                    deps=(t1, bc2))
                norm_last = t2
            dep_pb[0] = norm_last
            dep_pb[1] = norm_last
            last_ln = norm_last
        pe_tail = last_ln

    return last_ln


def build_A():
    c = Ctx()
    x0 = c.inp("x0", [TOK, D], F32)
    wwT = c.inp("wwT", [D, D], F32)
    wrT = c.inp("wrT", [D, D], F32)
    dec = c.inp("dec", [1, 1], F32)
    la = c.inp("la", [1, 1], F32)
    x1 = c.outp("x1", [KT, 128, TOK], BF16)

    x_fm = [c.sb(f"xfm{k}", [128, XCOLS], BF16) for k in range(KT)]
    done = emit_l0_attn(c, x0, wwT, wrT, dec, la, x_fm)
    s = c.seq
    for k in range(KT):
        for b in range(B_PER_CORE):
            s.dma("sync", x1[k, :, b * S:(b + 1) * S],
                  x_fm[k][:, BATCH_BASE[b]:BATCH_BASE[b] + S], deps=(done,))
    return c.finish()


def prep_A(inputs):
    x0 = f32c(np.asarray(inputs["L0_attn_out"]).reshape(NCORES, TOK, D))
    wwT = f32c(np.asarray(inputs["L0_attn_W_write"]).T)
    wrT = f32c(np.asarray(inputs["L0_attn_W_read"]).T)
    dec = f32c(np.asarray(inputs["L0_attn_decay"]).reshape(1, 1))
    la = f32c(np.asarray(inputs["L0_attn_log_alpha"]).reshape(1, 1))
    return [{"x0": x0[cc], "wwT": wwT, "wrT": wrT, "dec": dec, "la": la}
            for cc in range(NCORES)]


def run_A(inputs):
    nc = build_A()
    res = run_bass_kernel_spmd(nc, prep_A(inputs), core_ids=list(range(NCORES)))
    return [np.asarray(r["x1"]) for r in res.results]


def kernel(**inputs):
    return kernel_fused(**inputs)


def kernel_unfused(**inputs):
    x1 = run_A(inputs)
    x2 = run_B(inputs, x1)
    rc = run_C(inputs, x2)
    x3tok = [np.asarray(r["x3tok"]) for r in rc]
    oh = [np.asarray(r["oh_out"]) for r in rc]
    gate = [np.asarray(r["gate_out"]) for r in rc]
    rd = run_D(inputs, x3tok, oh)
    y = [np.asarray(r["y_out"]) for r in rd]
    sidx = [np.asarray(r["sidx_out"]) for r in rd]
    outs = run_E(y, x3tok, sidx, gate)
    return np.concatenate([o.reshape(B_PER_CORE, S, D) for o in outs], axis=0)



def emit_ln_block(c, s, pb, x_tiles, cols0, width, dep_in, ln, dep_pb):
    prev_mm2 = dep_pb.get("ln_mm2")
    mm1 = None
    mm2 = None
    sq_ops = []
    for k in range(KT):
        sq = s.op("scalar", lambda e, k=k: e.activation(
            out=ln["sq_bf"][:, 0:width], in_=x_tiles[k][:, cols0:cols0 + width], func=AF.Square),
            deps=(dep_in, prev_mm2))
        mm1 = s.op("tensor", lambda e, k=k: e.matmul(
            pb[2][0:1, 0:width], ln["ones_bf"][:, 0:1], x_tiles[k][:, cols0:cols0 + width],
            start=(k == 0), stop=(k == KT - 1)),
            deps=(ln["o_onesb"], dep_in, dep_pb[2]))
        mm2 = s.op("tensor", lambda e, k=k: e.matmul(
            pb[3][0:1, 0:width], ln["ones_bf"][:, 0:1], ln["sq_bf"][:, 0:width],
            start=(k == 0), stop=(k == KT - 1)),
            deps=(sq, dep_pb[3]))
        prev_mm2 = mm2
        sq_ops.append(sq)
    dep_pb["ln_mm2"] = mm2
    rows = ln["rows"]
    row2 = ln["row2"]
    o1 = s.op("vector", lambda e: e.tensor_scalar(out=rows[0:1, 0:width], in0=pb[2][0:1, 0:width],
                                                  scalar1=1.0 / D, scalar2=None, op0=ALU.mult), deps=(mm1,))
    o2 = s.op("vector", lambda e: e.tensor_scalar(out=rows[0:1, 512:512 + width], in0=pb[3][0:1, 0:width],
                                                  scalar1=1.0 / D, scalar2=None, op0=ALU.mult), deps=(mm2,))
    dep_pb[2] = o1
    dep_pb[3] = o2
    msq = s.op("vector", lambda e: e.tensor_mul(out=row2[0:1, 0:width], in0=rows[0:1, 0:width],
                                                in1=rows[0:1, 0:width]), deps=(o1,))
    var = s.op("vector", lambda e: e.tensor_sub(out=rows[0:1, 512:512 + width], in0=rows[0:1, 512:512 + width],
                                                in1=row2[0:1, 0:width]), deps=(msq, o2))
    ln1 = s.op("scalar", lambda e: e.activation(out=rows[0:1, 512:512 + width], in_=rows[0:1, 512:512 + width],
                                                func=AF.Ln, bias=ln["eps"][0:1, 0:1]), deps=(var, ln["o_eps"]))
    rstd = s.op("scalar", lambda e: e.activation(out=rows[0:1, 512:512 + width], in_=rows[0:1, 512:512 + width],
                                                 func=AF.Exp, scale=-0.5), deps=(ln1,))
    mrs = s.op("vector", lambda e: e.tensor_mul(out=rows[0:1, 0:width], in0=rows[0:1, 0:width],
                                                in1=rows[0:1, 512:512 + width]), deps=(rstd,))
    bc1 = s.op("tensor", lambda e: e.matmul(pb[0][:, 0:width], ln["ones_row"][0:1, :], rows[0:1, 512:512 + width],
                                            start=True, stop=True), deps=(mrs, ln["o_ones"], dep_pb[0]))
    bc2 = s.op("tensor", lambda e: e.matmul(pb[1][:, 0:width], ln["ones_row"][0:1, :], rows[0:1, 0:width],
                                            start=True, stop=True), deps=(mrs, ln["o_ones"], dep_pb[1]))
    norm_last = None
    for k in range(KT):
        t1 = s.op("vector", lambda e, k=k: e.tensor_tensor(
            out=ln["tmp_ln"][:, 0:width], in0=x_tiles[k][:, cols0:cols0 + width], in1=pb[0][:, 0:width],
            op=ALU.mult), deps=(bc1, sq_ops[k], norm_last))
        t2 = s.op("vector", lambda e, k=k: e.tensor_tensor(
            out=x_tiles[k][:, cols0:cols0 + width], in0=ln["tmp_ln"][:, 0:width], in1=pb[1][:, 0:width],
            op=ALU.subtract), deps=(t1, bc2))
        norm_last = t2
    dep_pb[0] = norm_last
    dep_pb[1] = norm_last
    return norm_last


_LN_N = [0]


def make_ln_kit(c, s, sq=None, tmp=None, rows=None, row2=None):
    _LN_N[0] += 1
    p = f"ln{_LN_N[0]}_"
    ln = {}
    ln["ones_bf"] = c.sb(p + "ones_bf", [128, 1], BF16)
    ln["o_onesb"] = s.op("vector", lambda e: e.memset(ln["ones_bf"][:], 1.0))
    ln["ones_row"] = c.sb(p + "ones_row", [1, 128], F32)
    ln["o_ones"] = s.op("vector", lambda e: e.memset(ln["ones_row"][:], 1.0))
    ln["sq_bf"] = sq if sq is not None else c.sb(p + "sq", [128, 512], BF16)
    ln["rows"] = rows if rows is not None else c.sb(p + "rows", [1, 1024], F32)
    ln["row2"] = row2 if row2 is not None else c.sb(p + "row2", [1, 512], F32)
    ln["tmp_ln"] = tmp if tmp is not None else c.sb(p + "tmp", [128, 512], BF16)
    ln["eps"] = c.sb(p + "eps", [1, 1], F32)
    ln["o_eps"] = s.op("vector", lambda e: e.memset(ln["eps"][:], 1e-5))
    return ln


def emit_l0_ffn(c, xin_ap=None, xout_ap=None, x_fm=None, dep_in=None):
    nc, s = c.nc, c.seq
    w1t = c.inp("w1t", [32, D, 128], F32)
    w2t = c.inp("w2t", [32, D, 128], F32)
    h1t = c.inp("h1t", [32, 4096, 128], F32)
    h2t = c.inp("h2t", [32, 4096, 128], F32)
    w3t = c.inp("w3t", [KT, 4096, 128], F32)
    s1v = c.inp("s1v", [128, 32], F32)
    s2v = c.inp("s2v", [128, 32], F32)
    sh1v = c.inp("sh1v", [128, 32], F32)
    sh2v = c.inp("sh2v", [128, 32], F32)
    s3v = c.inp("s3v", [128, KT], F32)

    g1d = c.dram("g1d", [4096, 1024], BF16)
    g2d = c.dram("g2d", [4096, 1024], BF16)

    pb = [c.ps(f"fb{i}", [128, 512], F32) for i in range(6)]
    dep_pb = {i: None for i in range(6)}

    if x_fm is None:
        x_half = [c.sb(f"xh{k}", [128, 1024], BF16) for k in range(KT)]
    hbuf = [c.sb(f"hb{o}", [128, 1024], BF16) for o in range(32)]
    wbg = [c.sb(f"wbg{i}", [128, 4096], BF16) for i in range(2)]
    wsm = [wbg[0][:, 0:1024], wbg[0][:, 1024:2048], wbg[1][:, 0:1024], wbg[1][:, 1024:2048]]
    scl = c.sb("scl", [128, 32 * 4 + KT], F32)
    g1r = c.sb("g1r", [128, 2048], BF16)
    ab = [g1r[:, 0:512], g1r[:, 512:1024]]
    ln = make_ln_kit(c, s, sq=g1r[:, 1024:1536], tmp=g1r[:, 1536:2048])

    ld_s = []
    for i, ap in enumerate((s1v, s2v, sh1v, sh2v)):
        ld_s.append(s.dma("sync", scl[:, 32 * i:32 * (i + 1)], ap))
    ld_s.append(s.dma("sync", scl[:, 128:128 + KT], s3v))
    dep_scl = ld_s[-1]
    S1, S2, SH1, SH2, S3 = (lambda o: scl[:, o:o + 1],
                            lambda o: scl[:, 32 + o:33 + o],
                            lambda o: scl[:, 64 + o:65 + o],
                            lambda o: scl[:, 96 + o:97 + o],
                            lambda o: scl[:, 128 + o:129 + o])

    final_op = dep_in
    for half in range(4):
        hc0 = 1024 * half
        if x_fm is not None:
            cb0 = BATCH_BASE[half // 2] + 1024 * (half % 2)
            x_half = [x_fm[k][:, cb0:cb0 + 1024] for k in range(KT)]
            x_lds = [final_op] * KT
            dep_x = final_op
        else:
            x_lds = []
            for k in range(KT):
                x_lds.append(s.dma("sync", x_half[k][:], xin_ap[k, :, hc0:hc0 + 1024],
                                   deps=(final_op,)))
            dep_x = x_lds[-1]

        wdep = [dep_x, dep_x]
        last_h = {}
        prev_evh = [None]
        for o in range(32):
            par = o % 2
            ldw = s.dma("gpsimd", wsm[2 * par][:].rearrange("p (a f) -> p a f", a=8),
                        w1t[o].rearrange("(a p) f -> p a f", p=128), deps=(wdep[par],))
            ldw2 = s.dma("gpsimd", wsm[2 * par + 1][:].rearrange("p (a f) -> p a f", a=8),
                         w2t[o].rearrange("(a p) f -> p a f", p=128), deps=(wdep[par], ldw))
            mm_last = None
            for q in range(2):
                qp = (o + q) % 2
                ma = None
                mb = None
                for k in range(KT):
                    ma = s.op("tensor", lambda e, o=o, q=q, k=k, par=par, qp=qp: e.matmul(
                        pb[qp][:, 0:512], wsm[2 * par][:, 128 * k:128 * (k + 1)],
                        x_half[k][:, 512 * q:512 * (q + 1)], start=(k == 0), stop=(k == KT - 1)),
                        deps=(dep_pb[qp], ldw, x_lds[k]))
                    mb = s.op("tensor", lambda e, o=o, q=q, k=k, par=par, qp=qp: e.matmul(
                        pb[2 + qp][:, 0:512], wsm[2 * par + 1][:, 128 * k:128 * (k + 1)],
                        x_half[k][:, 512 * q:512 * (q + 1)], start=(k == 0), stop=(k == KT - 1)),
                        deps=(dep_pb[2 + qp], ldw2, x_lds[k]))
                eva = s.op("scalar", lambda e, o=o, qp=qp: e.activation(
                    out=ab[0][:], in_=pb[qp][:, 0:512], func=AF.Silu, scale=S1(o)),
                    deps=(ma, dep_scl, prev_evh[0]))
                evb = s.op("vector", lambda e, o=o, qp=qp: e.tensor_scalar(
                    out=ab[1][:], in0=pb[2 + qp][:, 0:512], scalar1=S2(o), scalar2=None, op0=ALU.mult),
                    deps=(mb, dep_scl, prev_evh[0]))
                evh = s.op("vector", lambda e, o=o, q=q: e.tensor_mul(
                    out=hbuf[o][:, 512 * q:512 * (q + 1)], in0=ab[0][:], in1=ab[1][:]),
                    deps=(eva, evb))
                prev_evh[0] = evh
                dep_pb[qp] = eva
                dep_pb[2 + qp] = evh
                mm_last = evh
            wdep[par] = mm_last
            last_h[o] = mm_last

        def hid_stage(win_ap, rhs_tiles, rhs_dep, scale_fn, out_dram, func, init_dep):
            wdep2 = [init_dep, init_dep]
            st_ops = []
            for o in range(32):
                par = o % 2
                ldw = s.dma("gpsimd", wbg[par][:].rearrange("p (a f) -> p a f", a=32),
                            win_ap[o].rearrange("(a p) f -> p a f", p=128), deps=(wdep2[par],))
                prev_st = st_ops[-1] if st_ops else None
                row_evs = []
                for q in range(2):
                    qp = (o * 2 + q) % 4
                    mm = None
                    for k in range(32):
                        mm = s.op("tensor", lambda e, o=o, q=q, k=k, par=par, qp=qp: e.matmul(
                            pb[qp][:, 0:512], wbg[par][:, 128 * k:128 * (k + 1)],
                            rhs_tiles[k][:, 512 * q:512 * (q + 1)], start=(k == 0), stop=(k == 31)),
                            deps=(dep_pb[qp], ldw, rhs_dep if isinstance(rhs_dep, Op) else rhs_dep[k]))
                    ev = s.op("scalar", lambda e, o=o, q=q, qp=qp: e.activation(
                        out=g1r[:, 512 * q:512 * (q + 1)], in_=pb[qp][:, 0:512], func=func,
                        scale=scale_fn(o)), deps=(mm, dep_scl, prev_st))
                    dep_pb[qp] = ev
                    row_evs.append(ev)
                st = s.dma("sync", out_dram[128 * o:128 * (o + 1), :], g1r[:, 0:1024],
                           deps=tuple(row_evs))
                wdep2[par] = st
                st_ops.append(st)
            return st_ops

        st2 = hid_stage(h1t, hbuf, last_h, SH1, g1d, AF.Gelu, last_h[31])

        g1_lds = []
        for o in range(32):
            g1_lds.append(s.dma("sync", hbuf[o][:], g1d[128 * o:128 * (o + 1), :],
                                deps=(st2[o], st2[-1])))
        st3 = hid_stage(h2t, hbuf, g1_lds, SH2, g2d, AF.Gelu, st2[-1])

        g2_lds = []
        for o in range(32):
            g2_lds.append(s.dma("sync", hbuf[o][:], g2d[128 * o:128 * (o + 1), :],
                                deps=(st3[o], st3[-1])))

        wdep4 = [st3[-1], st3[-1]]
        resid_ops = []
        for eidx in range(KT):
            par = eidx % 2
            ldw = s.dma("gpsimd", wbg[par][:].rearrange("p (a f) -> p a f", a=32),
                        w3t[eidx].rearrange("(a p) f -> p a f", p=128), deps=(wdep4[par],))
            last_ev = None
            for q in range(2):
                qp = q % 2
                mm = None
                for k in range(32):
                    mm = s.op("tensor", lambda e, eidx=eidx, q=q, k=k, par=par, qp=qp: e.matmul(
                        pb[qp][:, 0:512], wbg[par][:, 128 * k:128 * (k + 1)],
                        hbuf[k][:, 512 * q:512 * (q + 1)], start=(k == 0), stop=(k == 31)),
                        deps=(dep_pb[qp], ldw, g2_lds[k]))
                ev = s.op("vector", lambda e, eidx=eidx, q=q, qp=qp: e.scalar_tensor_tensor(
                    out=x_half[eidx][:, 512 * q:512 * (q + 1)], in0=pb[qp][:, 0:512],
                    scalar=S3(eidx), in1=x_half[eidx][:, 512 * q:512 * (q + 1)],
                    op0=ALU.mult, op1=ALU.add), deps=(mm, dep_scl))
                dep_pb[qp] = ev
                last_ev = ev
            wdep4[par] = last_ev
            resid_ops.append(last_ev)
        dep_resid = resid_ops[-1]

        st_final = []
        for q in range(2):
            lnop = emit_ln_block(c, s, pb, x_half, 512 * q, 512, dep_resid, ln, dep_pb)
            if xout_ap is not None:
                for k in range(KT):
                    st_final.append(s.dma("sync", xout_ap[k, :, hc0 + 512 * q:hc0 + 512 * (q + 1)],
                                          x_half[k][:, 512 * q:512 * (q + 1)], deps=(lnop,)))
            final_op = lnop
        if st_final:
            final_op = st_final[-1]
    return final_op, st_final


def build_B():
    c = Ctx()
    xin = c.inp("xin", [KT, 128, TOK], BF16)
    xout = c.outp("xout", [KT, 128, TOK], BF16)
    emit_l0_ffn(c, xin, xout)
    return c.finish()


def prep_B_weights(inputs):
    def tileT(w):
        wT = f32c(np.asarray(w).T)
        I_, O_ = wT.shape
        return f32c(wT.reshape(I_, O_ // 128, 128).transpose(1, 0, 2))
    def colv(v, ntiles):
        return f32c(np.asarray(v).reshape(ntiles, 128).T)
    return {
        "w1t": tileT(inputs["L0_ffn_w1_w"]),
        "w2t": tileT(inputs["L0_ffn_w2_w"]),
        "h1t": tileT(inputs["L0_ffn_h1_w"]),
        "h2t": tileT(inputs["L0_ffn_h2_w"]),
        "w3t": tileT(inputs["L0_ffn_w3_w"]),
        "s1v": colv(inputs["L0_ffn_w1_s"], 32),
        "s2v": colv(inputs["L0_ffn_w2_s"], 32),
        "sh1v": colv(inputs["L0_ffn_h1_s"], 32),
        "sh2v": colv(inputs["L0_ffn_h2_s"], 32),
        "s3v": colv(inputs["L0_ffn_w3_s"], KT),
    }


def run_B(inputs, x1_list):
    nc = build_B()
    w = prep_B_weights(inputs)
    maps = [dict(w, xin=np.ascontiguousarray(x1_list[cc])) for cc in range(NCORES)]
    res = run_bass_kernel_spmd(nc, maps, core_ids=list(range(NCORES)))
    return [np.asarray(r["xout"]) for r in res.results]



def emit_l1_attn(c, x_fm_ext=None, dep_in=None, gate_sb=None,
                 x3tok_dram=None, oh_dram=None):
    nc, s = c.nc, c.seq
    fused = x_fm_ext is not None
    if not fused:
        xin = c.inp("xin", [KT, 128, TOK], BF16)
    wq_t = c.inp("wq_t", [KT, D, 128], F32)
    wk_f = c.inp("wk_f", [D, D], F32)
    wv_f = c.inp("wv_f", [D, D], F32)
    wo_t = c.inp("wo_t", [KT, D, 128], F32)
    wdn = c.inp("wdn", [D, 256], F32)
    wuk = c.inp("wuk", [256, D], F32)
    wuv = c.inp("wuv", [256, D], F32)
    dtab = c.inp("dtab", [28, 128, 512], BF16)
    wg = c.inp("wg", [D, 16], F32)
    bg = c.inp("bg", [16, 1], F32)

    if not fused:
        xout = c.outp("xout", [KT, 128, TOK], BF16)
        x3tok = c.outp("x3tok", [TOK, D], BF16)
        oh_out = c.outp("oh_out", [16, TOK], BF16)
        gate_out = c.outp("gate_out", [128, TOK // 128], F32)
    else:
        x3tok = x3tok_dram
        oh_out = oh_dram

    pb = [c.ps(f"cb{i}", [128, 512], F32) for i in range(6)]
    pTx = [c.ps("cTa", [128, 128], BF16), c.ps("cTb", [128, 128], BF16)]
    dep_pb = {i: None for i in range(6)}
    dep_pT = {0: None, 1: None}
    ln = make_ln_kit(c, s)

    ident_bf = c.sb("idb", [128, 128], BF16)
    ident_f = c.sb("idf", [128, 128], F32)
    ii = c.sb("ci", [128, 128], I32)
    jj = c.sb("cj", [128, 128], I32)
    o1 = s.op("gpsimd", lambda e: e.iota(ii[:], pattern=[[1, 128]], base=0, channel_multiplier=0))
    o2 = s.op("gpsimd", lambda e: e.iota(jj[:], pattern=[[0, 128]], base=0, channel_multiplier=1))
    dep_idb = s.op("vector", lambda e: e.tensor_tensor(out=ident_bf[:], in0=ii[:], in1=jj[:], op=ALU.is_equal),
                   deps=(o1, o2))
    dep_idf = s.op("vector", lambda e: e.tensor_tensor(out=ident_f[:], in0=ii[:], in1=jj[:], op=ALU.is_equal),
                   deps=(o1, o2))

    if fused:
        x_fm = x_fm_ext
        x_lds = [dep_in] * KT
        CB = BATCH_BASE
        def qcol(q):
            return BATCH_BASE[q // 4] + 512 * (q % 4)
        def tcol(tt):
            return BATCH_BASE[tt // 16] + 128 * (tt % 16)
    else:
        x_fm = [c.sb(f"cx{k}", [128, TOK], BF16) for k in range(KT)]
        x_lds = [s.dma("sync", x_fm[k][:], xin[k]) for k in range(KT)]
        CB = [0, S]
        def qcol(q):
            return 512 * q
        def tcol(tt):
            return 128 * tt

    wsum_bf = [c.sb(f"ws{k}", [128, D], BF16) for k in range(KT)]
    wdn_bf = [c.sb(f"wd{k}", [128, 256], BF16) for k in range(KT)]
    wuk_bf = [c.sb(f"wukb{l}", [128, D], BF16) for l in range(2)]
    wuv_bf = [c.sb(f"wuvb{l}", [128, D], BF16) for l in range(2)]
    wg_bf = [c.sb(f"wgb{k}", [128, 16], BF16) for k in range(KT)]
    stg = [c.sb(f"cstg{i}", [128, 1024], F32) for i in range(2)]
    prev_use = [None, None]
    dep_w = []
    for k in range(KT):
        l1 = s.dma("sync", stg[0][:], wk_f[128 * k:128 * (k + 1), :], deps=(prev_use[0],))
        l2 = s.dma("sync", stg[1][:], wv_f[128 * k:128 * (k + 1), :], deps=(prev_use[1],))
        ad = s.op("vector", lambda e: e.tensor_add(out=stg[0][:], in0=stg[0][:], in1=stg[1][:]), deps=(l1, l2))
        cp = s.op("vector", lambda e, k=k: e.tensor_copy(out=wsum_bf[k][:], in_=stg[0][:]), deps=(ad,))
        prev_use = [cp, cp]
        dep_w.append(cp)
    l_dn = [s.dma("gpsimd", wdn_bf[k][:], wdn[128 * k:128 * (k + 1), :]) for k in range(KT)]
    l_uk = [s.dma("gpsimd", wuk_bf[l][:], wuk[128 * l:128 * (l + 1), :]) for l in range(2)]
    l_uv = [s.dma("gpsimd", wuv_bf[l][:], wuv[128 * l:128 * (l + 1), :]) for l in range(2)]
    l_wg = [s.dma("gpsimd", wg_bf[k][:], wg[128 * k:128 * (k + 1), :]) for k in range(KT)]
    dtab_sb = c.sb("dtab_sb", [128, 28 * 512], BF16)
    l_dt = s.dma("sync", dtab_sb[:].rearrange("p (a f) -> p a f", a=28),
                 dtab.rearrange("a p f -> p a f"), deps=(prev_use[0],))

    kv_att = [c.sb(f"kva{j}", [128, S], BF16) for j in range(KT)]
    latent = [c.sb(f"lat{l}", [128, S], BF16) for l in range(2)]
    q_pair = c.sb("q_pair", [128, S], BF16)
    krec = c.sb("krec", [128, S], BF16)
    vA = c.sb("vA", [128, 16 * 65], BF16)
    vB = c.sb("vB", [128, 16 * 65], BF16)
    o_vA1 = s.op("vector", lambda e: e.memset(vA[:], 1.0))
    o_vB1 = s.op("vector", lambda e: e.memset(vB[:], 1.0))
    ptA = c.sb("ptA", [128, 8 * 512], BF16)
    ptB = c.sb("ptB", [128, 8 * 512], BF16)
    _stgb = stg[0][:].bitcast(BF16)
    wqp = [_stgb[:, 0:D], _stgb[:, D:2 * D]]
    rrow = stg[1][0:1, 512:1024]
    rb_sb = stg[1][0:64, 0:512]

    last_ln = None
    for b in range(B_PER_CORE):
        bc = CB[b]
        ev_kv = []
        for o in range(KT):
            qp = o % 2
            mm = None
            for q in range(4):
                for k in range(KT):
                    mm = s.op("tensor", lambda e, o=o, q=q, k=k, qp=qp, bc=bc: e.matmul(
                        pb[qp][:, 0:512], wsum_bf[k][:, 128 * o:128 * (o + 1)],
                        x_fm[k][:, bc + 512 * q:bc + 512 * (q + 1)], start=(k == 0), stop=(k == KT - 1)),
                        deps=(dep_pb[qp], dep_w[k], x_lds[k], last_ln))
                ev = s.op("vector", lambda e, o=o, q=q, qp=qp: e.tensor_copy(
                    out=kv_att[o][:, 512 * q:512 * (q + 1)], in_=pb[qp][:, 0:512]), deps=(mm,))
                dep_pb[qp] = ev
            ev_kv.append(ev)
        ev_lat = []
        for l in range(2):
            qp = l % 2
            for q in range(4):
                mm = None
                for k in range(KT):
                    mm = s.op("tensor", lambda e, l=l, q=q, k=k, qp=qp: e.matmul(
                        pb[qp][:, 0:512], wdn_bf[k][:, 128 * l:128 * (l + 1)],
                        kv_att[k][:, 512 * q:512 * (q + 1)], start=(k == 0), stop=(k == KT - 1)),
                        deps=(dep_pb[qp], l_dn[k], ev_kv[k]))
                ev = s.op("vector", lambda e, l=l, q=q, qp=qp: e.tensor_copy(
                    out=latent[l][:, 512 * q:512 * (q + 1)], in_=pb[qp][:, 0:512]), deps=(mm,))
                dep_pb[qp] = ev
            ev_lat.append(ev)
        dep_lat = ev_lat[-1]

        wq_dep = [dep_lat, dep_lat]
        av_hist = [None, None]
        att_evs = []
        for hp in range(KT):
            wpar = hp % 2
            ldq = s.dma("gpsimd", wqp[wpar][:].rearrange("p (a f) -> p a f", a=KT),
                        wq_t[hp].rearrange("(a p) f -> p a f", p=128), deps=(wq_dep[wpar],))
            ev_q = None
            for q in range(4):
                qp = q % 2
                mm = None
                for k in range(KT):
                    mm = s.op("tensor", lambda e, q=q, k=k, qp=qp, wpar=wpar, bc=bc: e.matmul(
                        pb[qp][:, 0:512], wqp[wpar][:, 128 * k:128 * (k + 1)],
                        x_fm[k][:, bc + 512 * q:bc + 512 * (q + 1)], start=(k == 0), stop=(k == KT - 1)),
                        deps=(dep_pb[qp], ldq, att_evs[-1] if att_evs else None))
                ev_q = s.op("vector", lambda e, q=q, qp=qp: e.tensor_scalar(
                    out=q_pair[:, 512 * q:512 * (q + 1)], in0=pb[qp][:, 0:512],
                    scalar1=1.0 / 32.0, scalar2=None, op0=ALU.mult),
                    deps=(mm, att_evs[-1] if att_evs else None))
                dep_pb[qp] = ev_q
            ev_kr = None
            for q in range(4):
                qp = q % 2
                mm = None
                for l in range(2):
                    mm = s.op("tensor", lambda e, hp=hp, q=q, l=l, qp=qp: e.matmul(
                        pb[qp][:, 0:512], wuk_bf[l][:, 128 * hp:128 * (hp + 1)],
                        latent[l][:, 512 * q:512 * (q + 1)], start=(l == 0), stop=(l == 1)),
                        deps=(dep_pb[qp], l_uk[l], dep_lat, att_evs[-1] if att_evs else None))
                ev_kr = s.op("vector", lambda e, q=q, qp=qp: e.tensor_copy(
                    out=krec[:, 512 * q:512 * (q + 1)], in_=pb[qp][:, 0:512]),
                    deps=(mm, att_evs[-1] if att_evs else None))
                dep_pb[qp] = ev_kr
            ev_v = None
            for tt in range(16):
                bank = 2 + (tt % 2)
                mm = None
                for l in range(2):
                    mm = s.op("tensor", lambda e, hp=hp, tt=tt, l=l, bank=bank: e.matmul(
                        pb[bank][:, 0:128], latent[l][:, 128 * tt:128 * (tt + 1)],
                        wuv_bf[l][:, 128 * hp:128 * (hp + 1)], start=(l == 0), stop=(l == 1)),
                        deps=(dep_pb[bank], l_uv[l], dep_lat, att_evs[-1] if att_evs else None))
                e1 = s.op("vector", lambda e, tt=tt, bank=bank: e.tensor_copy(
                    out=vA[:, 65 * tt:65 * tt + 64], in_=pb[bank][:, 0:64]),
                    deps=(mm, o_vA1, att_evs[-1] if att_evs else None))
                ev_v = s.op("vector", lambda e, tt=tt, bank=bank: e.tensor_copy(
                    out=vB[:, 65 * tt:65 * tt + 64], in_=pb[bank][:, 64:128]),
                    deps=(mm, o_vB1, att_evs[-1] if att_evs else None))
                dep_pb[bank] = ev_v
            for qc in range(4):
                avA = av_hist[0]
                avB = av_hist[1]
                for kh in range(2):
                    ev_ptA = []
                    ev_ptB = []
                    for ki in range(8):
                        kt = 8 * kh + ki
                        m = kt - 4 * qc + 12
                        bA = 0 if ki % 2 == 0 else 2
                        bB = 1 if ki % 2 == 0 else 3
                        mmA = s.op("tensor", lambda e, kt=kt, qc=qc, bA=bA: e.matmul(
                            pb[bA][:, 0:512], krec[0:64, 128 * kt:128 * (kt + 1)],
                            q_pair[0:64, 512 * qc:512 * (qc + 1)], start=True, stop=True),
                            deps=(dep_pb[bA], ev_kr, ev_q))
                        mmB = s.op("tensor", lambda e, kt=kt, qc=qc, bB=bB: e.matmul(
                            pb[bB][:, 0:512], krec[64:128, 128 * kt:128 * (kt + 1)],
                            q_pair[64:128, 512 * qc:512 * (qc + 1)], start=True, stop=True),
                            deps=(dep_pb[bB], ev_kr, ev_q))
                        tA = s.op("vector", lambda e, ki=ki, m=m, bA=bA: e.tensor_tensor(
                            out=ptA[:, 512 * ki:512 * (ki + 1)], in0=pb[bA][:, 0:512],
                            in1=dtab_sb[:, 512 * m:512 * (m + 1)], op=ALU.mult),
                            deps=(mmA, l_dt, avA))
                        tB = s.op("vector", lambda e, ki=ki, m=m, bB=bB: e.tensor_tensor(
                            out=ptB[:, 512 * ki:512 * (ki + 1)], in0=pb[bB][:, 0:512],
                            in1=dtab_sb[:, 512 * m:512 * (m + 1)], op=ALU.mult),
                            deps=(mmB, l_dt, avB))
                        dep_pb[bA] = tA
                        dep_pb[bB] = tB
                        eA = s.op("scalar", lambda e, ki=ki: e.activation(
                            out=ptA[:, 512 * ki:512 * (ki + 1)], in_=ptA[:, 512 * ki:512 * (ki + 1)],
                            func=AF.Exp), deps=(tA,))
                        eB = s.op("scalar", lambda e, ki=ki: e.activation(
                            out=ptB[:, 512 * ki:512 * (ki + 1)], in_=ptB[:, 512 * ki:512 * (ki + 1)],
                            func=AF.Exp), deps=(tB,))
                        ev_ptA.append(eA)
                        ev_ptB.append(eB)
                    for ki in range(8):
                        kt = 8 * kh + ki
                        avA = s.op("tensor", lambda e, tt=kt, ki=ki, kt=kt: e.matmul(
                            pb[4][0:65, 0:512], vA[:, 65 * tt:65 * tt + 65], ptA[:, 512 * ki:512 * (ki + 1)],
                            start=(kt == 0), stop=(kt == 15)),
                            deps=(dep_pb[4] if kt == 0 else None, ev_ptA[ki], ev_v))
                        avB = s.op("tensor", lambda e, tt=kt, ki=ki, kt=kt: e.matmul(
                            pb[5][0:65, 0:512], vB[:, 65 * tt:65 * tt + 65], ptB[:, 512 * ki:512 * (ki + 1)],
                            start=(kt == 0), stop=(kt == 15)),
                            deps=(dep_pb[5] if kt == 0 else None, ev_ptB[ki], ev_v))
                for head, av, p65, pt_dep in ((0, avA, 4, None), (1, avB, 5, None)):
                    bco = 0 if head == 0 else 1
                    rc = s.op("vector", lambda e, p65=p65: e.reciprocal(out=rrow[:], in_=pb[p65][64:65, 0:512]),
                              deps=(av if head == 0 else avB,))
                    bcm = s.op("tensor", lambda e, bco=bco: e.matmul(
                        pb[bco][0:64, 0:512], ln["ones_row"][0:1, 0:64], rrow[0:1, :],
                        start=True, stop=True), deps=(rc, ln["o_ones"], dep_pb[bco]))
                    cpb = s.op("vector", lambda e, bco=bco: e.tensor_copy(out=rb_sb[:], in_=pb[bco][0:64, 0:512]),
                               deps=(bcm,))
                    dep_pb[bco] = cpb
                    evo = s.op("vector", lambda e, hp=hp, qc=qc, head=head, p65=p65: e.tensor_tensor(
                        out=kv_att[hp][64 * head:64 * (head + 1), 512 * qc:512 * (qc + 1)],
                        in0=pb[p65][0:64, 0:512], in1=rb_sb[:], op=ALU.mult),
                        deps=(cpb, dep_lat))
                    dep_pb[p65] = evo
                    att_evs.append(evo)
                av_hist[0] = avA
                av_hist[1] = avB
            wq_dep[wpar] = att_evs[-1]

        wo_dep = [att_evs[-1], att_evs[-1]]
        resid = []
        for eidx in range(KT):
            wpar = eidx % 2
            ldo = s.dma("gpsimd", wqp[wpar][:].rearrange("p (a f) -> p a f", a=KT),
                        wo_t[eidx].rearrange("(a p) f -> p a f", p=128), deps=(wo_dep[wpar],))
            last_ev = None
            for q in range(4):
                qp = q % 2
                mm = None
                for j in range(KT):
                    mm = s.op("tensor", lambda e, eidx=eidx, q=q, j=j, qp=qp, wpar=wpar: e.matmul(
                        pb[qp][:, 0:512], wqp[wpar][:, 128 * j:128 * (j + 1)],
                        kv_att[j][:, 512 * q:512 * (q + 1)], start=(j == 0), stop=(j == KT - 1)),
                        deps=(dep_pb[qp], ldo, att_evs[-1]))
                ev = s.op("vector", lambda e, eidx=eidx, q=q, qp=qp, bc=bc: e.tensor_tensor(
                    out=x_fm[eidx][:, bc + 512 * q:bc + 512 * (q + 1)], in0=pb[qp][:, 0:512],
                    in1=x_fm[eidx][:, bc + 512 * q:bc + 512 * (q + 1)], op=ALU.add),
                    deps=(mm,))
                dep_pb[qp] = ev
                last_ev = ev
            wo_dep[wpar] = last_ev
            resid.append(last_ev)
        for q in range(4):
            last_ln = emit_ln_block(c, s, pb, x_fm, bc + 512 * q, 512, resid[-1], ln, dep_pb)

    _lg1 = ptA[:].bitcast(F32)
    _lg2 = ptB[:].bitcast(F32)
    def lgv(col0, w):
        if col0 < 2048:
            return _lg1[0:16, col0:col0 + w]
        return _lg2[0:16, col0 - 2048:col0 - 2048 + w]
    bg_sb = c.sb("bg_sb", [16, 1], F32)
    l_bg = s.dma("sync", bg_sb[:], bg)
    ev_lg = []
    for q in range(TOK // 512):
        qp = q % 2
        mm = None
        for k in range(KT):
            mm = s.op("tensor", lambda e, q=q, k=k, qp=qp, qc0=qcol(q): e.matmul(
                pb[qp][0:16, 0:512], wg_bf[k][:, 0:16], x_fm[k][:, qc0:qc0 + 512],
                start=(k == 0), stop=(k == KT - 1)),
                deps=(dep_pb[qp], l_wg[k], last_ln))
        ev = s.op("vector", lambda e, q=q, qp=qp: e.tensor_scalar(
            out=lgv(512 * q, 512), in0=pb[qp][0:16, 0:512],
            scalar1=bg_sb[:, 0:1], scalar2=None, op0=ALU.add), deps=(mm, l_bg, av_hist[0], av_hist[1]))
        dep_pb[qp] = ev
        ev_lg.append(ev)

    ltok = c.sb("ltok", [128, 16], F32)
    ptok = c.sb("ptok", [128, 16], F32)
    nmax = c.sb("nmax", [128, 1], F32)
    ssum = c.sb("ssum", [128, 1], F32)
    oh_sb = c.sb("oh_sb", [128, 16], BF16)
    ohfm_v = [kv_att[1][0:16, :], kv_att[2][0:16, :]]
    if gate_sb is None:
        gate_sb = c.sb("gate_sb", [128, TOK // 128], F32)
    xtok_sb = kv_att[0][:, 0:D]
    prev = {"tr": None, "ev": None, "st": None, "oh": None, "xst": None}
    for tt in range(TOK // 128):
        bank = 2 + (tt % 2)
        q = (128 * tt) // 512
        tr = s.op("tensor", lambda e, tt=tt, bank=bank: e.transpose(
            pb[bank][:, 0:16], lgv(128 * tt, 128), ident_f[0:16, 0:16]),
            deps=(ev_lg[q], dep_idf, dep_pb[bank]))
        evl = s.op("vector", lambda e, bank=bank: e.tensor_copy(out=ltok[:], in_=pb[bank][:, 0:16]),
                   deps=(tr, prev["oh"]))
        dep_pb[bank] = evl
        mx = s.op("vector", lambda e: e.reduce_max(out=nmax[:], in_=ltok[:], axis=AX.X, negate=True),
                  deps=(evl,))
        ex = s.op("scalar", lambda e: e.activation(out=ptok[:], in_=ltok[:], func=AF.Exp, bias=nmax[:, 0:1]),
                  deps=(mx, prev["st"]))
        sm = s.op("vector", lambda e: e.reduce_sum(out=ssum[:], in_=ptok[:], axis=AX.X), deps=(ex,))
        gt = s.op("vector", lambda e, tt=tt: e.reciprocal(out=gate_sb[:, tt:tt + 1], in_=ssum[:]), deps=(sm,))
        oh = s.op("vector", lambda e: e.tensor_scalar(
            out=oh_sb[:], in0=ltok[:], scalar1=nmax[:, 0:1], scalar2=0.0, op0=ALU.add, op1=ALU.is_ge),
            deps=(mx, prev["st"]))
        pko = tt % 2
        troh = s.op("tensor", lambda e, pko=pko: e.transpose(
            pTx[pko][0:16, 0:128], oh_sb[:], ident_bf[:]), deps=(oh, dep_idb, dep_pT[pko]))
        evoh = s.op("scalar", lambda e, tt=tt, pko=pko: e.copy(
            ohfm_v[tt // 16][:, 128 * (tt % 16):128 * (tt % 16 + 1)], pTx[pko][0:16, 0:128]),
            deps=(troh,))
        dep_pT[pko] = evoh
        prev["st"] = evoh
        prev["oh"] = oh
        xevs = []
        for k in range(KT):
            pk = k % 2
            trx = s.op("tensor", lambda e, tt=tt, k=k, pk=pk, tc0=tcol(tt): e.transpose(
                pTx[pk][:], x_fm[k][:, tc0:tc0 + 128], ident_bf[:]),
                deps=(dep_pT[pk], last_ln, dep_idb))
            evx = s.op("scalar", lambda e, k=k, pk=pk: e.copy(
                xtok_sb[:, 128 * k:128 * (k + 1)], pTx[pk][:]), deps=(trx, prev["xst"]))
            dep_pT[pk] = evx
            xevs.append(evx)
        xst = s.dma("sync", x3tok[128 * tt:128 * (tt + 1), :], xtok_sb[:], deps=tuple(xevs))
        prev["xst"] = xst
    if not fused:
        s.dma("sync", gate_out, gate_sb[:], deps=(gt,))
    st_oh1 = s.dma("sync", oh_out[:, 0:S], ohfm_v[0], deps=(evoh,))
    st_oh2 = s.dma("sync", oh_out[:, S:TOK], ohfm_v[1], deps=(evoh,))
    if not fused:
        for k in range(KT):
            s.dma("sync", xout[k], x_fm[k][:], deps=(last_ln, prev["xst"]))
    return (gt, prev["xst"], st_oh1, st_oh2)


def build_C():
    c = Ctx()
    emit_l1_attn(c)
    return c.finish()


def prep_C_weights(inputs):
    def coltile(w):
        w = f32c(w)
        Dd, O_ = w.shape
        return f32c(w.reshape(Dd, O_ // 128, 128).transpose(1, 0, 2))
    lam = 0.01
    idx = np.arange(128)[:, None] - np.arange(512)[None, :]
    dtab = np.stack([np.exp(-lam * np.abs(128.0 * (m - 12) + idx)) for m in range(28)])
    return {
        "wq_t": coltile(inputs["L1_attn_W_q"]),
        "wk_f": f32c(inputs["L1_attn_W_k"]),
        "wv_f": f32c(inputs["L1_attn_W_v"]),
        "wo_t": coltile(inputs["L1_attn_W_o"]),
        "wdn": f32c(inputs["L1_attn_W_down_kv"]),
        "wuk": f32c(inputs["L1_attn_W_up_k"]),
        "wuv": f32c(inputs["L1_attn_W_up_v"]),
        "dtab": bf(dtab),
        "wg": f32c(inputs["L1_ffn_Wg"]),
        "bg": f32c(np.asarray(inputs["L1_ffn_bg"]).reshape(16, 1)),
    }


def run_C(inputs, x2_list):
    nc = build_C()
    w = prep_C_weights(inputs)
    maps = [dict(w, xin=np.ascontiguousarray(x2_list[cc])) for cc in range(NCORES)]
    res = run_bass_kernel_spmd(nc, maps, core_ids=list(range(NCORES)))
    return res.results



GTOK = NCORES * TOK
CAP = 2048
BIGIDX = 60000.0


def emit_moe(c, x_all_ext=None, oh_all_ext=None, y_out_ext=None, dep_in=None, cb8_ext=None):
    nc, s = c.nc, c.seq
    fused = x_all_ext is not None
    if fused:
        x_all = x_all_ext
        oh_all = oh_all_ext
    else:
        x_all = c.inp("x_all", [GTOK, D], BF16)
        oh_all = c.inp("oh_all", [NCORES * 16, TOK], BF16)
    w1e = c.inp("w1e", [2, 32, D, 128], F32)
    w2e = c.inp("w2e", [2, KT, 4096, 128], F32)
    b1e = c.inp("b1e", [2, 128, 32], F32)
    b2e = c.inp("b2e", [2, 128, KT], F32)
    cb8 = cb8_ext if cb8_ext is not None else c.inp("cb8", [8, 1], F32)

    if fused:
        y_out = y_out_ext
    else:
        y_out = c.outp("y_out", [TOK, D], BF16)
        sidx_out = c.outp("sidx_out", [128, GTOK // 128], I32)

    buf = c.dram("bufD", [TOK, D], BF16)
    sidx_dram = c.dram("sidxD", [NCORES, TOK], U16)
    loc_dram = c.dram("locD", [NCORES, TOK], U16)

    pb = [c.ps(f"db{i}", [128, 512], F32) for i in range(6)]
    pTx = [c.ps("dTa", [128, 128], BF16), c.ps("dTb", [128, 128], BF16)]
    dep_pb = {i: None for i in range(6)}
    dep_pT = {0: None, 1: None}

    ident_bf = c.sb("didb", [128, 128], BF16)
    ident_f = c.sb("didf", [128, 128], F32)
    ii = c.sb("dci", [128, 128], I32)
    jj = c.sb("dcj", [128, 128], I32)
    o1 = s.op("gpsimd", lambda e: e.iota(ii[:], pattern=[[1, 128]], base=0, channel_multiplier=0))
    o2 = s.op("gpsimd", lambda e: e.iota(jj[:], pattern=[[0, 128]], base=0, channel_multiplier=1))
    dep_idb = s.op("vector", lambda e: e.tensor_tensor(out=ident_bf[:], in0=ii[:], in1=jj[:], op=ALU.is_equal),
                   deps=(o1, o2))
    dep_idf = s.op("vector", lambda e: e.tensor_tensor(out=ident_f[:], in0=ii[:], in1=jj[:], op=ALU.is_equal),
                   deps=(o1, o2))
    xrow = [c.sb(f"xrowD{i}", [128, D], BF16) for i in range(4)]
    sidx_cols = c.sb("sxcD", [128, GTOK // 128], I32)
    loc_cols = c.sb("lccD", [128, GTOK // 128], I32)
    zrow = c.sb("zrowD", [128, D], BF16)
    yrow = c.sb("yrowD", [128, D], BF16)
    b1_sb = c.sb("b1D", [128, 32 * 2], F32)
    b2_sb = c.sb("b2D", [128, KT * 2], F32)

    with contextlib.ExitStack() as rst:
        def rsb(name, shape, dtype):
            return rst.enter_context(nc.sbuf_tensor(name, shape, dtype))
        packed_bf = rsb("packed_bf", [128, TOK], BF16)
        packed_f = rsb("packed_f", [128, TOK], F32)
        ones_big = rsb("ones_big", [128, TOK], F32)
        cum = rsb("cumD", [128, TOK], F32)
        gi = rsb("giD", [128, TOK], I32)
        hi_bf = rsb("hiD", [128, TOK], BF16)
        lo_bf = rsb("loD", [128, TOK], BF16)
        rows_hi = rsb("rhiD", [8, TOK], F32)
        rows_e = rsb("reD", [8, TOK], F32)
        keep8 = rsb("keep8D", [8, TOK], F32)
        su16 = rsb("su16D", [8, TOK], U16)
        rowbuf = rsb("rowbufD", [1, 256], F32)
        base_col = rsb("base_colD", [128, 1], F32)
        segsel = rsb("segselD", [128, 8], BF16)
        ecol_i = rsb("ecolD", [128, 1], I32)
        ecol = rsb("ecolfD", [128, 1], F32)
        cb_sb = rsb("cbD", [8, 1], F32)
        sidx_cols_u = rsb("sxcuD", [128, GTOK // 128], U16)
        loc_cols_u = rsb("lccuD", [128, GTOK // 128], U16)

        ld_oh = s.dma("sync", packed_bf[:], oh_all, deps=(dep_in,))
        cp_f = s.op("vector", lambda e: e.tensor_copy(out=packed_f[:], in_=packed_bf[:]), deps=(ld_oh,))
        o_ones = s.op("vector", lambda e: e.memset(ones_big[:], 1.0))
        sc_op = s.op("vector", lambda e: e.tensor_tensor_scan(
            out=cum[:], data0=packed_f[:], data1=ones_big[:], initial=0.0, op0=ALU.add, op1=ALU.mult),
            deps=(cp_f, o_ones))
        trs = s.op("tensor", lambda e: e.transpose(pb[0][0:1, 0:128], cum[:, TOK - 1:TOK], ident_f[:]),
                   deps=(sc_op, dep_idf))
        ev_r = s.op("vector", lambda e: e.tensor_copy(out=rowbuf[0:1, 0:128], in_=pb[0][0:1, 0:128]),
                    deps=(trs,))
        dep_pb[0] = ev_r
        prev_add = s.op("vector", lambda e: e.memset(rowbuf[0:1, 128:144], 0.0), deps=(ev_r,))
        for seg in range(1, NCORES):
            prev_add = s.op("vector", lambda e, seg=seg: e.tensor_add(
                out=rowbuf[0:1, 128 + 16 * seg:128 + 16 * (seg + 1)],
                in0=rowbuf[0:1, 128 + 16 * (seg - 1):128 + 16 * seg],
                in1=rowbuf[0:1, 16 * (seg - 1):16 * seg]), deps=(prev_add,))
        trb = s.op("tensor", lambda e: e.transpose(pb[1][0:128, 0:1], rowbuf[0:1, 128:256],
                                                   ident_f[0:1, 0:1]), deps=(prev_add, dep_pb[1]))
        ev_b = s.op("vector", lambda e: e.tensor_copy(out=base_col[:], in_=pb[1][0:128, 0:1]), deps=(trb,))
        dep_pb[1] = ev_b

        g1 = s.op("vector", lambda e: e.tensor_sub(out=cum[:], in0=cum[:], in1=packed_f[:]),
                  deps=(sc_op, ev_r))
        g2 = s.op("vector", lambda e: e.tensor_scalar(out=cum[:], in0=cum[:], scalar1=base_col[:, 0:1],
                                                      scalar2=None, op0=ALU.add), deps=(g1, ev_b))
        g3 = s.op("vector", lambda e: e.tensor_copy(out=gi[:], in_=cum[:]), deps=(g2,))
        g4 = s.op("vector", lambda e: e.tensor_scalar(out=gi[:], in0=gi[:], scalar1=8, scalar2=None,
                                                      op0=ALU.arith_shift_right), deps=(g3,))
        g5 = s.op("vector", lambda e: e.tensor_copy(out=hi_bf[:], in_=gi[:]), deps=(g4,))
        g6 = s.op("vector", lambda e: e.tensor_copy(out=gi[:], in_=cum[:]), deps=(g5,))
        g7 = s.op("vector", lambda e: e.tensor_scalar(out=gi[:], in0=gi[:], scalar1=255, scalar2=None,
                                                      op0=ALU.bitwise_and), deps=(g6,))
        g8 = s.op("vector", lambda e: e.tensor_copy(out=lo_bf[:], in_=gi[:]), deps=(g7,))
        mh = s.op("vector", lambda e: e.tensor_mul(out=hi_bf[:], in0=hi_bf[:], in1=packed_bf[:]),
                  deps=(g5, g8))
        mlo = s.op("vector", lambda e: e.tensor_mul(out=lo_bf[:], in0=lo_bf[:], in1=packed_bf[:]),
                   deps=(g8, mh))
        e1_ = s.op("gpsimd", lambda e: e.iota(ecol_i[:], pattern=[[0, 1]], base=0, channel_multiplier=1))
        e2_ = s.op("vector", lambda e: e.tensor_scalar(out=ecol_i[:], in0=ecol_i[:], scalar1=15,
                                                       scalar2=None, op0=ALU.bitwise_and), deps=(e1_,))
        e3_ = s.op("vector", lambda e: e.tensor_copy(out=ecol[:], in_=ecol_i[:]), deps=(e2_,))
        me = s.op("vector", lambda e: e.tensor_scalar(out=packed_bf[:], in0=packed_bf[:],
                                                      scalar1=ecol[:, 0:1], scalar2=None, op0=ALU.mult),
                  deps=(e3_, mh, mlo))
        pdiv = ecol_i
        sl3 = s.op("gpsimd", lambda e: e.iota(pdiv[:], pattern=[[0, 1]], base=0, channel_multiplier=1),
                   deps=(me,))
        sl4 = s.op("vector", lambda e: e.tensor_scalar(out=pdiv[:], in0=pdiv[:], scalar1=4, scalar2=None,
                                                       op0=ALU.arith_shift_right), deps=(sl3,))
        pdiv_f = rsb("pdivfD", [128, 1], F32)
        sl4b = s.op("vector", lambda e: e.tensor_copy(out=pdiv_f[:], in_=pdiv[:]), deps=(sl4,))
        sl_i = rsb("slD", [128, 8], I32)
        sl_f = rsb("slfD", [128, 8], F32)
        sl5 = s.op("gpsimd", lambda e: e.iota(sl_i[:], pattern=[[1, 8]], base=0, channel_multiplier=0))
        sl5b = s.op("vector", lambda e: e.tensor_copy(out=sl_f[:], in_=sl_i[:]), deps=(sl5,))
        sl6 = s.op("vector", lambda e: e.tensor_scalar(out=sl_f[:], in0=sl_f[:], scalar1=pdiv_f[:, 0:1],
                                                       scalar2=None, op0=ALU.subtract), deps=(sl5b, sl4b))
        sl7 = s.op("vector", lambda e: e.tensor_scalar(out=segsel[:], in0=sl_f[:], scalar1=0.0, scalar2=None,
                                                       op0=ALU.is_equal), deps=(sl6,))
        for q in range(TOK // 512):
            qp = q % 2
            m1 = s.op("tensor", lambda e, q=q, qp=qp: e.matmul(
                pb[2 + qp][0:8, 0:512], segsel[:], hi_bf[:, 512 * q:512 * (q + 1)], start=True, stop=True),
                deps=(dep_pb[2 + qp], sl7, mh))
            m2 = s.op("tensor", lambda e, q=q, qp=qp: e.matmul(
                pb[4 + qp][0:8, 0:512], segsel[:], lo_bf[:, 512 * q:512 * (q + 1)], start=True, stop=True),
                deps=(dep_pb[4 + qp], sl7, mlo))
            ev1 = s.op("vector", lambda e, q=q, qp=qp: e.tensor_copy(
                out=rows_hi[:, 512 * q:512 * (q + 1)], in_=pb[2 + qp][0:8, 0:512]), deps=(m1,))
            ev2 = s.op("vector", lambda e, q=q, qp=qp: e.scalar_tensor_tensor(
                out=rows_hi[:, 512 * q:512 * (q + 1)], in0=rows_hi[:, 512 * q:512 * (q + 1)], scalar=256.0,
                in1=pb[4 + qp][0:8, 0:512], op0=ALU.mult, op1=ALU.add), deps=(ev1, m2))
            m3 = s.op("tensor", lambda e, q=q, qp=qp: e.matmul(
                pb[2 + qp][0:8, 0:512], segsel[:], packed_bf[:, 512 * q:512 * (q + 1)], start=True,
                stop=True), deps=(ev1, me))
            ev3 = s.op("vector", lambda e, q=q, qp=qp: e.tensor_copy(
                out=rows_e[:, 512 * q:512 * (q + 1)], in_=pb[2 + qp][0:8, 0:512]), deps=(m3,))
            dep_pb[2 + qp] = ev3
            dep_pb[4 + qp] = ev2
        ld_cb = s.dma("sync", cb_sb[:], cb8)
        k1 = s.op("vector", lambda e: e.tensor_scalar(out=keep8[:], in0=rows_hi[:], scalar1=float(CAP),
                                                      scalar2=None, op0=ALU.is_lt), deps=(ev2,))
        sx1 = s.op("vector", lambda e: e.tensor_scalar(out=rows_e[:], in0=rows_e[:], scalar1=float(CAP),
                                                       scalar2=None, op0=ALU.mult), deps=(ev3,))
        sx2 = s.op("vector", lambda e: e.tensor_add(out=rows_e[:], in0=rows_e[:], in1=rows_hi[:]),
                   deps=(sx1, ev2))
        sx3 = s.op("vector", lambda e: e.tensor_mul(out=rows_e[:], in0=rows_e[:], in1=keep8[:]),
                   deps=(sx2, k1))
        sx4 = s.op("vector", lambda e: e.tensor_scalar(out=keep8[:], in0=keep8[:], scalar1=-BIGIDX,
                                                       scalar2=BIGIDX, op0=ALU.mult, op1=ALU.add),
                   deps=(sx3,))
        sx5 = s.op("vector", lambda e: e.tensor_add(out=rows_e[:], in0=rows_e[:], in1=keep8[:]),
                   deps=(sx4,))
        lc1 = s.op("vector", lambda e: e.tensor_scalar(out=keep8[:], in0=rows_e[:], scalar1=cb_sb[0:8, 0:1],
                                                       scalar2=None, op0=ALU.subtract), deps=(sx5, ld_cb))
        lc2 = s.op("vector", lambda e: e.tensor_scalar(out=rows_hi[:], in0=keep8[:], scalar1=0.0,
                                                       scalar2=None, op0=ALU.is_ge), deps=(lc1,))
        lc3 = s.op("vector", lambda e: e.tensor_scalar(out=cum[0:8, 0:TOK], in0=keep8[:],
                                                       scalar1=float(TOK), scalar2=None, op0=ALU.is_lt),
                   deps=(lc2, g6))
        lc4 = s.op("vector", lambda e: e.tensor_mul(out=rows_hi[:], in0=rows_hi[:], in1=cum[0:8, 0:TOK]),
                   deps=(lc3,))
        lc5 = s.op("vector", lambda e: e.tensor_mul(out=keep8[:], in0=keep8[:], in1=rows_hi[:]),
                   deps=(lc4,))
        lc6 = s.op("vector", lambda e: e.tensor_scalar(out=rows_hi[:], in0=rows_hi[:], scalar1=-BIGIDX,
                                                       scalar2=BIGIDX, op0=ALU.mult, op1=ALU.add),
                   deps=(lc5,))
        lc7 = s.op("vector", lambda e: e.tensor_add(out=keep8[:], in0=keep8[:], in1=rows_hi[:]),
                   deps=(lc6,))
        cu1 = s.op("vector", lambda e: e.tensor_copy(out=su16[:], in_=rows_e[:]), deps=(sx5, lc1))
        st1 = s.dma("sync", sidx_dram, su16[:], deps=(cu1,))
        cu2 = s.op("vector", lambda e: e.tensor_copy(out=su16[:], in_=keep8[:]), deps=(lc7, st1))
        st2 = s.dma("sync", loc_dram, su16[:], deps=(cu2,))
        tt1 = s.dma("scalar", sidx_cols_u[:], sidx_dram.rearrange("s (b f) -> (s b) f", f=128),
                    deps=(st1,), transpose=True)
        tt2 = s.dma("scalar", loc_cols_u[:], loc_dram.rearrange("s (b f) -> (s b) f", f=128),
                    deps=(st2,), transpose=True)
        cv1 = s.op("vector", lambda e: e.tensor_copy(out=sidx_cols[:], in_=sidx_cols_u[:]), deps=(tt1,))
        cv2 = s.op("vector", lambda e: e.tensor_copy(out=loc_cols[:], in_=loc_cols_u[:]), deps=(tt2,))
        st_sidx = st1 if fused else s.dma("sync", sidx_out, sidx_cols[:], deps=(cv1,))

        oz = s.op("vector", lambda e: e.memset(zrow[:], 0.0))
        z_sts = []
        for tt in range(TOK // 128):
            z_sts.append(s.dma("sync", buf[128 * tt:128 * (tt + 1), :], zrow[:], deps=(oz,)))
        prev_scat = [None] * 4
        scat_ops = []
        for tt in range(GTOK // 128):
            par = tt % 4
            ld = s.dma("sync", xrow[par][:], x_all[128 * tt:128 * (tt + 1), :],
                       deps=(prev_scat[par], dep_in))
            deps_sc = (ld, cv2) + tuple(z_sts[-NDMASEM:])
            sc2 = s.op("gpsimd", lambda e, tt=tt, par=par: e.indirect_dma_start(
                out=buf, out_offset=IndirectOffsetOnAxis(loc_cols[:, tt:tt + 1], 0),
                in_=xrow[par][:], in_offset=None, bounds_check=bc_reg(e, TOK - 1), oob_is_err=False),
                deps=deps_sc, dma=True)
            prev_scat[par] = sc2
            scat_ops.append(sc2)
        barrier = tuple(scat_ops[-NDMASEM:]) + (st_sidx, cv1, cv2)
    bfm = [c.sb(f"bfmD{k}", [128, CAP], BF16) for k in range(KT)]
    hD = [c.sb(f"hD{o}", [128, CAP], BF16) for o in range(32)]
    wstgD = [c.sb(f"wstgD{i}", [128, 4096], BF16) for i in range(2)]
    lb1 = [s.dma("sync", b1_sb[:, 32 * ex:32 * (ex + 1)], b1e[ex]) for ex in range(2)]
    lb2 = [s.dma("sync", b2_sb[:, KT * ex:KT * (ex + 1)], b2e[ex]) for ex in range(2)]
    prev_yst = None
    prev_bfm_user = None
    for ex in range(2):
        tr_evs = []
        for tt in range(CAP // 128):
            par = tt % 4
            ld = s.dma("sync", xrow[par][:], buf[CAP * ex + 128 * tt:CAP * ex + 128 * (tt + 1), :],
                       deps=barrier + (prev_scat[par], prev_bfm_user))
            ev = None
            tr = None
            for k in range(KT):
                pk = k % 2
                tr = s.op("tensor", lambda e, k=k, par=par, pk=pk: e.transpose(
                    pTx[pk][:], xrow[par][:, 128 * k:128 * (k + 1)], ident_bf[:]),
                    deps=(ld, dep_idb, dep_pT[pk]))
                ev = s.op("scalar", lambda e, k=k, tt=tt, pk=pk: e.copy(
                    bfm[k][:, 128 * tt:128 * (tt + 1)], pTx[pk][:]),
                    deps=(tr, prev_bfm_user) + barrier)
                dep_pT[pk] = ev
            prev_scat[par] = tr
            tr_evs.append(ev)
        dep_bfm = tr_evs[-1]
        wdep = [dep_bfm, dep_bfm]
        last_hA = {}
        for o in range(32):
            wpar = o % 2
            ldw = s.dma("gpsimd", wstgD[wpar][:, 0:1024].rearrange("p (a f) -> p a f", a=8),
                        w1e[ex, o].rearrange("(a p) f -> p a f", p=128), deps=(wdep[wpar],) + barrier)
            last_ev = None
            for q in range(CAP // 512):
                qp = q % 2
                mm = None
                for k in range(KT):
                    mm = s.op("tensor", lambda e, o=o, q=q, k=k, qp=qp, wpar=wpar: e.matmul(
                        pb[qp][:, 0:512], wstgD[wpar][:, 128 * k:128 * (k + 1)],
                        bfm[k][:, 512 * q:512 * (q + 1)], start=(k == 0), stop=(k == KT - 1)),
                        deps=(dep_pb[qp], ldw, dep_bfm))
                ev = s.op("scalar", lambda e, o=o, q=q, qp=qp, ex=ex: e.activation(
                    out=hD[o][:, 512 * q:512 * (q + 1)], in_=pb[qp][:, 0:512], func=AF.Relu,
                    bias=b1_sb[:, 32 * ex + o:32 * ex + o + 1]), deps=(mm, lb1[ex]))
                dep_pb[qp] = ev
                last_ev = ev
            wdep[wpar] = last_ev
            last_hA[o] = last_ev
        wdepB = [last_hA[31], last_hA[31]]
        yfm_evs = {}
        for didx in range(KT):
            wpar = didx % 2
            ldw = s.dma("gpsimd", wstgD[wpar][:].rearrange("p (a f) -> p a f", a=32),
                        w2e[ex, didx].rearrange("(a p) f -> p a f", p=128), deps=(wdepB[wpar],))
            last_ev = None
            for q in range(CAP // 512):
                qp = q % 2
                mm = None
                for k in range(32):
                    mm = s.op("tensor", lambda e, didx=didx, q=q, k=k, qp=qp, wpar=wpar: e.matmul(
                        pb[qp][:, 0:512], wstgD[wpar][:, 128 * k:128 * (k + 1)],
                        hD[k][:, 512 * q:512 * (q + 1)], start=(k == 0), stop=(k == 31)),
                        deps=(dep_pb[qp], ldw, last_hA[k]))
                ev = s.op("vector", lambda e, didx=didx, q=q, qp=qp, ex=ex: e.tensor_scalar(
                    out=bfm[didx][:, 512 * q:512 * (q + 1)], in0=pb[qp][:, 0:512],
                    scalar1=b2_sb[:, KT * ex + didx:KT * ex + didx + 1], scalar2=None, op0=ALU.add),
                    deps=(mm, lb2[ex]))
                dep_pb[qp] = ev
                last_ev = ev
            wdepB[wpar] = last_ev
            yfm_evs[didx] = last_ev
        for tt in range(CAP // 128):
            xevs = []
            for k in range(KT):
                pk = k % 2
                tr = s.op("tensor", lambda e, k=k, tt=tt, pk=pk: e.transpose(
                    pTx[pk][:], bfm[k][:, 128 * tt:128 * (tt + 1)], ident_bf[:]),
                    deps=(yfm_evs[k], dep_pT[pk]))
                ev = s.op("scalar", lambda e, k=k, pk=pk: e.copy(
                    yrow[:, 128 * k:128 * (k + 1)], pTx[pk][:]), deps=(tr, prev_yst))
                dep_pT[pk] = ev
                xevs.append(ev)
            prev_yst = s.dma("sync", y_out[CAP * ex + 128 * tt:CAP * ex + 128 * (tt + 1), :], yrow[:],
                             deps=tuple(xevs))
        prev_bfm_user = prev_yst
    return prev_yst, sidx_dram


def build_D():
    c = Ctx()
    emit_moe(c)
    return c.finish()


def prep_D_weights(inputs, core):
    w1 = f32c(inputs["L1_ffn_W1"][2 * core:2 * core + 2])
    w2 = f32c(inputs["L1_ffn_W2"][2 * core:2 * core + 2])
    w1e = f32c(w1.reshape(2, D, 32, 128).transpose(0, 2, 1, 3))
    w2e = f32c(w2.reshape(2, 4096, KT, 128).transpose(0, 2, 1, 3))
    b1 = f32c(inputs["L1_ffn_b1"][2 * core:2 * core + 2])
    b2 = f32c(inputs["L1_ffn_b2"][2 * core:2 * core + 2])
    b1e = f32c(b1.reshape(2, 32, 128).transpose(0, 2, 1))
    b2e = f32c(b2.reshape(2, KT, 128).transpose(0, 2, 1))
    cb = np.full((8, 1), 4096.0 * core, np.float32)
    return {"w1e": w1e, "w2e": w2e, "b1e": b1e, "b2e": b2e, "cb8": cb}


def run_D(inputs, x3tok_list, oh_list):
    nc = build_D()
    x_all = np.concatenate([np.asarray(x) for x in x3tok_list], axis=0)
    oh_all = np.concatenate([np.asarray(o) for o in oh_list], axis=0)
    maps = []
    for cc in range(NCORES):
        m = prep_D_weights(inputs, cc)
        m["x_all"] = x_all
        m["oh_all"] = oh_all
        maps.append(m)
    res = run_bass_kernel_spmd(nc, maps, core_ids=list(range(NCORES)))
    return res.results



def emit_final(c, y_all_ext=None, x3t_ext=None, sidx_dram=None, gate_sb_ext=None,
               cb8_ap=None, dep_in=None):
    nc, s = c.nc, c.seq
    fused = y_all_ext is not None
    if fused:
        y_all = y_all_ext
        x3t = x3t_ext
    else:
        y_all = c.inp("y_all", [GTOK, D], BF16)
        x3t = c.inp("x3t", [TOK, D], BF16)
        sidx_c = c.inp("sidx_c", [128, TOK // 128], I32)
        gate_c = c.inp("gate_c", [128, TOK // 128], F32)
    out = c.outp("out", [TOK, D], F32)

    si_sb = c.sb("siE", [128, TOK // 128], I32)
    if fused:
        gt_sb = gate_sb_ext
        cidf = c.sb("cidfE", [128, 1], F32)
        cidi = c.sb("cidiE", [128, 1], I32)
        onesr = c.sb("onesrE", [1, 128], F32)
        own_u = c.sb("ownuE", [128, TOK], U16)
        si_u = c.sb("siuE", [128, TOK // 128], U16)
        own_dram = c.dram("ownE", [1, TOK], U16)
        pE = c.ps("pE", [128, 8], F32)
        lc = s.dma("sync", cidf[0:1, 0:1], cb8_ap[0:1, 0:1], deps=(dep_in,))
        c1 = s.op("vector", lambda e: e.tensor_scalar(out=cidf[0:1, 0:1], in0=cidf[0:1, 0:1],
                                                      scalar1=1.0 / TOK, scalar2=None, op0=ALU.mult),
                  deps=(lc,))
        c0 = s.op("vector", lambda e: e.memset(onesr[:], 1.0))
        cbm = s.op("tensor", lambda e: e.matmul(pE[:, 0:1], onesr[0:1, :], cidf[0:1, 0:1],
                                                start=True, stop=True), deps=(c1, c0))
        c2a = s.op("vector", lambda e: e.tensor_copy(out=cidf[:], in_=pE[:, 0:1]), deps=(cbm,))
        c2 = s.op("vector", lambda e: e.tensor_copy(out=cidi[:], in_=cidf[:]), deps=(c2a,))
        gidx = s.op("gpsimd", lambda e: e.indirect_dma_start(
            out=own_u[:], out_offset=None, in_=sidx_dram,
            in_offset=IndirectOffsetOnAxis(cidi[:, 0:1], 0),
            bounds_check=bc_reg(e, NCORES - 1), oob_is_err=False), deps=(c2, dep_in), dma=True)
        stq = s.dma("sync", own_dram, own_u[0:1, :], deps=(gidx,))
        ttq = s.dma("scalar", si_u[:], own_dram.rearrange("o (a f) -> (o a) f", f=128),
                    deps=(stq,), transpose=True)
        l1 = s.op("vector", lambda e: e.tensor_copy(out=si_sb[:], in_=si_u[:]), deps=(ttq,))
        l2 = dep_in
    else:
        gt_sb = c.sb("gtE", [128, TOK // 128], F32)
        l1 = s.dma("sync", si_sb[:], sidx_c)
        l2 = s.dma("sync", gt_sb[:], gate_c)
    eps = c.sb("epsE", [128, 1], F32)
    o_eps = s.op("vector", lambda e: e.memset(eps[:], 1e-5))

    yg = [c.sb(f"ygE{i}", [128, D], BF16) for i in range(2)]
    xr = [c.sb(f"xrE{i}", [128, D], BF16) for i in range(2)]
    sf = c.sb("sfE", [128, D], F32)
    sq = c.sb("sqE", [128, D], F32)
    st_ = c.sb("stE", [128, 8], F32)
    ov = c.sb("ovE", [128, D], F32)
    prev = [None, None]
    prev_ln = None
    for tt in range(TOK // 128):
        par = tt % 2
        mz = s.op("vector", lambda e, par=par: e.memset(yg[par][:], 0.0), deps=(prev[par],))
        g = s.op("gpsimd", lambda e, tt=tt, par=par: e.indirect_dma_start(
            out=yg[par][:], out_offset=None, in_=y_all,
            in_offset=IndirectOffsetOnAxis(si_sb[:, tt:tt + 1], 0),
            bounds_check=bc_reg(e, GTOK - 1), oob_is_err=False), deps=(mz, l1), dma=True)
        lx = s.dma("sync", xr[par][:], x3t[128 * tt:128 * (tt + 1), :], deps=(prev[par],))
        sm = s.op("vector", lambda e, tt=tt, par=par: e.scalar_tensor_tensor(
            out=sf[:], in0=yg[par][:], scalar=gt_sb[:, tt:tt + 1], in1=xr[par][:],
            op0=ALU.mult, op1=ALU.add), deps=(g, lx, l2, prev_ln))
        sqo = s.op("vector", lambda e: e.tensor_mul(out=sq[:], in0=sf[:], in1=sf[:]), deps=(sm,))
        r1 = s.op("vector", lambda e: e.reduce_sum(out=st_[:, 0:1], in_=sf[:], axis=AX.X), deps=(sm,))
        r2 = s.op("vector", lambda e: e.reduce_sum(out=st_[:, 1:2], in_=sq[:], axis=AX.X), deps=(sqo,))
        r3 = s.op("vector", lambda e: e.tensor_scalar(out=st_[:, 0:2], in0=st_[:, 0:2], scalar1=1.0 / D,
                                                      scalar2=None, op0=ALU.mult), deps=(r1, r2))
        r4 = s.op("vector", lambda e: e.tensor_mul(out=st_[:, 2:3], in0=st_[:, 0:1], in1=st_[:, 0:1]),
                  deps=(r3,))
        r5 = s.op("vector", lambda e: e.tensor_sub(out=st_[:, 1:2], in0=st_[:, 1:2], in1=st_[:, 2:3]),
                  deps=(r4,))
        r6 = s.op("scalar", lambda e: e.activation(out=st_[:, 1:2], in_=st_[:, 1:2], func=AF.Ln,
                                                   bias=eps[:, 0:1]), deps=(r5, o_eps))
        r7 = s.op("scalar", lambda e: e.activation(out=st_[:, 1:2], in_=st_[:, 1:2], func=AF.Exp,
                                                   scale=-0.5), deps=(r6,))
        n1 = s.op("vector", lambda e: e.tensor_scalar(out=ov[:], in0=sf[:], scalar1=st_[:, 0:1],
                                                      scalar2=None, op0=ALU.subtract), deps=(r7,))
        n2 = s.op("vector", lambda e: e.tensor_scalar(out=ov[:], in0=ov[:], scalar1=st_[:, 1:2],
                                                      scalar2=None, op0=ALU.mult), deps=(n1,))
        stout = s.dma("sync", out[128 * tt:128 * (tt + 1), :], ov[:], deps=(n2,))
        prev[par] = sm
        prev_ln = stout
    return prev_ln


def build_E():
    c = Ctx()
    emit_final(c)
    return c.finish()


def run_E(y_list, x3tok_list, sidx_list, gate_list):
    nc = build_E()
    y_all = np.concatenate([np.asarray(y) for y in y_list], axis=0)
    maps = []
    for cc in range(NCORES):
        maps.append({
            "y_all": y_all,
            "x3t": np.ascontiguousarray(x3tok_list[cc]),
            "sidx_c": np.ascontiguousarray(np.asarray(sidx_list[cc])[:, 32 * cc:32 * (cc + 1)]),
            "gate_c": np.ascontiguousarray(gate_list[cc]),
        })
    res = run_bass_kernel_spmd(nc, maps, core_ids=list(range(NCORES)))
    return [np.asarray(r["out"]) for r in res.results]



def build_fused():
    c = Ctx()
    s = c.seq
    x0 = c.inp("x0", [TOK, D], F32)
    wwT = c.inp("wwT", [D, D], F32)
    wrT = c.inp("wrT", [D, D], F32)
    dec = c.inp("dec", [1, 1], F32)
    la = c.inp("la", [1, 1], F32)
    cb8 = c.inp("cb8", [8, 1], F32)

    gate_sb = c.sb("gate_g", [128, TOK // 128], F32)
    x3tok_loc = c.dram("x3tok_loc", [TOK, D], BF16)
    ohfm_loc = c.dram("ohfm_loc", [16, TOK], BF16)
    x_all_g = c.dram("x_all_g", [GTOK, D], BF16)
    ohall_g = c.dram("ohall_g", [NCORES * 16, TOK], BF16)
    y_loc = c.dram("y_loc", [TOK, D], BF16)
    y_all_g = c.dram("y_all_g", [GTOK, D], BF16)

    c.push()
    x_fm = [c.sb(f"xg{k}", [128, XCOLS], BF16) for k in range(KT)]
    c.push()
    emit_l0_attn(c, x0, wwT, wrT, dec, la, x_fm)
    c.pop()
    s.fence()
    c.push()
    emit_l0_ffn(c, x_fm=x_fm)
    c.pop()
    s.fence()
    c.push()
    emit_l1_attn(c, x_fm_ext=x_fm, gate_sb=gate_sb, x3tok_dram=x3tok_loc, oh_dram=ohfm_loc)
    c.pop()
    c.pop()
    s.fence()
    rg = [list(range(NCORES))]
    s.op("gpsimd", lambda e: e.collective_compute(
        "AllGather", ALU.bypass, replica_groups=rg, ins=[x3tok_loc.opt()], outs=[x_all_g.opt()]))
    s.op("gpsimd", lambda e: e.collective_compute(
        "AllGather", ALU.bypass, replica_groups=rg, ins=[ohfm_loc.opt()], outs=[ohall_g.opt()]))
    s.fence()
    c.push()
    _, sidx_dram = emit_moe(c, x_all_ext=x_all_g, oh_all_ext=ohall_g, y_out_ext=y_loc, cb8_ext=cb8)
    c.pop()
    s.fence()
    s.op("gpsimd", lambda e: e.collective_compute(
        "AllGather", ALU.bypass, replica_groups=rg, ins=[y_loc.opt()], outs=[y_all_g.opt()]))
    s.fence()
    c.push()
    emit_final(c, y_all_ext=y_all_g, x3t_ext=x3tok_loc, sidx_dram=sidx_dram,
               gate_sb_ext=gate_sb, cb8_ap=cb8)
    c.pop()
    return c.finish()


def kernel_fused(**inputs):
    nc = build_fused()
    wB = prep_B_weights(inputs)
    wC = prep_C_weights(inputs)
    mapsA = prep_A(inputs)
    maps = []
    for cc in range(NCORES):
        m = dict(mapsA[cc])
        m.update(wB)
        m.update(wC)
        m.update(prep_D_weights(inputs, cc))
        maps.append(m)
    res = run_bass_kernel_spmd(nc, maps, core_ids=list(range(NCORES)))
    outs = [np.asarray(r["out"]) for r in res.results]
    return np.concatenate([o.reshape(B_PER_CORE, S, D) for o in outs], axis=0)
